# version 31
# baseline (speedup 1.0000x reference)
import os
import numpy as np
import ml_dtypes
from contextlib import ExitStack

import concourse.bass as bass
import concourse.mybir as mybir
from concourse.bass_utils import run_bass_kernel_spmd

F32 = mybir.dt.float32
BF16 = mybir.dt.bfloat16
I32 = mybir.dt.int32
AF = mybir.ActivationFunctionType
ALU = mybir.AluOpType
BF = ml_dtypes.bfloat16

D = 1024
TOK = 4096
NT = TOK // 128
NCORE = 8
EPS = 1e-5

ENG_ATTR = {"t": "tensor", "v": "vector", "s": "scalar", "g": "gpsimd", "q": "sync"}


class Op:
    __slots__ = ("eng", "emit", "after", "inc", "needed", "target", "name", "chan")

    def __init__(self, eng, emit, after, inc, name, chan):
        self.eng = eng
        self.emit = emit
        self.after = after
        self.inc = inc
        self.needed = False
        self.target = 0
        self.name = name
        self.chan = chan


class P:

    def __init__(self, nc):
        self.nc = nc
        self.ops = []
        self.last = {}
        self.entry = None
        self.sid = 0

    def _add(self, o):
        if self.entry is not None and o is not self.entry:
            o.after.append(self.entry)
        self.ops.append(o)
        self.last[o.chan] = o
        return o

    def op(self, eng, emit, after=(), name=""):
        return self._add(Op(eng, emit, [a for a in after if a is not None], 1,
                            name, ("c", eng, self.sid)))

    def dma(self, eng, out, in_, after=(), name="", group=None):
        assert group is not None, f"dma {name} needs a semaphore group"
        return self._add(Op(eng,
                            lambda h, out=out, in_=in_: h.dma_start(out=out,
                                                                    in_=in_),
                            [a for a in after if a is not None], 16, name,
                            ("d", group)))

    def barrier(self, name="bar"):
        deps = list(self.last.values())
        b = Op("q", lambda h: h.nop(), deps, 1, name, ("c", "q", self.sid))
        self._add(b)
        self.entry = b
        return b

    def new_stage(self):
        self.sid += 1

    def emit_all(self, es):
        import numpy as np
        for o in self.ops:
            if o.inc == 16:
                o.needed = True
            for a in o.after:
                a.needed = True
        chans = {}
        chan_id = {}
        for o in self.ops:
            if o.needed:
                if o.chan not in chan_id:
                    chan_id[o.chan] = len(chan_id)
                chans[o.chan] = chans.get(o.chan, 0) + o.inc
                o.target = chans[o.chan]
        NCH = len(chan_id)
        clocks = {}
        zero = np.zeros(NCH, dtype=np.int64)
        for idx, o in enumerate(self.ops):
            cl = None
            for a in o.after:
                ca = clocks.get(id(a))
                if ca is None:
                    continue
                cl = ca.copy() if cl is None else np.maximum(cl, ca)
            if o.needed:
                if cl is None:
                    cl = zero.copy()
                ci = chan_id[o.chan]
                if o.target > cl[ci]:
                    cl[ci] = o.target
                clocks[id(o)] = cl
            elif cl is not None:
                clocks[id(o)] = cl
        sems = {c: es.enter_context(self.nc.semaphore("sem_" + "_".join(map(str, c))))
                for c in chans}
        nwait = [0, 0]
        with self.nc.Block() as blk:
            for eng, attr in ENG_ATTR.items():
                def section(h, eng=eng):
                    guar = zero.copy()
                    for o in self.ops:
                        if o.eng != eng:
                            continue
                        for a in o.after:
                            ci = chan_id[a.chan]
                            nwait[1] += 1
                            if a.target > guar[ci]:
                                h.wait_ge(sems[a.chan], a.target)
                                nwait[0] += 1
                                ca = clocks.get(id(a))
                                if ca is not None:
                                    np.maximum(guar, ca, out=guar)
                                else:
                                    guar[ci] = max(guar[ci], a.target)
                        inst = o.emit(h)
                        if o.needed:
                            inst.then_inc(sems[o.chan], o.inc)
                getattr(blk, attr)(section)
        if os.environ.get("BASS_DEBUG_WAITS"):
            print(f"[emit] waits emitted {nwait[0]} / {nwait[1]} deps; "
                  f"{len(chans)} chans; {len(self.ops)} ops")


SBUF_BYTES = 212832


class Arena:
    def __init__(self, ap):
        self.ap = ap
        self.off = 0
        self.marks = []
        self.peak = 0

    def alloc(self, cols, dtype, name=""):
        size = {F32: 4, BF16: 2, I32: 4}[dtype]
        self.off = (self.off + 31) & ~31
        start = self.off
        self.off += cols * size
        self.peak = max(self.peak, self.off)
        assert self.off <= SBUF_BYTES, f"SBUF overflow at {name}: {self.off}"
        return self.ap[:, start:start + cols * size].bitcast(dtype)

    def push(self):
        self.marks.append(self.off)

    def pop(self):
        self.off = self.marks.pop()


def make_tables():
    t = {}
    t["ident"] = np.eye(128, dtype=BF).view(np.uint16)
    half = 32
    theta = 10000.0 ** (-np.arange(half, dtype=np.float64) / half)
    ang = np.arange(TOK, dtype=np.float64)[:, None] * theta[None, :]
    t["crep"] = np.tile(np.cos(ang), (1, 16)).astype(BF).view(np.uint16)
    t["srep"] = np.tile(np.sin(ang), (1, 16)).astype(BF).view(np.uint16)
    H = 16
    gam = 1.0 - 2.0 ** (-5.0 - np.arange(H, dtype=np.float64))
    j = np.arange(128, dtype=np.float64)
    hm = np.zeros((128, H * 36), dtype=np.float64)
    for h in range(H):
        for mi in range(36):
            m = mi - 3
            hm[:, h * 36 + mi] = np.exp((128.0 * m - j) * np.log(gam[h]))
    t["decayB"] = hm.astype(np.float32)
    ga = np.zeros((128, H * 4), dtype=np.float64)
    i = np.arange(128, dtype=np.float64)
    for h in range(H):
        for sub in range(4):
            ga[:, h * 4 + sub] = 0.125 * gam[h] ** (sub * 128 + i)
    t["gammaA"] = ga.astype(np.float32)
    m4 = np.zeros((128, 4 * 1024), dtype=np.float64)
    jj = np.arange(128)[:, None]
    qq = np.arange(512)[None, :]
    for oi, o in enumerate((0, 128, 256, 384)):
        blk = ((o + jj) <= qq)
        m4[:, oi * 1024:oi * 1024 + 512] = blk
        m4[:, oi * 1024 + 512:(oi + 1) * 1024] = blk
    t["mask4"] = m4.astype(BF).view(np.uint16)
    pm = np.full((128, 128), -1e32, dtype=np.float32)
    pp_, ff_ = np.arange(128)[:, None], np.arange(128)[None, :]
    pm[(pp_ % 8) == (ff_ % 8)] = 0.0
    t["l1bmask"] = pm
    return t


TABLES = make_tables()

PREP_TRANSPOSE = {
    "l0b_qT": ("L0_attn_b_Wq", (D, D), False),
    "l0b_kT": ("L0_attn_b_Wk", (D, D), False),
    "l0b_vT": ("L0_attn_b_Wv", (D, D), False),
    "l0b_oT": ("L0_attn_b_Wo", (D, D), False),
    "l1a_qT": ("L1_attn_a_Wq", (D, D), False),
    "l1a_kT": ("L1_attn_a_Wk", (D, D), False),
    "l1a_vT": ("L1_attn_a_Wv", (D, D), False),
    "l1a_gT": ("L1_attn_a_Wg", (D, D), False),
    "l1a_oT": ("L1_attn_a_Wo", (D, D), False),
    "ffn0_gT": ("L0_ffn_w_gate", (3072, D), True),
    "ffn0_uT": ("L0_ffn_w_up", (3072, D), True),
    "ffn0_dT": ("L0_ffn_w_down", (D, 3072), True),
    "gate_wT": ("L1_ffn_gate_w", (8, D), False),
}

STAGES = ["l0a", "ffn0a", "l0b", "ffn0b", "l1a", "moea", "l1b", "moeb"]

STAGE_PREPS = {
    "dbg_ln": [],
    "l0a": [],
    "ffn0a": ["ffn0_gT", "ffn0_uT", "ffn0_dT"],
    "l0b": ["l0b_qT", "l0b_kT", "l0b_vT", "l0b_oT"],
    "ffn0b": ["ffn0_gT", "ffn0_uT", "ffn0_dT"],
    "l1a": ["l1a_qT", "l1a_kT", "l1a_vT", "l1a_gT", "l1a_oT"],
    "moea": ["gate_wT", "moe_w"],
    "l1b": [],
    "moeb": ["gate_wT", "moe_w"],
}

RAW_INPUTS = {
    "dbg_ln": [],
    "l0a": ["L0_attn_a_W_qkv", "L0_attn_a_W_o"],
    "ffn0a": [],
    "l0b": [],
    "ffn0b": [],
    "l1a": ["L1_attn_a_gn_w", "L1_attn_a_gn_b"],
    "moea": ["L1_ffn_ln_gamma", "L1_ffn_ln_beta"],
    "l1b": ["L1_attn_b_Wq", "L1_attn_b_bq", "L1_attn_b_Wdown", "L1_attn_b_bdown",
            "L1_attn_b_Wup", "L1_attn_b_bup", "L1_attn_b_Wout", "L1_attn_b_bout"],
    "moeb": ["L1_ffn_ln_gamma", "L1_ffn_ln_beta"],
}

STAGE_TABLES = {
    "dbg_ln": [],
    "l0a": ["crep", "srep", "mask4"],
    "ffn0a": [],
    "l0b": ["crep", "srep", "mask4"],
    "ffn0b": [],
    "l1a": ["crep", "srep", "decayB", "gammaA", "mask4"],
    "moea": [],
    "l1b": ["crep", "srep", "l1bmask"],
    "moeb": [],
}

INPUT_SHAPES = {
    "L0_attn_a_x": (8, 4096, 1024),
    "L0_attn_a_W_qkv": (1024, 3072), "L0_attn_a_W_o": (1024, 1024),
    "L0_attn_b_Wq": (1024, 1024), "L0_attn_b_Wk": (1024, 1024),
    "L0_attn_b_Wv": (1024, 1024), "L0_attn_b_Wo": (1024, 1024),
    "L0_ffn_w_gate": (3072, 1024), "L0_ffn_w_up": (3072, 1024),
    "L0_ffn_w_down": (1024, 3072),
    "L1_attn_a_Wq": (1024, 1024), "L1_attn_a_Wk": (1024, 1024),
    "L1_attn_a_Wv": (1024, 1024), "L1_attn_a_Wg": (1024, 1024),
    "L1_attn_a_Wo": (1024, 1024), "L1_attn_a_gn_w": (1024,),
    "L1_attn_a_gn_b": (1024,),
    "L1_attn_b_Wq": (1024, 1024), "L1_attn_b_bq": (1024,),
    "L1_attn_b_Wdown": (1024, 256), "L1_attn_b_bdown": (256,),
    "L1_attn_b_Wup": (256, 2048), "L1_attn_b_bup": (2048,),
    "L1_attn_b_Wout": (1024, 1024), "L1_attn_b_bout": (1024,),
    "L1_ffn_ln_gamma": (1024,), "L1_ffn_ln_beta": (1024,),
    "L1_ffn_gate_w": (8, 1024), "L1_ffn_w1": (8, 2048, 1024),
    "L1_ffn_w2": (8, 1024, 2048),
}
INPUT_DTYPES = {k: ("int32" if k.startswith("L0_ffn_w") else "float32")
                for k in INPUT_SHAPES}


def ln_stats_pass(p, ar, G, src, n_tiles=None):
    n_tiles = NT if n_tiles is None else n_tiles
    stats_m = ar.alloc(n_tiles, F32, "stats_m")
    stats_v = ar.alloc(n_tiles, F32, "stats_v")
    rstd = ar.alloc(n_tiles, F32, "rstd")
    negmr = ar.alloc(n_tiles, F32, "negmr")
    bnbuf = ar.alloc(14, F32, "bnbuf")
    xb = [ar.alloc(D, F32, f"lnx{i}") for i in range(2)]
    prev_use = [None, None]
    last = None
    for t in range(n_tiles):
        b = t % 2
        ld = p.dma("q", xb[b], src[t * 128:(t + 1) * 128, :],
                   after=(prev_use[b],), name=f"lnld{t}", group=f"lnx{b}")
        b1 = p.op("v", lambda v, b=b: v.bn_stats(bnbuf[:, 0:6], xb[b][:, 0:512]),
                  after=(ld, last), name=f"bns{t}a")
        b2 = p.op("v", lambda v, b=b: v.bn_stats(bnbuf[:, 6:12],
                                                 xb[b][:, 512:1024]),
                  after=(ld,), name=f"bns{t}b")
        ba = p.op("v", lambda v: v.bn_aggr(bnbuf[:, 12:14], bnbuf[:, 0:12]),
                  after=(b1, b2), name=f"bna{t}")
        cp = p.op("v", lambda v, t=t: v.tensor_copy(stats_m[:, t:t + 1],
                                                    bnbuf[:, 12:13]),
                  after=(ba,), name=f"cpm{t}")
        cp2 = p.op("v", lambda v, t=t: v.tensor_copy(stats_v[:, t:t + 1],
                                                     bnbuf[:, 13:14]),
                   after=(ba,), name=f"cpv{t}")
        prev_use[b] = cp2
        last = cp2
    statbar = p.barrier("statbar")
    o1 = p.op("s", lambda s: s.activation(rstd[:, :], stats_v[:, :], AF.Ln,
                                          bias=G["eps"], scale=1.0),
              after=(last, statbar, G["consts_ready"]), name="lnv")
    o2 = p.op("s", lambda s: s.activation(rstd[:, :], rstd[:, :], AF.Exp,
                                          bias=0.0, scale=-0.5), after=(o1,),
              name="expv")
    o3 = p.op("v", lambda v: v.scalar_tensor_tensor(
        negmr[:, :], stats_m[:, :], -1.0, rstd[:, :], ALU.mult, ALU.mult),
        after=(o2,), name="negmr")
    return rstd, negmr, o3


def stage_ffn0(p, ar, G, src, dst, barrier):
    HID = 3072
    NHC = HID // 128
    TPB = 2
    BW = TPB * 128
    NB = NT // TPB
    SC = 1.0 / 127.0
    SC2 = 1.0 / (127.0 * 127.0)
    PS = G["psum_f32"]
    pb = G["psum_bf"][0]
    ar.push()
    rstd, negmr, stat_done = ln_stats_pass(p, ar, G, src)
    wg = ar.alloc(8 * HID, BF16, "wgT")
    wu = ar.alloc(8 * HID, BF16, "wuT")
    wd = [ar.alloc(12 * D, BF16, f"wdT{h}") for h in range(2)]
    xnT = ar.alloc(8 * BW, BF16, "xnT")
    mT = ar.alloc(NHC * BW, BF16, "mT")
    x_blk = [ar.alloc(D, F32, f"xb{i}") for i in range(TPB)]
    xn_buf = ar.alloc(D, BF16, "xn")
    gs_buf = ar.alloc(BW, BF16, "gs")
    uu_buf = ar.alloc(BW, BF16, "uu")
    xo_buf = [ar.alloc(D, F32, f"xo{i}") for i in range(2)]

    wld = []
    for c in range(8):
        wld.append(p.dma("q", wg[:, c * HID:(c + 1) * HID],
                         G["ffn0_gT"][c * 128:(c + 1) * 128, :],
                         after=(barrier,), name=f"wgld{c}", group="wbig"))
        wld.append(p.dma("s", wu[:, c * HID:(c + 1) * HID],
                         G["ffn0_uT"][c * 128:(c + 1) * 128, :],
                         after=(barrier,), name=f"wuld{c}", group="wbigu"))
    w_ready = wld[-1]
    wdT_v = G["ffn0_dT"].rearrange("(k pp) c -> pp k c", pp=128)

    stores = []
    last_tr = None
    last_trev = None
    last_gu_mm = None
    last_m = {}
    last_down = {}
    last_gu_ev = {}
    wd_ld_prev = [None, None]
    for blk in range(NB):
        wd_ld = []
        for hf in range(2):
            dep = [barrier, wd_ld_prev[hf]]
            if blk > 0:
                dep += [last_down.get((blk - 1, c, tt)) for c in range(2)
                        for tt in range(TPB)]
            wd_ld.append(p.dma(("q", "s")[hf],
                               wd[hf].rearrange("p (k c) -> p k c", k=12),
                               wdT_v[:, hf * 12:(hf + 1) * 12, :],
                               after=tuple(dep), name=f"wdld{blk}_{hf}",
                               group=f"wd{hf}"))
            wd_ld_prev[hf] = wd_ld[-1]
        xn_done = []
        for tt in range(TPB):
            t = blk * TPB + tt
            dep = [stat_done]
            if blk > 0:
                dep += [last_down.get((blk - 1, c, tt)) for c in range(2)]
            ld = p.dma("q", x_blk[tt], src[t * 128:(t + 1) * 128, :],
                       after=tuple(dep), name=f"xld{t}", group=f"xb{tt}")
            xo = p.op("s", lambda s, tt=tt, t=t: s.activation(
                xn_buf[:, :], x_blk[tt][:, :], AF.Identity,
                bias=negmr[:, t:t + 1], scale=rstd[:, t:t + 1]),
                after=(ld, stat_done, last_tr), name=f"xn{t}")
            trs = []
            for c in range(8):
                dep2 = (xo, last_trev, G["setup"]) if c == 0 else (xo,)
                o = p.op("t", lambda t_, c=c: t_.transpose(
                    pb[:, c * 128:(c + 1) * 128], xn_buf[:, c * 128:(c + 1) * 128],
                    G["ident"]), after=dep2, name=f"tr{t}_{c}")
                trs.append(o)
            last_tr = trs[-1]
            ev = p.op("v", lambda v, tt=tt: v.tensor_copy(
                xnT[:, :].rearrange("p (c k) -> p c k", c=8)[:, :, tt * 128:(tt + 1) * 128],
                pb[:, :].rearrange("p (c k) -> p c k", c=8)),
                after=(last_tr, last_gu_mm if tt == 0 else None),
                name=f"trev{t}")
            last_trev = ev
            xn_done.append(ev)
        for hc in range(NHC):
            bank_g = PS[hc % 2]
            bank_u = PS[2 + hc % 2]

            def mk_gu(bank, w, hc=hc):
                def emit(t_):
                    r = None
                    for c in range(8):
                        r = t_.matmul(bank[:, 0:BW],
                                      w[:, c * HID + hc * 128:c * HID + hc * 128 + 128],
                                      xnT[:, c * BW:(c + 1) * BW],
                                      start=(c == 0), stop=(c == 7))
                    return r
                return emit
            mm_g = p.op("t", mk_gu(bank_g, wg),
                        after=(xn_done[-1], w_ready, last_gu_ev.get((hc - 2, "g"))),
                        name=f"g{blk}_{hc}")
            mm_u = p.op("t", mk_gu(bank_u, wu),
                        after=(last_gu_ev.get((hc - 2, "u")),), name=f"u{blk}_{hc}")
            last_gu_mm = mm_u
            evg = p.op("s", lambda s, bank_g=bank_g: s.activation(
                gs_buf[:, :], bank_g[:, 0:BW], AF.Silu, bias=0.0, scale=SC),
                after=(mm_g, last_m.get((blk, hc - 1))), name=f"evg{blk}_{hc}")
            evu = p.op("v", lambda v, bank_u=bank_u: v.tensor_copy(
                uu_buf[:, :], bank_u[:, 0:BW]),
                after=(mm_u, last_m.get((blk, hc - 1))), name=f"evu{blk}_{hc}")
            dep_m = [evg, evu]
            if blk > 0:
                dep_m += [last_down.get((blk - 1, c, TPB - 1)) for c in range(2)]
            mm_m = p.op("v", lambda v, hc=hc: v.tensor_tensor(
                mT[:, hc * BW:(hc + 1) * BW], gs_buf[:, :], uu_buf[:, :],
                ALU.mult), after=tuple(dep_m), name=f"m{blk}_{hc}")
            last_gu_ev[(hc, "g")] = evg
            last_gu_ev[(hc, "u")] = evu
            last_m[(blk, hc)] = mm_m
        m_all = last_m[(blk, NHC - 1)]
        for tt in range(TPB):
            t = blk * TPB + tt
            res_ops = []
            for col in range(2):
                bank = PS[4 + col]

                def mk_down(bank=bank, col=col, tt=tt):
                    def emit(t_):
                        r = None
                        for hc in range(NHC):
                            hf, k = divmod(hc, 12)
                            r = t_.matmul(
                                bank[:, :],
                                mT[:, hc * BW + tt * 128:hc * BW + tt * 128 + 128],
                                wd[hf][:, k * D + col * 512:k * D + col * 512 + 512],
                                start=(hc == 0), stop=(hc == NHC - 1))
                        return r
                    return emit
                dep = (m_all, wd_ld[0], wd_ld[1], last_down.get((blk, col, tt - 1)),
                       last_down.get((blk - 1, col, TPB - 1)) if tt == 0 else None)
                o = p.op("t", mk_down(), after=dep, name=f"d{blk}_{tt}_{col}")
                xo = xo_buf[tt % 2]
                ro = p.op("v", lambda v, col=col, tt=tt, xo=xo, bank=bank:
                          v.scalar_tensor_tensor(
                              xo[:, col * 512:(col + 1) * 512], bank[:, :], SC2,
                              x_blk[tt][:, col * 512:(col + 1) * 512],
                              ALU.mult, ALU.add),
                          after=(o, stores[-2] if len(stores) >= 2 else None),
                          name=f"res{blk}_{tt}_{col}")
                res_ops.append(ro)
                last_down[(blk, col, tt)] = ro
            st = p.dma("q", dst[t * 128:(t + 1) * 128, :], xo_buf[tt % 2],
                       after=tuple(res_ops), name=f"xst{t}", group=f"st{tt % 2}")
            stores.append(st)
    ar.pop()
    return stores



def stage_dbg_ln(p, ar, G, src, dst, barrier):
    ar.push()
    rstd, negmr, stat_done = ln_stats_pass(p, ar, G, src)
    p.dma("q", G["dbg_stats"][0:128, 0:32], rstd[:, :], after=(stat_done,),
          name="dmpr", group="dbgs")
    p.dma("q", G["dbg_stats"][0:128, 32:64], negmr[:, :], after=(stat_done,),
          name="dmpn", group="dbgs")
    xb = [ar.alloc(D, F32, f"dxb{i}") for i in range(2)]
    xn = [ar.alloc(D, F32, f"dxn{i}") for i in range(2)]
    stores = []
    prev = {}
    for t in range(NT):
        b = t % 2
        ld = p.dma("q", xb[b], src[t * 128:(t + 1) * 128, :],
                   after=(stat_done, prev.get(("xn", b))), name=f"dld{t}",
                   group=f"dxb{b}")
        xo = p.op("s", lambda s, b=b, t=t: s.activation(
            xn[b][:, :], xb[b][:, :], AF.Identity,
            bias=negmr[:, t:t + 1], scale=rstd[:, t:t + 1]),
            after=(ld, stat_done, prev.get(("st", b))), name=f"dxn{t}")
        prev[("xn", b)] = xo
        st = p.dma("q", dst[t * 128:(t + 1) * 128, :], xn[b][:, :],
                   after=(xo,), name=f"dst{t}", group=f"dst{b}")
        prev[("st", b)] = st
        stores.append(st)
    ar.pop()
    return stores



def stage_attn(p, ar, G, cfg, src, dst, barrier):
    NH_G = 8
    BIS = cfg.get("bisect", 6)
    NPAIR = NH_G // 2
    STW = 512
    NST = TOK // STW
    PS = G["psum_f32"]
    pbf = G["psum_bf"]
    ar.push()
    rstd, negmr, stat_done = ln_stats_pass(p, ar, G, src)
    kT = ar.alloc(NPAIR * TOK, BF16, "kT")
    v65 = ar.alloc(NT * NH_G * 65, BF16, "v65")
    qT = ar.alloc(NPAIR * STW, BF16, "qT")
    RET = cfg["post"] == "retention"
    if not RET:
        yT = ar.alloc(NH_G * STW, BF16, "yT")
    xnT = ar.alloc(8 * 128, BF16, "xnT_a")
    wq = ar.alloc(8 * STW, BF16, "awq")
    wk = ar.alloc(8 * STW, BF16, "awk")
    wv = ar.alloc(8 * STW, BF16, "awv")
    RET0 = cfg["post"] == "retention"
    wo = ar.alloc((NPAIR if RET0 else NH_G) * D, BF16, "awo")
    xb = [ar.alloc(D, F32, f"axb{i}") for i in range(2)]
    xrb = [ar.alloc(D, F32, f"xrb{i}") for i in range(2)]
    xn_buf = ar.alloc(D, BF16, "axn")
    qk_raw = [ar.alloc(2 * STW, BF16, f"qkraw{i}") for i in range(2)]
    qk_rope = [ar.alloc(2 * STW, BF16, f"qkrope{i}") for i in range(2)]
    rtmpA = ar.alloc(2 * STW, BF16, "rtmpA")
    rtmpB = rtmpA
    ctab = [ar.alloc(2 * STW, BF16, f"ctab{i}") for i in range(1 if RET0 else 2)]
    if RET0:
        ctab = ctab * 2
    stab = [c[:, :].rearrange("p (a x) -> p a x", a=2)[:, ::-1] for c in ctab]
    NATT = 2 if cfg["post"] == "retention" else 3
    att = [ar.alloc(2 * STW, BF16, f"attp{i}") for i in range(NATT)]
    if cfg["post"] == "softmax":
        rec_sb = ar.alloc(STW, F32, "rec_sb")
        rec_bc = [ar.alloc(STW, F32, f"recbc{h}") for h in range(2)]
    ones_f32 = ar.alloc(128, F32, "ones_f32")
    mask_sb = ar.alloc(4 * 512 if RET0 else 4 * 1024, BF16, "mask_sb")
    xo_buf = [ar.alloc(D, F32, f"axo{i}") for i in range(2)]

    ones_set = p.op("g", lambda g_: g_.memset(ones_f32[:, :], 1.0), name="aones")
    if RET0:
        mld = p.dma("q", mask_sb.rearrange("p (a x) -> p a x", a=4),
                    G["tbl_mask4"].rearrange("p (a d x) -> p a d x", a=4,
                                             d=2)[:, :, 0],
                    name="amask", group="const")
    else:
        mld = p.dma("q", mask_sb, G["tbl_mask4"], name="amask", group="const")
    if RET:
        wgt = ar.alloc(8 * STW, BF16, "awg")
        gate_st = ar.alloc(4 * STW, BF16, "gate_st")
        sil_r = ar.alloc(STW, F32, "sil_r")
        decayB_sb = ar.alloc(16 * 36, F32, "decayB_sb")
        gammaA_sb = ar.alloc(16 * 4, F32, "gammaA_sb")
        yraw = ar.alloc(STW, BF16, "yraw")
        y_tok = ar.alloc(4 * STW, F32, "y_tok")
        y_gn = y_tok
        gated = ar.alloc(4 * STW, BF16, "gated")
        gatedT = ar.alloc(NPAIR * STW, BF16, "gatedT")
        gnw_bc = ar.alloc(D, F32, "gnw_bc")
        gnb_bc = ar.alloc(D, F32, "gnb_bc")
        gn_row = ar.alloc(D, F32, "gn_row")
        gn_y2 = ar.alloc(STW, F32, "gn_y2")
        sums = ar.alloc(32, F32, "gn_s1")
        sumsq = ar.alloc(32, F32, "gn_s2")
        mbuf = ar.alloc(32, F32, "gn_m")
        vbuf = ar.alloc(32, F32, "gn_v")
        rstd2 = ar.alloc(32, F32, "gn_r")
        negm2 = ar.alloc(32, F32, "gn_n")
        bn2 = ar.alloc(8, F32, "gn_bn")
        G["decayB_sb"] = decayB_sb
        d1 = p.dma("q", decayB_sb, G["tbl_decayB"], name="dcl", group="const")
        d2 = p.dma("q", gammaA_sb, G["tbl_gammaA"], name="gal", group="const")
        d3 = p.dma("q", gn_row[0:1, :], G["L1_attn_a_gn_w"].rearrange("(a c) -> a c", a=1),
                   name="gnwl", group="const")
        bc1 = p.op("t", lambda t_: t_.matmul(G["psum_f32"][4][:, :],
                                             ones_f32[0:1, 0:128], gn_row[0:1, 0:512],
                                             start=True, stop=True),
                   after=(d3, ones_set, G["setup"]), name="gnbc1")
        bc2 = p.op("t", lambda t_: t_.matmul(G["psum_f32"][5][:, :],
                                             ones_f32[0:1, 0:128], gn_row[0:1, 512:1024],
                                             start=True, stop=True),
                   after=(d3, ones_set), name="gnbc2")
        e1 = p.op("s", lambda s: s.copy(gnw_bc[:, 0:512], G["psum_f32"][4][:, :]),
                  after=(bc1,), name="gnev1")
        e2 = p.op("s", lambda s: s.copy(gnw_bc[:, 512:1024], G["psum_f32"][5][:, :]),
                  after=(bc2,), name="gnev2")
        d4 = p.dma("q", gn_row[0:1, :], G["L1_attn_a_gn_b"].rearrange("(a c) -> a c", a=1),
                   after=(bc1, bc2), name="gnbl", group="const")
        bc3 = p.op("t", lambda t_: t_.matmul(G["psum_f32"][4][:, :],
                                             ones_f32[0:1, 0:128], gn_row[0:1, 0:512],
                                             start=True, stop=True),
                   after=(d4, e1), name="gnbc3")
        bc4 = p.op("t", lambda t_: t_.matmul(G["psum_f32"][5][:, :],
                                             ones_f32[0:1, 0:128], gn_row[0:1, 512:1024],
                                             start=True, stop=True),
                   after=(d4, e2), name="gnbc4")
        e3 = p.op("s", lambda s: s.copy(gnb_bc[:, 0:512], G["psum_f32"][4][:, :]),
                  after=(bc3,), name="gnev3")
        e4 = p.op("s", lambda s: s.copy(gnb_bc[:, 512:1024], G["psum_f32"][5][:, :]),
                  after=(bc4,), name="gnev4")
        gn_ready = p.op("q", lambda h: h.nop(), after=(e3, e4, d1, d2), name="gnrdy")

    stores_all = []
    for g in range(2):
        wld = []
        for c in range(8):
            sl = slice(c * 128, (c + 1) * 128)
            if cfg["wsrc"] == "qkv":
                W = G["L0_attn_a_W_qkv"]
                for wbuf, off in ((wq, 0), (wk, 1024), (wv, 2048)):
                    wld.append(p.dma("g", wbuf[:, c * STW:(c + 1) * STW],
                                     W[sl, off + g * 512:off + g * 512 + 512],
                                     name=f"awl{g}{c}{off}", group="awq"))
            else:
                wlist = [(wq, cfg["wq"]), (wk, cfg["wk"]), (wv, cfg["wv"])]
                if RET:
                    wlist.append((wgt, cfg["wg"]))
                for wbuf, nm in wlist:
                    wld.append(p.dma("q", wbuf[:, c * STW:(c + 1) * STW],
                                     G[nm][sl, g * 512:g * 512 + 512],
                                     name=f"awl{nm}{g}{c}", group="awq"))
        if cfg.get("wo_pairs"):
            for pr in range(NPAIR):
                rsl = slice(g * 512 + pr * 128, g * 512 + pr * 128 + 128)
                wld.append(p.dma("q", wo[:, pr * D:(pr + 1) * D],
                                 G[cfg["wo"]][rsl, :], name=f"awo{g}{pr}",
                                 group="awq"))
        else:
            for h in range(NH_G):
                rsl = slice(g * 512 + h * 64, g * 512 + h * 64 + 64)
                if cfg["wsrc"] == "qkv":
                    wld.append(p.dma("g", wo[0:64, h * D:(h + 1) * D],
                                     G["L0_attn_a_W_o"][rsl, :],
                                     name=f"awo{g}{h}", group="awq"))
                else:
                    wld.append(p.dma("q", wo[0:64, h * D:(h + 1) * D],
                                     G[cfg["wo"]][rsl, :],
                                     name=f"awo{g}{h}", group="awq"))
        w_ready = wld[-1]
        vones = p.op("g", lambda g_: g_.memset(
            v65[:, :].rearrange("p (t h e) -> p t h e", t=NT, h=NH_G)[:, :, :, 64:65],
            1.0), after=(ones_set,), name=f"avones{g}")

        lt = {}
        if RET and g == 0:
            lt[("ynorm", 0)] = e3
            lt[("ynorm", 1)] = e4
        st_stores = []
        for st in range(NST):
            def emit_qktr(tsk):
                bb = tsk % 2
                trq = []
                for pr in range(NPAIR):
                    o = p.op("t", lambda t_, pr=pr, bb=bb: t_.transpose(
                        pbf[1][:, pr * 128:(pr + 1) * 128],
                        qk_rope[bb][:, pr * 128:(pr + 1) * 128], G["ident"]),
                        after=(lt.get(("rope2", bb)), lt.get("qkT_ev"),
                               lt.get("prep_ev1"))
                        if pr == 0 else (),
                        name=f"aqtr{g}{tsk}_{pr}")
                    trq.append(o)
                for pr in range(NPAIR):
                    o = p.op("t", lambda t_, pr=pr, bb=bb: t_.transpose(
                        pbf[1][:, 512 + pr * 128:512 + (pr + 1) * 128],
                        qk_rope[bb][:, 512 + pr * 128:512 + (pr + 1) * 128],
                        G["ident"]), after=(), name=f"aktr{g}{tsk}_{pr}")
                    trq.append(o)
                tt_in = tsk % 4
                evq2 = p.op("v", lambda v, tt_in=tt_in: v.tensor_copy(
                    qT[:, :].rearrange("p (pr x) -> p pr x", pr=NPAIR)[
                        :, :, tt_in * 128:(tt_in + 1) * 128],
                    pbf[1][:, 0:512].rearrange("p (pr x) -> p pr x", pr=NPAIR)),
                    after=(trq[-1], lt.get("sT_last") if tt_in == 0 else None),
                    name=f"aqev{g}{tsk}")
                evk2 = p.op("v", lambda v, tsk=tsk: v.tensor_copy(
                    kT[:, :].rearrange("p (pr x) -> p pr x", pr=NPAIR)[
                        :, :, tsk * 128:(tsk + 1) * 128],
                    pbf[1][:, 512:1024].rearrange("p (pr x) -> p pr x", pr=NPAIR)),
                    after=(trq[-1],), name=f"akev{g}{tsk}")
                lt["qkT_ev"] = evk2
                lt["qT_ev"] = evq2
                lt["kT_ev"] = evk2
                lt[("rope_done", tsk % 2)] = evk2

            for tt in range(4):
                t = st * 4 + tt
                b = t % 2
                ld = p.dma("q", xb[b], src[t * 128:(t + 1) * 128, :],
                           after=(stat_done, lt.get(("xb", b)),),
                           name=f"axld{g}_{t}", group=f"axb{b}")
                ldc = p.dma("q", ctab[b][:, 0:512],
                            G["tbl_crep"][t * 128:(t + 1) * 128, :],
                            after=(lt.get(("ropetab", b)),), name=f"acr{g}_{t}",
                            group=f"acr{b}")
                lds = p.dma("q", ctab[b][:, 512:1024],
                            G["tbl_srep"][t * 128:(t + 1) * 128, :],
                            after=(lt.get(("ropetab", b)),), name=f"asr{g}_{t}",
                            group=f"acr{b}")
                xo = p.op("s", lambda s, b=b, t=t: s.activation(
                    xn_buf[:, :], xb[b][:, :], AF.Identity,
                    bias=negmr[:, t:t + 1], scale=rstd[:, t:t + 1]),
                    after=(ld, stat_done, lt.get("xn_tr")), name=f"axn{g}{t}")
                lt[("xb", b)] = xo
                trs = []
                for c in range(8):
                    o = p.op("t", lambda t_, c=c: t_.transpose(
                        pbf[0][:, c * 128:(c + 1) * 128],
                        xn_buf[:, c * 128:(c + 1) * 128], G["ident"]),
                        after=(xo, lt.get("xnT_ev"), lt.get("prep_ev0"),
                               G["setup"]) if c == 0 else (xo,),
                        name=f"atr{g}{t}_{c}")
                    trs.append(o)
                lt["xn_tr"] = trs[-1]
                ev = p.op("s", lambda s: s.copy(
                    xnT[:, :].rearrange("p (c k) -> p c k", c=8),
                    pbf[0][:, :].rearrange("p (c k) -> p c k", c=8)),
                    after=(trs[-1], lt.get("proj_mm")), name=f"atrev{g}{t}")
                lt["xnT_ev"] = ev

                def mk_proj(wbuf, bank, perm):
                    def emit(t_):
                        r = None
                        for c in range(8):
                            rhs = wbuf[:, c * STW:(c + 1) * STW]
                            if perm:
                                rhs = rhs.rearrange("p (h half fr) -> p half h fr",
                                                    half=2, h=8, fr=32)
                            r = t_.matmul(bank[:, :], xnT[:, c * 128:(c + 1) * 128],
                                          rhs, start=(c == 0), stop=(c == 7))
                        return r
                    return emit
                mmq = p.op("t", mk_proj(wq, PS[0], True),
                           after=(ev, w_ready, lt.get(("attev", 0)),
                                  lt.get("pevq")), name=f"amq{g}{t}")
                mmk = p.op("t", mk_proj(wk, PS[1], True),
                           after=(lt.get("pevk"),), name=f"amk{g}{t}")
                mmv = p.op("t", mk_proj(wv, PS[2], False),
                           after=(lt.get(("attev", 1)), lt.get("pevv")),
                           name=f"amv{g}{t}")
                lt["proj_mm"] = mmv
                qv = qk_raw[b][:, :].rearrange("p (half qk x) -> p half qk x",
                                               half=2, qk=2)
                evq = p.op("s", lambda s, qv=qv: s.copy(
                    qv[:, :, 0], PS[0][:, :].rearrange("p (half x) -> p half x",
                                                       half=2)),
                    after=(mmq, lt.get(("rope13", b))), name=f"aevq{g}{t}")
                evk = p.op("s", lambda s, qv=qv: s.copy(
                    qv[:, :, 1], PS[1][:, :].rearrange("p (half x) -> p half x",
                                                       half=2)),
                    after=(mmk,), name=f"aevk{g}{t}")
                lt["pevq"] = evq
                lt["pevk"] = evk
                evv = p.op("s", lambda s, t=t: s.copy(
                    v65[:, t * NH_G * 65:(t + 1) * NH_G * 65].rearrange(
                        "p (h e) -> p h e", h=NH_G)[:, :, 0:64],
                    PS[2][:, :].rearrange("p (h d) -> p h d", h=NH_G)),
                    after=(mmv, vones), name=f"aevv{g}{t}")
                lt["pevv"] = evv
                if RET:
                    mmg = p.op("t", mk_proj(wgt, PS[3], False),
                               after=(lt.get("pevg"),), name=f"amg{g}{t}")
                    sv = p.op("s", lambda s: s.activation(
                        sil_r[:, :], PS[3][:, :], AF.Exp, bias=0.0, scale=-1.0),
                        after=(mmg, lt.get("silr"), lt.get("silr2")),
                        name=f"asx{g}{t}")
                    s2 = p.op("v", lambda v: v.tensor_scalar_add(
                        sil_r[:, :], sil_r[:, :], 1.0), after=(sv,),
                        name=f"asa{g}{t}")
                    s3 = p.op("v", lambda v: v.reciprocal(sil_r[:, :], sil_r[:, :]),
                              after=(s2,), name=f"asr{g}{t}")
                    s4 = p.op("v", lambda v, tt=tt: v.scalar_tensor_tensor(
                        gate_st[:, tt * STW:(tt + 1) * STW], PS[3][:, :], 1.0,
                        sil_r[:, :], ALU.mult, ALU.mult),
                        after=(s3, lt.get(("gate_rd", tt))), name=f"asg{g}{t}")
                    lt["pevg"] = s4
                    lt["silr"] = s4
                    lt["silr2"] = s4
                r1 = p.op("v", lambda v, b=b: v.tensor_tensor(
                    rtmpA[:, :], qk_raw[b][:, :], ctab[b][:, :], ALU.mult),
                    after=(evq, evk, ldc, lds, lt.get("rtmpA_rd")),
                    name=f"arp{g}{t}a")
                r2 = p.op("v", lambda v, b=b: v.tensor_tensor(
                    qk_rope[b][:, :].rearrange(
                        "p (qk h half fr) -> p qk h half fr",
                        qk=2, h=8, half=2)[:, :, :, 0],
                    rtmpA[:, 0:512].rearrange("p (qk h fr) -> p qk h fr",
                                              qk=2, h=8),
                    rtmpA[:, 512:1024].rearrange("p (qk h fr) -> p qk h fr",
                                                 qk=2, h=8),
                    ALU.subtract), after=(r1,), name=f"arp{g}{t}b")
                r3 = p.op("v", lambda v, b=b: v.tensor_tensor(
                    rtmpB[:, :].rearrange("p (a x) -> p a x", a=2),
                    qk_raw[b][:, :].rearrange("p (a x) -> p a x", a=2),
                    stab[b], ALU.mult),
                    after=(evq, evk, ldc, lds, lt.get("rtmpB_rd"),
                           r2),
                    name=f"arp{g}{t}c")
                lt[("rope13", b)] = r3
                lt["rtmpA_rd"] = r2
                r4 = p.op("v", lambda v, b=b: v.tensor_tensor(
                    qk_rope[b][:, :].rearrange("p (qk h half fr) -> p qk h half fr",
                                               qk=2, h=8, half=2)[:, :, :, 1],
                    rtmpB[:, 0:512].rearrange("p (qk h fr) -> p qk h fr",
                                              qk=2, h=8),
                    rtmpB[:, 512:1024].rearrange("p (qk h fr) -> p qk h fr",
                                                 qk=2, h=8),
                    ALU.add), after=(r3, r2), name=f"arp{g}{t}d")
                lt["rtmpB_rd"] = r4
                lt["rtmpA_rd"] = r4
                if RET:
                    lt[("ropetab", 0)] = r4
                    lt[("ropetab", 1)] = r4
                else:
                    lt[("ropetab", b)] = r4
                lt[("rope2", b)] = r4
                lt[("rope", b)] = r4
                if G.get("prep_tick"):
                    pv = G["prep_tick"](lt.get("xnT_ev"), lt.get("qkT_ev"))
                    if pv is not None:
                        lt["prep_ev0"] = pv[0] or lt.get("prep_ev0")
                        lt["prep_ev1"] = pv[1] or lt.get("prep_ev1")
                if tt >= 1:
                    emit_qktr(t - 1)
            emit_qktr(st * 4 + 3)
            nkc = 4 * (st + 1)
            av_last = []
            for pr in range(NPAIR):
                hA, hB = 2 * pr, 2 * pr + 1
                avp = []
                pend = {}

                def emit_avs(kc, last_chunk):
                    rr = kc % NATT
                    e = pend.pop(kc)
                    for hi, h in enumerate((hA, hB)):
                        bankC = PS[4 + hi]
                        dep = [e]
                        if kc == 0:
                            dep += [lt.get(("ynorm", hi)), lt.get("res_last"),
                                    lt.get(("bcE", hi))]
                        av = p.op("t", lambda t_, kc=kc, h=h, hi=hi,
                                  bankC=bankC, rr=rr, last_chunk=last_chunk:
                                  t_.matmul(
                            bankC[0:65, :],
                            v65[:, (kc * NH_G + h) * 65:(kc * NH_G + h) * 65 + 65],
                            att[rr][:, hi * 512:hi * 512 + 512],
                            start=(kc == 0), stop=last_chunk),
                            after=tuple(dep), name=f"av{g}_{st}_{pr}_{kc}_{hi}")
                        avp.append(av)
                    lt[("avr", rr)] = avp[-1]

                for kc in range(nkc):
                    pp = kc % 2
                    rr = kc % NATT
                    psd = G["psum_d"][pp]
                    mmA = p.op("t", lambda t_, pr=pr, kc=kc, pp=pp: t_.matmul(
                        PS[2 * pp][:, :],
                        kT[0:64, pr * TOK + kc * 128:pr * TOK + kc * 128 + 128],
                        qT[0:64, pr * STW:(pr + 1) * STW],
                        start=True, stop=True, tile_position=(0, 0)),
                        after=(lt.get("qT_ev"), lt.get("kT_ev"),
                               lt.get("pevq"), lt.get("pevk"), lt.get("pevv"),
                               lt.get("pevg"), lt.get(("attev", pp))),
                        name=f"sT{g}_{st}_{pr}_{kc}A")
                    mmB = p.op("t", lambda t_, pr=pr, kc=kc, pp=pp: t_.matmul(
                        PS[2 * pp + 1][:, :],
                        kT[64:128, pr * TOK + kc * 128:pr * TOK + kc * 128 + 128],
                        qT[64:128, pr * STW:(pr + 1) * STW],
                        start=True, stop=True, tile_position=(64, 0)),
                        after=(), name=f"sT{g}_{st}_{pr}_{kc}B")
                    ab = att[rr]
                    if cfg["post"] == "softmax":
                        e = p.op("s", lambda s, psd=psd, ab=ab: s.activation(
                            ab[:, :], psd[:, :], AF.Exp, bias=0.0, scale=0.125),
                            after=(mmA, mmB, lt.get(("avr", rr))),
                            name=f"ex{g}_{st}_{pr}_{kc}")
                    else:
                        h_abs = g * NH_G + 2 * pr
                        m = 4 * st - kc + 3
                        eA = p.op("s", lambda s, psd=psd, ab=ab, h_abs=h_abs,
                                  m=m: s.activation(
                                      ab[:, 0:512], psd[:, 0:512], AF.Identity,
                                      bias=0.0,
                                      scale=G["decayB_sb"][:, h_abs * 36 + m:h_abs * 36 + m + 1]),
                                  after=(mmA, lt.get(("avr", rr))),
                                  name=f"dcA{g}_{st}_{pr}_{kc}")
                        e = p.op("v", lambda v, psd=psd, ab=ab, h_abs=h_abs,
                                 m=m: v.tensor_scalar(
                                     ab[:, 512:1024], psd[:, 512:1024],
                                     G["decayB_sb"][:, (h_abs + 1) * 36 + m:(h_abs + 1) * 36 + m + 1],
                                     None, ALU.mult),
                                 after=(mmB, eA, lt.get(("avr", rr))),
                                 name=f"dcB{g}_{st}_{pr}_{kc}")
                    if kc >= 4 * st:
                        oi = kc - 4 * st
                        if RET:
                            eH = p.op("v", lambda v, ab=ab, oi=oi: v.tensor_tensor(
                                ab[:, 0:512], ab[:, 0:512],
                                mask_sb[:, oi * 512:(oi + 1) * 512], ALU.mult),
                                after=(e, mld), name=f"mkA{g}_{st}_{pr}_{kc}")
                            e = p.op("v", lambda v, ab=ab, oi=oi: v.tensor_tensor(
                                ab[:, 512:1024], ab[:, 512:1024],
                                mask_sb[:, oi * 512:(oi + 1) * 512], ALU.mult),
                                after=(eH,), name=f"mkB{g}_{st}_{pr}_{kc}")
                        else:
                            e = p.op("v", lambda v, ab=ab, oi=oi: v.tensor_tensor(
                                ab[:, :], ab[:, :],
                                mask_sb[:, oi * 1024:(oi + 1) * 1024], ALU.mult),
                                after=(e, mld), name=f"mk{g}_{st}_{pr}_{kc}")
                    lt[("attev", pp)] = e
                    pend[kc] = e
                    if kc >= 1:
                        emit_avs(kc - 1, False)
                emit_avs(nkc - 1, True)
                lt["sT_last"] = avp[-1]
                av_last = avp[-2:]
                for hi, h in enumerate((hA, hB)):
                    bankC = PS[4 + hi]
                    if cfg["post"] == "softmax":
                        rc = p.op("v", lambda v, bankC=bankC: v.reciprocal(
                            rec_sb[64:65, :], bankC[64:65, :]),
                            after=(avp[-1], lt.get("rec_rd")),
                            name=f"rc{g}_{st}_{pr}_{hi}")
                        bc = p.op("t", lambda t_, bankC=bankC: t_.matmul(
                            bankC[64:128, :], ones_f32[64:65, 0:64],
                            rec_sb[64:65, :], start=True, stop=True,
                            tile_position=(64, 64)),
                            after=(rc, ones_set), name=f"bc{g}_{st}_{pr}_{hi}")
                        lt["rec_rd"] = bc
                        evb = p.op("s", lambda s, bankC=bankC, hi=hi: s.copy(
                            rec_bc[hi][0:64, :], bankC[64:128, :]),
                            after=(bc, lt.get(("recbc_rd", hi))),
                            name=f"eb{g}_{st}_{pr}_{hi}")
                        lt[("bcE", hi)] = evb
                        yn = p.op("v", lambda v, h=h, hi=hi, bankC=bankC:
                                  v.scalar_tensor_tensor(
                                      yT[0:64, h * STW:(h + 1) * STW],
                                      bankC[0:64, :], 1.0, rec_bc[hi][0:64, :],
                                      ALU.mult, ALU.mult),
                                  after=(evb, lt.get(("yT_rd", h))),
                                  name=f"yn{g}_{st}_{pr}_{hi}")
                        lt[("recbc_rd", hi)] = yn
                        lt[("ynorm", hi)] = yn
                    else:
                        h_abs = g * NH_G + h
                        yev = p.op("s", lambda s, bankC=bankC: s.copy(
                            yraw[0:64, :], bankC[0:64, :]),
                            after=(avp[-1], lt.get("yraw_rd")),
                            name=f"ye{g}_{st}_{pr}_{hi}")
                        ytr = []
                        for sub in range(4):
                            o = p.op("t", lambda t_, sub=sub: t_.transpose(
                                pbf[1][:, sub * 64:(sub + 1) * 64],
                                yraw[0:64, sub * 128:(sub + 1) * 128],
                                G["ident"][0:64, 0:64]),
                                after=(yev, lt.get("ypsum_rd"),
                                       lt.get("qkT_ev")) if sub == 0 else (yev,),
                                name=f"ytr{g}_{st}_{pr}_{hi}_{sub}")
                            ytr.append(o)
                        lt["yraw_rd"] = ytr[-1]
                        yevs = []
                        for sub in range(4):
                            o = p.op("s", lambda s, sub=sub, h=h, h_abs=h_abs:
                                     s.activation(
                                         y_tok[:, sub * STW + h * 64:sub * STW + h * 64 + 64],
                                         pbf[1][:, sub * 64:(sub + 1) * 64],
                                         AF.Identity, bias=0.0,
                                         scale=gammaA_sb[:, h_abs * 4 + sub:h_abs * 4 + sub + 1]),
                                     after=(ytr[-1], lt.get("ytok_rd") if sub == 0 else None),
                                     name=f"ysc{g}_{st}_{pr}_{hi}_{sub}")
                            yevs.append(o)
                        if yevs:
                            lt["ypsum_rd"] = yevs[-1]
                            lt["qkT_ev"] = yevs[-1]
                        lt[("ynorm", hi)] = yevs[-1] if yevs else yev
                        lt[("ytok_w", h)] = yevs[-1]
            if RET:
                ytok_done = [lt[("ytok_w", h)] for h in range(NH_G)]
                if cfg.get("dbg") and g == 0 and st == 0:
                    p.dma("q", G["dbg_yT"][:, 0:4096].bitcast(F32)[:, 0:2048],
                          y_tok[:, :], after=tuple(ytok_done), name="dytok",
                          group="dbg")
                    p.dma("q", G["dbg_qT"], qT[:, :], after=tuple(ytok_done),
                          name="dqT2", group="dbg")
                    p.barrier("dbgbar1")
                gops = []
                for sub in (range(4) if BIS >= 3 else []):
                    sq = p.op("s", lambda s, sub=sub: s.square(
                        gn_y2[:, :], y_tok[:, sub * STW:(sub + 1) * STW]),
                        after=tuple(ytok_done) + (lt.get("y2_rd"),),
                        name=f"gsq{g}_{st}_{sub}")
                    r1 = p.op("v", lambda v, sub=sub: v.reduce_sum(
                        sums[:, sub * 8:(sub + 1) * 8].rearrange("p (a o) -> p a o", o=1),
                        y_tok[:, sub * STW:(sub + 1) * STW].rearrange(
                            "p (a d) -> p a d", a=8), axis=mybir.AxisListType.X),
                        after=tuple(ytok_done) + (lt.get("sums_rd"),),
                        name=f"gr1{g}_{st}_{sub}")
                    r2 = p.op("v", lambda v, sub=sub: v.reduce_sum(
                        sumsq[:, sub * 8:(sub + 1) * 8].rearrange("p (a o) -> p a o", o=1),
                        gn_y2[:, :].rearrange("p (a d) -> p a d", a=8),
                        axis=mybir.AxisListType.X),
                        after=(sq, lt.get("sums_rd")), name=f"gr2{g}_{st}_{sub}")
                    lt["y2_rd"] = r2
                    gops += [r1, r2]
                if BIS < 3:
                    g6 = ytok_done[-1]
                else:
                    g1 = p.op("v", lambda v: v.tensor_scalar_mul(
                        mbuf[:, :], sums[:, :], 1.0 / 64.0),
                        after=tuple(gops), name=f"gm{g}_{st}")
                    g2 = p.op("v", lambda v: v.tensor_tensor(
                        vbuf[:, :], mbuf[:, :], mbuf[:, :], ALU.mult),
                        after=(g1,), name=f"gv1{g}_{st}")
                    g3 = p.op("v", lambda v: v.scalar_tensor_tensor(
                        vbuf[:, :], sumsq[:, :], 1.0 / 64.0, vbuf[:, :],
                        ALU.mult, ALU.subtract), after=(g2,), name=f"gv2{g}_{st}")
                    g4 = p.op("s", lambda s: s.activation(
                        rstd2[:, :], vbuf[:, :], AF.Ln, bias=G["eps"], scale=1.0),
                        after=(g3,), name=f"gl{g}_{st}")
                    g5 = p.op("s", lambda s: s.activation(
                        rstd2[:, :], rstd2[:, :], AF.Exp, bias=0.0, scale=-0.5),
                        after=(g4,), name=f"ge{g}_{st}")
                    g6 = p.op("v", lambda v: v.scalar_tensor_tensor(
                        negm2[:, :], mbuf[:, :], -1.0, rstd2[:, :],
                        ALU.mult, ALU.mult), after=(g5,), name=f"gn{g}_{st}")
                    lt["sums_rd"] = g6
                if cfg.get("dbg") and g == 0 and st == 0:


                    p.dma("q", G["dbg_stats"][:, 32:64], rstd2[:, :], after=(g6,),
                          name="drs", group="dbg")
                    p.barrier("dbgbar3")
                seg_ops = []
                for sub in (range(4) if BIS >= 4 else []):
                    for h in range(NH_G):
                        idx = sub * 8 + h
                        o = p.op("s", lambda s, sub=sub, h=h, idx=idx: s.activation(
                            y_gn[:, sub * STW + h * 64:sub * STW + h * 64 + 64],
                            y_tok[:, sub * STW + h * 64:sub * STW + h * 64 + 64],
                            AF.Identity, bias=negm2[:, idx:idx + 1],
                            scale=rstd2[:, idx:idx + 1]),
                            after=(g6, lt.get("ygn_rd") if idx == 0 else None),
                            name=f"gs{g}_{st}_{sub}_{h}")
                        seg_ops.append(o)
                if not seg_ops:
                    seg_ops = [g6]
                lt["ytok_rd"] = seg_ops[-1]
                if cfg.get("dbg") and g == 0 and st == 0:
                    p.dma("q", G["dbg_gnb"][:, 0:512].bitcast(F32)[:, 0:512][:, 0:512]
                          if False else G["dbg_gnb"][:, 0:512], y_gn[:, 0:512],
                          after=(seg_ops[-1],), name="dseg", group="dbg")
                    p.dma("q", G["dbg_stats"][:, 0:32], negm2[:, :],
                          after=(seg_ops[-1],), name="dnm", group="dbg")
                    p.barrier("dbgbar6")
                gate_done = []
                for sub in (range(4) if BIS >= 5 else []):
                    c1 = p.op("v", lambda v, sub=sub, g=g: v.tensor_tensor(
                        gn_y2[:, :], y_gn[:, sub * STW:(sub + 1) * STW],
                        gnw_bc[:, g * 512:g * 512 + 512], ALU.mult),
                        after=(seg_ops[-1], lt.get("y2_rd"), gn_ready),
                        name=f"gc1{g}_{st}_{sub}")
                    c2 = p.op("v", lambda v, sub=sub, g=g: v.tensor_tensor(
                        gn_y2[:, :], gn_y2[:, :],
                        gnb_bc[:, g * 512:g * 512 + 512], ALU.add),
                        after=(c1,), name=f"gc2{g}_{st}_{sub}")
                    if cfg.get("dbg") and g == 0 and st == 0 and sub == 0:
                        p.dma("q", G["dbg_gnw"][:, 512:1024], gn_y2[:, :],
                              after=(c1,), name="dgy1", group="dbg")
                        p.barrier("dbgbar5")
                        p.dma("q", G["dbg_gnw"][:, 0:512], gn_y2[:, :],
                              after=(c2,), name="dgy2", group="dbg")
                        p.barrier("dbgbar4")
                    c3 = p.op("v", lambda v, sub=sub: v.tensor_tensor(
                        gated[:, sub * STW:(sub + 1) * STW], gn_y2[:, :],
                        gate_st[:, sub * STW:(sub + 1) * STW], ALU.mult),
                        after=(c2, lt.get(("gated_rd", sub))),
                        name=f"gc3{g}_{st}_{sub}")
                    lt["y2_rd"] = c3
                    lt[("gate_rd", sub)] = c3
                    gate_done.append(c3)
                if not gate_done:
                    gate_done = [seg_ops[-1]] * 4
                lt["ygn_rd"] = gate_done[-1]
                trb = []
                for sub in (range(4) if BIS >= 6 else []):
                    bk = pbf[sub % 2]
                    for pr in range(NPAIR):
                        o = p.op("t", lambda t_, sub=sub, pr=pr, bk=bk: t_.transpose(
                            bk[:, pr * 128:(pr + 1) * 128],
                            gated[:, sub * STW + pr * 128:sub * STW + pr * 128 + 128],
                            G["ident"]),
                            after=(gate_done[sub], lt.get(("gtb_ev", sub % 2)),
                                   lt.get("ypsum_rd") if sub <= 1 else None)
                            if pr == 0 else (gate_done[sub],),
                            name=f"gt{g}_{st}_{sub}_{pr}")
                        trb.append(o)
                    ev = p.op("v", lambda v, sub=sub, bk=bk: v.tensor_copy(
                        gatedT[:, :].rearrange("p (pr x) -> p pr x", pr=NPAIR)[
                            :, :, sub * 128:(sub + 1) * 128],
                        bk[:, 0:512].rearrange("p (pr x) -> p pr x", pr=NPAIR)),
                        after=(trb[-1], lt.get(("gatedT_rd", sub)),),
                        name=f"gte{g}_{st}_{sub}")
                    lt[("gtb_ev", sub % 2)] = ev
                    lt[("gated_rd", sub)] = ev
                if BIS >= 6:
                    lt["xnT_ev"] = lt[("gtb_ev", 1)]
                    y_done = [lt[("gtb_ev", 0)], lt[("gtb_ev", 1)]]
                else:
                    y_done = [gate_done[-1], gate_done[-1]]
                if cfg.get("dbg") and g == 0 and st == 0:
                    p.dma("q", G["dbg_v"][:, 0:2048].bitcast(F32)[:, 0:1024],
                          y_gn[:, 0:1024], after=(gate_done[-1],),
                          name="dygn", group="dbg")
                    p.dma("q", G["dbg_gate"], gate_st[:, :],
                          after=(gate_done[-1],), name="dgate", group="dbg")
                    p.dma("q", G["dbg_qrope"][:, 0:512], gated[:, 0:512],
                          after=(gate_done[-1],), name="dgated", group="dbg")
                    p.dma("q", G["dbg_qraw"][:, 0:512], gated[:, 512:1024],
                          after=(gate_done[-1],), name="dgated2", group="dbg")
                    p.barrier("dbgbar2")
            else:
                y_done = [lt[("ynorm", 0)], lt[("ynorm", 1)]]
            if cfg.get("dbg") and g == 0 and st == 0 and not RET:
                p.dma("q", G["dbg_yT"][0:64, :], yT[0:64, :], after=tuple(y_done),
                      name="dyT", group="dbg")
                p.dma("q", G["dbg_qT"], qT[:, :], after=tuple(y_done),
                      name="dqT", group="dbg")
                p.dma("q", G["dbg_kT"], kT[:, 0:NPAIR * STW].rearrange(
                    "p (pr x) -> p pr x", pr=NPAIR), after=tuple(y_done),
                    name="dkT", group="dbg")
                p.dma("q", G["dbg_v"], v65[:, 0:4 * NH_G * 65],
                      after=tuple(y_done), name="dv", group="dbg")
            base = src if g == 0 else dst
            for tt in range(4):
                t = st * 4 + tt
                b = tt % 2
                ldr = p.dma("q", xrb[b], base[t * 128:(t + 1) * 128, :],
                            after=(lt.get(("xr", b)),), name=f"arl{g}_{t}",
                            group=f"axr{b}")
                res_ops = []
                for col in range(2):
                    bank = PS[4 + col]

                    if cfg.get("wo_pairs"):
                        def mk_op(bank=bank, col=col, tt=tt):
                            def emit(t_):
                                r = None
                                for pr in range(NPAIR):
                                    r = t_.matmul(
                                        bank[:, :],
                                        gatedT[:, pr * STW + tt * 128:pr * STW + tt * 128 + 128],
                                        wo[:, pr * D + col * 512:pr * D + col * 512 + 512],
                                        start=(pr == 0), stop=(pr == NPAIR - 1))
                                return r
                            return emit
                    else:
                        def mk_op(bank=bank, col=col, tt=tt):
                            def emit(t_):
                                r = None
                                for h in range(NH_G):
                                    r = t_.matmul(
                                        bank[:, :],
                                        yT[0:64, h * STW + tt * 128:h * STW + tt * 128 + 128],
                                        wo[0:64, h * D + col * 512:h * D + col * 512 + 512],
                                        start=(h == 0), stop=(h == NH_G - 1))
                                return r
                            return emit
                    dep = [y_done[0], y_done[1], w_ready, lt.get(("res_bank", col))]
                    mo = p.op("t", mk_op(), after=tuple(dep),
                              name=f"op{g}_{st}_{tt}_{col}")
                    lt[("yT_rd", 2 * col)] = mo
                    lt[("yT_rd", 2 * col + 1)] = mo
                    ro = p.op("v", lambda v, col=col, b=b, bank=bank, tt=tt:
                              v.scalar_tensor_tensor(
                                  xo_buf[tt % 2][:, col * 512:(col + 1) * 512],
                                  bank[:, :], 1.0,
                                  xrb[b][:, col * 512:(col + 1) * 512],
                                  ALU.mult, ALU.add),
                              after=(mo, ldr, lt.get(("xo_st", tt % 2))),
                              name=f"ro{g}_{st}_{tt}_{col}")
                    lt[("res_bank", col)] = ro
                    res_ops.append(ro)
                for h in range(NH_G):
                    lt[("yT_rd", h)] = res_ops[-1]
                for sub in range(4):
                    lt[("gatedT_rd", sub)] = res_ops[-1]
                lt[("xr", b)] = res_ops[-1]
                lt["res_last"] = res_ops[-1]
                sto = p.dma("q", dst[t * 128:(t + 1) * 128, :], xo_buf[tt % 2],
                            after=tuple(res_ops), name=f"ast{g}_{t}",
                            group=f"ast{tt % 2}")
                lt[("xo_st", tt % 2)] = sto
                st_stores.append(sto)
        p.barrier(f"attnpass{g}")
        stores_all = st_stores
    ar.pop()
    return stores_all


def stage_moe(p, ar, G, src, dst, barrier):
    F_ = 2048
    NFC = F_ // 128
    QW = min(1024, TOK)
    NQ = TOK // QW
    TPQ = QW // 128
    NHALF = QW // 512
    PS = G["psum_f32"]
    pbf = G["psum_bf"]
    ar.push()
    rstd1, negm1, stat1 = ln_stats_pass(p, ar, G, src)
    stats2m = ar.alloc(NT, F32, "m2")
    stats2v = ar.alloc(NT, F32, "v2")
    s12 = ar.alloc(NT, F32, "s12")
    b12 = ar.alloc(NT, F32, "b12")
    bn2 = ar.alloc(14, F32, "mbn")
    ln1b = ar.alloc(D, F32, "ln1")
    xb2 = [ar.alloc(D, F32, f"mxb{i}") for i in range(2)]
    lt = {}
    last = None
    prev_use = [None, None]
    for t in range(NT):
        b = t % 2
        ld = p.dma("q", xb2[b], src[t * 128:(t + 1) * 128, :],
                   after=(stat1, prev_use[b]), name=f"mld{t}", group=f"mxb{b}")
        af = p.op("s", lambda s, b=b, t=t: s.activation(
            ln1b[:, :], xb2[b][:, :], AF.Identity,
            bias=negm1[:, t:t + 1], scale=rstd1[:, t:t + 1]),
            after=(ld, stat1, last), name=f"maf{t}")
        b1 = p.op("v", lambda v: v.bn_stats(bn2[:, 0:6], ln1b[:, 0:512]),
                  after=(af, last), name=f"mb1{t}")
        b2 = p.op("v", lambda v: v.bn_stats(bn2[:, 6:12], ln1b[:, 512:1024]),
                  after=(af,), name=f"mb2{t}")
        ba = p.op("v", lambda v: v.bn_aggr(bn2[:, 12:14], bn2[:, 0:12]),
                  after=(b1, b2), name=f"mba{t}")
        c1 = p.op("v", lambda v, t=t: v.tensor_copy(stats2m[:, t:t + 1],
                                                    bn2[:, 12:13]),
                  after=(ba,), name=f"mc1{t}")
        c2 = p.op("v", lambda v, t=t: v.tensor_copy(stats2v[:, t:t + 1],
                                                    bn2[:, 13:14]),
                  after=(ba,), name=f"mc2{t}")
        prev_use[b] = af
        last = c2
    o1 = p.op("s", lambda s: s.activation(s12[:, :], stats2v[:, :], AF.Ln,
                                          bias=G["eps"], scale=1.0),
              after=(last, G["consts_ready"]), name="mlnv")
    o2 = p.op("s", lambda s: s.activation(s12[:, :], s12[:, :], AF.Exp,
                                          bias=0.0, scale=-0.5), after=(o1,),
              name="mexpv")
    o3 = p.op("v", lambda v: v.tensor_tensor(b12[:, :], negm1[:, :],
                                             stats2m[:, :], ALU.subtract),
              after=(o2, last), name="mb12a")
    o4 = p.op("v", lambda v: v.tensor_tensor(b12[:, :], b12[:, :], s12[:, :],
                                             ALU.mult), after=(o3,), name="mb12b")
    o5 = p.op("v", lambda v: v.tensor_tensor(s12[:, :], s12[:, :], rstd1[:, :],
                                             ALU.mult), after=(o4,), name="ms12")
    stat2 = o5

    gb_row = ar.alloc(D, F32, "mgrow")
    gam_bc = ar.alloc(D, F32, "mgam")
    bet_bc = ar.alloc(D, F32, "mbet")
    ones_f = ar.alloc(128, F32, "mones")
    ones_set = p.op("g", lambda g_: g_.memset(ones_f[:, :], 1.0), name="mone")
    d3 = p.dma("q", gb_row[0:1, :],
               G["L1_ffn_ln_gamma"].rearrange("(a c) -> a c", a=1),
               name="mgl", group="const")
    bc1 = p.op("t", lambda t_: t_.matmul(PS[0][:, :], ones_f[0:1, 0:128],
                                         gb_row[0:1, 0:512], start=True,
                                         stop=True),
               after=(d3, ones_set, G["setup"]), name="mbc1")
    bc2 = p.op("t", lambda t_: t_.matmul(PS[1][:, :], ones_f[0:1, 0:128],
                                         gb_row[0:1, 512:1024], start=True,
                                         stop=True),
               after=(d3, ones_set), name="mbc2")
    e1 = p.op("s", lambda s: s.copy(gam_bc[:, 0:512], PS[0][:, :]),
              after=(bc1,), name="mge1")
    e2 = p.op("s", lambda s: s.copy(gam_bc[:, 512:1024], PS[1][:, :]),
              after=(bc2,), name="mge2")
    d4 = p.dma("q", gb_row[0:1, :],
               G["L1_ffn_ln_beta"].rearrange("(a c) -> a c", a=1),
               after=(bc1, bc2), name="mbl", group="const")
    bc3 = p.op("t", lambda t_: t_.matmul(PS[0][:, :], ones_f[0:1, 0:128],
                                         gb_row[0:1, 0:512], start=True,
                                         stop=True), after=(d4, e1), name="mbc3")
    bc4 = p.op("t", lambda t_: t_.matmul(PS[1][:, :], ones_f[0:1, 0:128],
                                         gb_row[0:1, 512:1024], start=True,
                                         stop=True), after=(d4, e2), name="mbc4")
    e3 = p.op("s", lambda s: s.copy(bet_bc[:, 0:512], PS[0][:, :]),
              after=(bc3,), name="mge3")
    e4 = p.op("s", lambda s: s.copy(bet_bc[:, 512:1024], PS[1][:, :]),
              after=(bc4,), name="mge4")
    gb_ready = p.op("q", lambda h: h.nop(), after=(e3, e4), name="mgbr")

    gw_sb = ar.alloc(64, BF16, "mgw")
    gwld = p.dma("q", gw_sb.rearrange("p (k c) -> p k c", k=8),
                 G["gate_wT"].rearrange("(k pp) c -> pp k c", pp=128),
                 name="mgwl", group="const")

    rn_f = ln1b
    rn_bf = ar.alloc(D, BF16, "mrnb")
    rnT_t = ar.alloc(D, BF16, "mrntt")
    comb = ar.alloc(NT * 8, F32, "mcomb")
    rbuf = ar.alloc(8, F32, "mrb")
    mbuf = ar.alloc(8, F32, "mmk")
    tb1 = ar.alloc(8, F32, "mtb")
    rcol = ar.alloc(4, F32, "mrcl")

    def emit_rn(t, b, deps, name):
        a1 = p.op("s", lambda s, b=b, t=t: s.activation(
            rn_f[:, :], xb2[b][:, :], AF.Identity,
            bias=b12[:, t:t + 1], scale=s12[:, t:t + 1]),
            after=deps + (stat2, lt.get("rnf_rd")), name=name + "a")
        a2 = p.op("v", lambda v: v.tensor_tensor(rn_f[:, :], rn_f[:, :],
                                                 gam_bc[:, :], ALU.mult),
                  after=(a1, gb_ready), name=name + "b")
        a3 = p.op("v", lambda v: v.tensor_tensor(rn_bf[:, :], rn_f[:, :],
                                                 bet_bc[:, :], ALU.add),
                  after=(a2, lt.get("rnbf_rd")), name=name + "c")
        lt["rnf_rd"] = a3
        return a3

    prev_ld = [stat2, stat2]
    for t in range(NT):
        b = t % 2
        ld = p.dma("q", xb2[b], src[t * 128:(t + 1) * 128, :],
                   after=(prev_ld[b],), name=f"mrld{t}", group=f"mxb{b}")
        rn = emit_rn(t, b, (ld,), f"mrn{t}")
        prev_ld[b] = rn
        trs = []
        for c in range(8):
            o = p.op("t", lambda t_, c=c: t_.transpose(
                pbf[0][:, c * 128:(c + 1) * 128],
                rn_bf[:, c * 128:(c + 1) * 128], G["ident"]),
                after=(rn, G["setup"], lt.get("pbf0_rd")) if c == 0 else (rn,),
                name=f"mtr{t}_{c}")
            trs.append(o)
        lt["rnbf_rd"] = trs[-1]
        evt = p.op("v", lambda v: v.tensor_copy(rnT_t[:, :], pbf[0][:, :]),
                   after=(trs[-1], lt.get("rntt_rd")), name=f"mte{t}")
        lt["pbf0_rd"] = evt

        def mk_logit():
            def emit(t_):
                r = None
                for c in range(8):
                    r = t_.matmul(PS[2][:, 0:8],
                                  rnT_t[:, c * 128:(c + 1) * 128],
                                  gw_sb[:, c * 8:(c + 1) * 8],
                                  start=(c == 0), stop=(c == 7))
                return r
            return emit
        lg = p.op("t", mk_logit(), after=(evt, gwld, lt.get("lg_rd")),
                  name=f"mlg{t}")
        lt["rntt_rd"] = lg
        s1 = p.op("v", lambda v: v.tensor_reduce(
            rcol[:, 0:1], PS[2][:, 0:8], mybir.AxisListType.X, ALU.max),
            after=(lg, lt.get("rt_done")), name=f"mmx{t}")
        s2 = p.op("v", lambda v: v.tensor_scalar_mul(rcol[:, 1:2],
                                                     rcol[:, 0:1], -1.0),
                  after=(s1,), name=f"mnm{t}")
        s3 = p.op("s", lambda s: s.activation(rbuf[:, :], PS[2][:, 0:8],
                                              AF.Exp, bias=rcol[:, 1:2],
                                              scale=1.0),
                  after=(s2,), name=f"mex{t}")
        lt["lg_rd"] = s3
        s4 = p.op("v", lambda v: v.tensor_reduce(
            rcol[:, 2:3], rbuf[:, :], mybir.AxisListType.X, ALU.add),
            after=(s3,), name=f"msm{t}")
        s5 = p.op("v", lambda v: v.reciprocal(rcol[:, 3:4], rcol[:, 2:3]),
                  after=(s4,), name=f"mrc{t}")
        t1 = p.op("v", lambda v: v.tensor_reduce(
            rcol[:, 0:1], rbuf[:, :], mybir.AxisListType.X, ALU.max),
            after=(s5,), name=f"mt1{t}")
        t2 = p.op("v", lambda v: v.tensor_scalar(
            mbuf[:, :], rbuf[:, :], rcol[:, 0:1], None, ALU.is_ge),
            after=(t1,), name=f"mt2{t}")
        t3 = p.op("v", lambda v: v.tensor_tensor(tb1[:, :], rbuf[:, :],
                                                 mbuf[:, :], ALU.mult),
                  after=(t2,), name=f"mt3{t}")
        t4 = p.op("v", lambda v: v.tensor_tensor(tb1[:, :], rbuf[:, :],
                                                 tb1[:, :], ALU.subtract),
                  after=(t3,), name=f"mt4{t}")
        t5 = p.op("v", lambda v: v.tensor_reduce(
            rcol[:, 0:1], tb1[:, :], mybir.AxisListType.X, ALU.max),
            after=(t4,), name=f"mt5{t}")
        t6 = p.op("v", lambda v: v.tensor_scalar(
            tb1[:, :], tb1[:, :], rcol[:, 0:1], None, ALU.is_ge),
            after=(t5,), name=f"mt6{t}")
        t7 = p.op("v", lambda v: v.tensor_tensor(mbuf[:, :], mbuf[:, :],
                                                 tb1[:, :], ALU.add),
                  after=(t6,), name=f"mt7{t}")
        t8 = p.op("v", lambda v: v.tensor_tensor(mbuf[:, :], rbuf[:, :],
                                                 mbuf[:, :], ALU.mult),
                  after=(t7,), name=f"mt8{t}")
        t9 = p.op("v", lambda v, t=t: v.tensor_scalar(
            comb[:, t * 8:(t + 1) * 8], mbuf[:, :], rcol[:, 3:4], None,
            ALU.mult), after=(t8, lt.get("comb_rd")), name=f"mt9{t}")
        lt["rt_done"] = t9
    router_done = lt["rt_done"]

    rnT = ar.alloc(8 * QW, BF16, "mrnT")
    acc = ar.alloc(TPQ * D, F32, "macc")
    hT = ar.alloc(NFC * 512, BF16, "mhT")
    wb = [ar.alloc(16384, BF16, f"mw{i}") for i in range(3)]
    tmp_y = [ar.alloc(512, F32, f"mty{i}") for i in range(2)]
    stores = []

    w1v = G["moe_w1T"].rearrange("e (k pp) f -> e pp k f", pp=128)
    w2v = G["moe_w2T"].rearrange("e (k pp) c -> e pp k c", pp=128)
    wslot_user = [None, None, None]
    wload = {}

    def load_w(e, which, dep):
        si = (2 * e + (0 if which == 1 else 1)) % 3
        sq = ("q", "s", "g")[si]
        if which == 1:
            dma = p.dma(sq, wb[si].rearrange("p (k f) -> p k f", k=8),
                        w1v[e], after=(wslot_user[si], dep, barrier),
                        name=f"mwl1_{e}", group=f"mwg{si}")
        else:
            dma = p.dma(sq, wb[si].rearrange("p (k c) -> p k c", k=16),
                        w2v[e], after=(wslot_user[si], dep, barrier),
                        name=f"mwl2_{e}", group=f"mwg{si}")
        wload[(e, which)] = dma
        return si

    for q in range(NQ):
        rn_done = []
        for tt in range(TPQ):
            t = q * TPQ + tt
            b = tt % 2
            ld = p.dma("q", xb2[b], src[t * 128:(t + 1) * 128, :],
                       after=(lt.get(("mxa", b)), lt.get(("mxr", b)),
                              router_done),
                       name=f"mqld{t}", group=f"mxb{b}")
            rn = emit_rn(t, b, (ld,), f"mqrn{t}")
            ai = p.op("v", lambda v, tt=tt, b=b: v.tensor_copy(
                acc[:, tt * D:(tt + 1) * D], xb2[b][:, :]),
                after=(ld, lt.get(("acc_rd", tt)),), name=f"macc{q}_{tt}")
            lt[("mxa", b)] = ai
            lt[("mxr", b)] = rn
            lt[("macc_i", tt)] = ai
            trs = []
            for c in range(8):
                o = p.op("t", lambda t_, c=c: t_.transpose(
                    pbf[0][:, c * 128:(c + 1) * 128],
                    rn_bf[:, c * 128:(c + 1) * 128], G["ident"]),
                    after=(rn, lt.get("pbf0_rd")) if c == 0 else (rn,),
                    name=f"mqtr{t}_{c}")
                trs.append(o)
            lt["rnbf_rd"] = trs[-1]
            ev = p.op("v", lambda v, tt=tt: v.tensor_copy(
                rnT[:, :].rearrange("p (c k) -> p c k", c=8)[:, :, tt * 128:(tt + 1) * 128],
                pbf[0][:, :].rearrange("p (c k) -> p c k", c=8)),
                after=(trs[-1], lt.get("rnT_rd")), name=f"mqev{t}")
            lt["pbf0_rd"] = ev
            rn_done.append(ev)
        for e in range(8):
            if q == 0 and e == 0:
                s1i = load_w(0, 1, None)
                s2i = load_w(0, 2, None)
            else:
                s1i = (2 * e) % 3
                s2i = (2 * e + 1) % 3
            h_ops = []
            y_ops_all = []
            for half in range(NHALF):
                hev = []
                for fc in range(NFC):
                    bank = PS[fc % 2]

                    def mk_h(bank=bank, fc=fc, half=half, s1i=s1i):
                        def emit(t_):
                            r = None
                            for c in range(8):
                                r = t_.matmul(
                                    bank[:, :],
                                    wb[s1i][:, c * F_ + fc * 128:c * F_ + fc * 128 + 128],
                                    rnT[:, c * QW + half * 512:c * QW + half * 512 + 512],
                                    start=(c == 0), stop=(c == 7))
                            return r
                        return emit
                    dep = [rn_done[-1], wload[(e, 1)], lt.get(("hps", fc % 2))]
                    if half == 0 and fc == 0:
                        dep.append(lt.get("y_done"))
                    mm = p.op("t", mk_h(), after=tuple(dep),
                              name=f"mh{q}_{e}_{half}_{fc}")
                    h_ops.append(mm)
                    ge = p.op("s", lambda s, bank=bank, fc=fc: s.activation(
                        hT[:, fc * 512:(fc + 1) * 512], bank[:, :], AF.Gelu,
                        bias=0.0, scale=1.0),
                        after=(mm, lt.get(("hT_rd", fc))),
                        name=f"mg{q}_{e}_{half}_{fc}")
                    lt[("hps", fc % 2)] = ge
                    hev.append(ge)
                for sub in range(4):
                    t_glob = q * TPQ + half * 4 + sub
                    for col in range(2):
                        ybank = PS[2 + ((sub * 2 + col) % 4)]

                        def mk_y(ybank=ybank, sub=sub, col=col, s2i=s2i):
                            def emit(t_):
                                r = None
                                for fc in range(NFC):
                                    r = t_.matmul(
                                        ybank[:, :],
                                        hT[:, fc * 512 + sub * 128:fc * 512 + sub * 128 + 128],
                                        wb[s2i][:, fc * D + col * 512:fc * D + col * 512 + 512],
                                        start=(fc == 0), stop=(fc == NFC - 1))
                                    pass
                                return r
                            return emit
                        dep = [hev[-1], wload[(e, 2)],
                               lt.get(("yps", (sub * 2 + col) % 4))]
                        my = p.op("t", mk_y(), after=tuple(dep),
                                  name=f"my{q}_{e}_{half}_{sub}_{col}")
                        y_ops_all.append(my)
                        tb = tmp_y[(sub * 2 + col) % 2]
                        ev1 = p.op("s", lambda s, ybank=ybank, tb=tb, t_glob=t_glob,
                                   e=e: s.activation(
                                       tb[:, :], ybank[:, :], AF.Identity,
                                       bias=0.0,
                                       scale=comb[:, t_glob * 8 + e:t_glob * 8 + e + 1]),
                                   after=(my, lt.get(("tmp_rd", (sub * 2 + col) % 2))),
                                   name=f"mye{q}_{e}_{half}_{sub}_{col}")
                        lt[("yps", (sub * 2 + col) % 4)] = ev1
                        add = p.op("v", lambda v, tb=tb, half=half, sub=sub,
                                   col=col: v.tensor_tensor(
                                       acc[:, (half * 4 + sub) * D + col * 512:
                                           (half * 4 + sub) * D + col * 512 + 512],
                                       acc[:, (half * 4 + sub) * D + col * 512:
                                           (half * 4 + sub) * D + col * 512 + 512],
                                       tb[:, :], ALU.add),
                                   after=(ev1,
                                          lt.get(("accadd", half * 4 + sub, col)),
                                          lt.get(("macc_i", half * 4 + sub))),
                                   name=f"mya{q}_{e}_{half}_{sub}_{col}")
                        lt[("tmp_rd", (sub * 2 + col) % 2)] = add
                        lt[("accadd", half * 4 + sub, col)] = add
                for fc in range(NFC):
                    lt[("hT_rd", fc)] = y_ops_all[-1]
            lt["y_done"] = y_ops_all[-1]
            lt["rnT_rd_cand"] = h_ops[-1]
            wslot_user[s1i] = h_ops[-1]
            wslot_user[s2i] = y_ops_all[-1]
            if e < 7:
                load_w(e + 1, 1, None)
                load_w(e + 1, 2, None)
            elif q < NQ - 1:
                load_w(0, 1, None)
                load_w(0, 2, None)
        lt["rnT_rd"] = lt["rnT_rd_cand"]
        for tt in range(TPQ):
            t = q * TPQ + tt
            st = p.dma("q", dst[t * 128:(t + 1) * 128, :],
                       acc[:, tt * D:(tt + 1) * D],
                       after=(lt.get(("accadd", tt, 0)), lt.get(("accadd", tt, 1))),
                       name=f"mst{t}", group=f"mst{tt % 2}")
            lt[("acc_rd", tt)] = st
            stores.append(st)
    ar.pop()
    return stores


def stage_l1b(p, ar, G, src, dst, barrier):
    PS = G["psum_f32"]
    pbf = G["psum_bf"]
    ar.push()
    rstd, negmr, stat_done = ln_stats_pass(p, ar, G, src)
    wq = ar.alloc(8 * D, BF16, "bwq")
    wdn = ar.alloc(8 * 256, BF16, "bwdn")
    wup = ar.alloc(2 * 2048, BF16, "bwup")
    wout = ar.alloc(16 * D, BF16, "bwout")
    bq_r = ar.alloc(D, BF16, "bbq")
    bdn_r = ar.alloc(256, BF16, "bbdn")
    bup_r = ar.alloc(2048, BF16, "bbup")
    bout_r = ar.alloc(D, BF16, "bbout")
    ones_b = ar.alloc(128, BF16, "bones")
    mask_sb = ar.alloc(128, F32, "bmask")
    xb = [ar.alloc(D, F32, f"bxb{i}") for i in range(2)]
    xrb = [ar.alloc(D, F32, f"bxr{i}") for i in range(2)]
    xn_buf = ar.alloc(D, BF16, "bxn")
    xnT = ar.alloc(8 * 128, BF16, "bxnT")
    q_raw = ar.alloc(D, BF16, "bqraw")
    q_rope = ar.alloc(D, BF16, "bqrope")
    k_sep = ar.alloc(D, BF16, "bksep")
    k_rope = ar.alloc(D, BF16, "bkrope")
    v_nat = ar.alloc(D, BF16, "bvnat")
    lat_bf = ar.alloc(256, BF16, "blat")
    latT = ar.alloc(256, BF16, "blatT")
    tmp1 = ar.alloc(512, BF16, "btmp1")
    tmp2 = ar.alloc(512, BF16, "btmp2")
    crep_t = [ar.alloc(512, BF16, f"bcr{i}") for i in range(2)]
    srep_t = [ar.alloc(512, BF16, f"bsr{i}") for i in range(2)]
    Q2 = [ar.alloc(D, BF16, f"bQ2{i}") for i in range(2)]
    K2 = [ar.alloc(D, BF16, f"bK2{i}") for i in range(2)]
    V2 = [ar.alloc(D, BF16, f"bV2{i}") for i in range(2)]
    Q2T = ar.alloc(8 * 128, BF16, "bQ2T")
    K2T = ar.alloc(8 * 128, BF16, "bK2T")
    smsk = ar.alloc(128, F32, "bsmsk")
    att_b = ar.alloc(128, BF16, "batt")
    attn_b = [ar.alloc(128, BF16, f"batn{i}") for i in range(2)]
    attT_b = [ar.alloc(128, BF16, f"batT{i}") for i in range(2)]
    rsum = ar.alloc(2, F32, "brs")
    o2Ty = ar.alloc(16 * 128, BF16, "bo2T")
    xo_buf = [ar.alloc(D, F32, f"bxo{i}") for i in range(2)]

    ones_set = p.op("g", lambda g_: g_.memset(ones_b[:, :], 1.0), name="bON")
    mld = p.dma("q", mask_sb, G["tbl_l1bmask"], name="bMSK", group="const")

    for c in range(8):
        p.dma("g", wq[:, c * D:(c + 1) * D],
              G["L1_attn_b_Wq"][c * 128:(c + 1) * 128, :],
              name=f"bwq{c}", group="bw")
    wq_ready = p.dma("g", bq_r[0:1, :],
                     G["L1_attn_b_bq"].rearrange("(a c) -> a c", a=1),
                     name="bbq", group="bw")
    for c in range(8):
        p.dma("g", wdn[:, c * 256:(c + 1) * 256],
              G["L1_attn_b_Wdown"][c * 128:(c + 1) * 128, :],
              name=f"bwd{c}", group="bw")
    wdn_ready = p.dma("g", bdn_r[0:1, :],
                      G["L1_attn_b_bdown"].rearrange("(a c) -> a c", a=1),
                      name="bbd", group="bw")
    for c in range(2):
        p.dma("g", wup[:, c * 2048:(c + 1) * 2048],
              G["L1_attn_b_Wup"][c * 128:(c + 1) * 128, :],
              name=f"bwu{c}", group="bw")
    wup_ready = p.dma("g", bup_r[0:1, :],
                      G["L1_attn_b_bup"].rearrange("(a c) -> a c", a=1),
                      name="bbu", group="bw")
    for h in range(16):
        p.dma("g", wout[0:64, h * D:(h + 1) * D],
              G["L1_attn_b_Wout"][h * 64:(h + 1) * 64, :],
              name=f"bwo{h}", group="bw")
    w_ready = p.dma("g", bout_r[0:1, :],
                    G["L1_attn_b_bout"].rearrange("(a c) -> a c", a=1),
                    name="bbo", group="bw")
    htmp = G["l1b_tmp"]

    lt = {}
    stores = []

    def phase1(t):
        b = t % 2
        ld = p.dma("q", xb[b], src[t * 128:(t + 1) * 128, :],
                   after=(stat_done, lt.get(("xb", b))), name=f"bld{t}",
                   group=f"bxb{b}")
        ldc = p.dma("q", crep_t[b], G["tbl_crep"][t * 128:(t + 1) * 128, :],
                    after=(lt.get(("rope", b)),), name=f"bcrl{t}",
                    group=f"bcr{b}")
        lds = p.dma("q", srep_t[b], G["tbl_srep"][t * 128:(t + 1) * 128, :],
                    after=(lt.get(("rope", b)),), name=f"bsrl{t}",
                    group=f"bcr{b}")
        xo = p.op("s", lambda s, b=b, t=t: s.activation(
            xn_buf[:, :], xb[b][:, :], AF.Identity,
            bias=negmr[:, t:t + 1], scale=rstd[:, t:t + 1]),
            after=(ld, stat_done, lt.get("xn_tr")), name=f"bxn{t}")
        lt[("xb", b)] = xo
        trs = []
        for c in range(8):
            o = p.op("t", lambda t_, c=c: t_.transpose(
                pbf[0][:, c * 128:(c + 1) * 128],
                xn_buf[:, c * 128:(c + 1) * 128], G["ident"]),
                after=(xo, lt.get("xnT_ev"), lt.get(("atps", 0)),
                       lt.get(("atps", 1)), G["setup"]) if c == 0 else (xo,),
                name=f"btr{t}_{c}")
            trs.append(o)
        lt["xn_tr"] = trs[-1]
        ev = p.op("v", lambda v: v.tensor_copy(
            xnT[:, :].rearrange("p (c k) -> p c k", c=8),
            pbf[0][:, :].rearrange("p (c k) -> p c k", c=8)),
            after=(trs[-1], lt.get("xnT_rd")), name=f"btrev{t}")
        lt["xnT_ev"] = ev

        qmm = []
        for colh in range(2):
            bank = PS[colh]

            def mk_q(colh=colh, bank=bank):
                def emit(t_):
                    r = None
                    wv_ = wq[:, :].rearrange("p (c h half fr) -> p c half h fr",
                                             c=8, half=2, h=16, fr=32)
                    for c in range(8):
                        r = t_.matmul(bank[:, :], xnT[:, c * 128:(c + 1) * 128],
                                      wv_[:, c, colh], start=(c == 0),
                                      stop=False)
                    r = t_.matmul(bank[:, :], ones_b[0:1, 0:128],
                                  bq_r[0:1, :].rearrange(
                                      "a (h half fr) -> a half h fr", half=2,
                                      h=16, fr=32)[:, colh],
                                  start=False, stop=True)
                    return r
                return emit
            mm = p.op("t", mk_q(), after=(ev, wq_ready, ones_set,
                                          lt.get(f"qps{colh}"),
                                          lt.get(("sps", colh))),
                      name=f"bq{t}_{colh}")
            qmm.append(mm)
        evqa = p.op("s", lambda s: s.copy(q_raw[:, 0:512], PS[0][:, :]),
                    after=(qmm[0], lt.get("qraw_rd")), name=f"bevq{t}a")
        evqb = p.op("s", lambda s: s.copy(q_raw[:, 512:1024], PS[1][:, :]),
                    after=(qmm[1],), name=f"bevq{t}b")
        lt["qps0"] = evqa
        lt["qps1"] = evqb

        def mk_lat():
            def emit(t_):
                r = None
                for c in range(8):
                    r = t_.matmul(PS[2][:, 0:256],
                                  xnT[:, c * 128:(c + 1) * 128],
                                  wdn[:, c * 256:(c + 1) * 256],
                                  start=(c == 0), stop=False)
                r = t_.matmul(PS[2][:, 0:256], ones_b[0:1, 0:128],
                              bdn_r[0:1, :], start=False, stop=True)
                return r
            return emit
        lmm = p.op("t", mk_lat(), after=(ev, wdn_ready, lt.get("latps"),
                                         lt.get(("sps", 2))),
                   name=f"blat{t}")
        evl = p.op("s", lambda s: s.copy(lat_bf[:, :], PS[2][:, 0:256]),
                   after=(lmm, lt.get("lat_rd")), name=f"bevl{t}")
        lt["latps"] = evl
        lt["xnT_rd"] = lmm
        ltr = []
        for c in range(2):
            o = p.op("t", lambda t_, c=c: t_.transpose(
                pbf[1][:, c * 128:(c + 1) * 128],
                lat_bf[:, c * 128:(c + 1) * 128], G["ident"]),
                after=(evl, lt.get("pbf1_rd")) if c == 0 else (evl,),
                name=f"bltr{t}_{c}")
            ltr.append(o)
        lt["lat_rd"] = ltr[-1]
        evlT = p.op("v", lambda v: v.tensor_copy(latT[:, :], pbf[1][:, 0:256]),
                    after=(ltr[-1], lt.get("latT_rd")), name=f"bevlt{t}")
        lt["pbf1_rd"] = evlT

        kv_specs = [("k1", 0, 32, PS[2], slice(0, 16)),
                    ("k2", 32, 32, PS[3], slice(0, 16)),
                    ("va", 64, 64, PS[4], slice(0, 8)),
                    ("vb", 64, 64, PS[5], slice(8, 16))]
        kv_mm = []
        for si, (nm, joff, jw, bank, hsl) in enumerate(kv_specs):
            def mk_kv(nm=nm, joff=joff, jw=jw, bank=bank, hsl=hsl):
                def emit(t_):
                    r = None
                    for c in range(2):
                        wv_ = wup[:, c * 2048:(c + 1) * 2048].rearrange(
                            "p (h j) -> p h j", h=16)[:, hsl, joff:joff + jw]
                        r = t_.matmul(bank[:, :],
                                      latT[:, c * 128:(c + 1) * 128], wv_,
                                      start=(c == 0), stop=False)
                    bv = bup_r[0:1, :].rearrange("a (h j) -> a h j",
                                                 h=16)[:, hsl, joff:joff + jw]
                    r = t_.matmul(bank[:, :], ones_b[0:1, 0:128], bv,
                                  start=False, stop=True)
                    return r
                return emit
            mm = p.op("t", mk_kv(), after=(evlT, wup_ready,
                                           lt.get(f"kvps{si}"),
                                           (None, lt.get(("sps", 3)),
                                            lt.get(("avps", 0)),
                                            lt.get(("avps", 1)))[si]),
                      name=f"bkv{t}_{nm}")
            kv_mm.append(mm)
        evk1 = p.op("s", lambda s: s.copy(k_sep[:, 0:512], PS[2][:, :]),
                    after=(kv_mm[0], lt.get("ksep_rd")), name=f"bevk{t}a")
        evk2 = p.op("s", lambda s: s.copy(k_sep[:, 512:1024], PS[3][:, :]),
                    after=(kv_mm[1],), name=f"bevk{t}b")
        evv1 = p.op("s", lambda s: s.copy(v_nat[:, 0:512], PS[4][:, :]),
                    after=(kv_mm[2], lt.get("vnat_rd")), name=f"bevv{t}a")
        evv2 = p.op("s", lambda s: s.copy(v_nat[:, 512:1024], PS[5][:, :]),
                    after=(kv_mm[3],), name=f"bevv{t}b")
        lt["kvps0"] = evk1
        lt["kvps1"] = evk2
        lt["kvps2"] = evv1
        lt["kvps3"] = evv2

        def rope16(raw, out_buf, evd, name, b=b, ldc=ldc, lds=lds):
            cr, sr = crep_t[b], srep_t[b]
            o1 = p.op("v", lambda v, cr=cr: v.tensor_tensor(
                tmp1[:, :], raw[:, 0:512], cr[:, :], ALU.mult),
                after=(ldc, lds, evd, lt.get("rope_tmp")), name=name + "a")
            o2 = p.op("v", lambda v, sr=sr: v.tensor_tensor(
                tmp2[:, :], raw[:, 512:1024], sr[:, :], ALU.mult),
                name=name + "b")
            rv = out_buf[:, :].rearrange("p (h half fr) -> p half h fr",
                                         half=2, h=16, fr=32)
            o3 = p.op("v", lambda v, rv=rv: v.tensor_tensor(
                rv[:, 0], tmp1[:, :].rearrange("p (h fr) -> p h fr", h=16),
                tmp2[:, :].rearrange("p (h fr) -> p h fr", h=16),
                ALU.subtract), after=(o1, o2, lt.get(("rope_dst", name[:3]))),
                name=name + "c")
            o4 = p.op("v", lambda v, sr=sr: v.tensor_tensor(
                tmp1[:, :], raw[:, 0:512], sr[:, :], ALU.mult),
                after=(o3,), name=name + "d")
            o5 = p.op("v", lambda v, cr=cr: v.tensor_tensor(
                tmp2[:, :], raw[:, 512:1024], cr[:, :], ALU.mult),
                after=(o3,), name=name + "e")
            o6 = p.op("v", lambda v, rv=rv: v.tensor_tensor(
                rv[:, 1], tmp1[:, :].rearrange("p (h fr) -> p h fr", h=16),
                tmp2[:, :].rearrange("p (h fr) -> p h fr", h=16),
                ALU.add), after=(o4, o5), name=name + "f")
            lt["rope_tmp"] = o6
            return o6
        rq = rope16(q_raw, q_rope, evqb, f"brq{t}")
        lt["qraw_rd"] = rq
        rk = rope16(k_sep, k_rope, evk2, f"brk{t}")
        lt["ksep_rd"] = rk
        lt[("rope", b)] = rk

        hb = t % 2
        d3q = p.dma("q", htmp[0, hb], q_rope[:, :],
                    after=(rq, lt.get(("h4d", 0, hb))), name=f"bh3q{t}",
                    group=f"bhq{hb}")
        d3k = p.dma("s", htmp[1, hb], k_rope[:, :],
                    after=(rk, lt.get(("h4d", 1, hb))), name=f"bh3k{t}",
                    group=f"bhk{hb}")
        d3v = p.dma("g", htmp[2, hb], v_nat[:, :],
                    after=(evv1, evv2, lt.get(("h4d", 2, hb))),
                    name=f"bh3v{t}", group=f"bhv{hb}")
        lt[("rope_dst", "brq")] = d3q
        lt[("rope_dst", "brk")] = d3k
        lt["vnat_rd"] = d3v
        d4 = []
        for ti, (buf, grp, d3) in enumerate(((Q2, "q", d3q), (K2, "k", d3k),
                                             (V2, "v", d3v))):
            iview = htmp[ti, hb].rearrange("(sub tl) (h dd) -> h tl sub dd",
                                           sub=16, h=16)
            o = None
            tq = {"q": "q", "k": "s", "v": "g"}[grp]
            for h in range(16):
                o = p.dma(tq, buf[hb][h * 8:(h + 1) * 8, :].rearrange(
                    "p (sub dd) -> p sub dd", sub=16),
                    iview[h],
                    after=(d3, lt.get(("i2_rd", ti, hb))) if h == 0 else (d3,),
                    name=f"bh4{grp}{t}_{h}", group=f"bi{grp}{hb}")
            d4.append(o)
            lt[("h4d", ti, hb)] = o
        return {"d4": d4, "hb": hb, "b": b}

    def phase2(t, cc):
        d4, hb, b = cc["d4"], cc["hb"], cc["b"]
        tev = {}
        for ti, (buf, tbuf) in enumerate(((Q2, Q2T), (K2, K2T))):
            for batch in range(2):
                trq = []
                for ss in range(8):
                    s = batch * 8 + ss
                    slot = ss // 2
                    base = (ss % 2) * 64
                    o = p.op("t", lambda t_, s=s, slot=slot, base=base,
                             buf=buf, hb=hb: t_.transpose(
                                 pbf[1][base:base + 64,
                                        slot * 128:(slot + 1) * 128],
                                 buf[hb][:, s * 64:(s + 1) * 64],
                                 G["ident"],
                                 tile_position=(0, base) if base else None),
                             after=(d4[ti], lt.get("pbf1_rd")) if ss == 0
                             else (d4[ti],),
                             name=f"bst{t}_{ti}_{s}")
                    trq.append(o)
                e_ev = p.op("v", lambda v, tbuf=tbuf, batch=batch: v.tensor_copy(
                    tbuf[0:64, :].rearrange("p (sh k) -> p sh k", sh=8)[
                        :, batch * 4:batch * 4 + 4],
                    pbf[1][0:64, 0:512].rearrange("p (ss k) -> p ss k", ss=4)),
                    after=(trq[-1], lt.get("q2t_rd")),
                    name=f"bse{t}_{ti}_{batch}")
                o_ev = p.op("v", lambda v, tbuf=tbuf, batch=batch: v.tensor_copy(
                    tbuf[64:128, :].rearrange("p (sh k) -> p sh k", sh=8)[
                        :, batch * 4:batch * 4 + 4],
                    pbf[1][64:128, 0:512].rearrange("p (ss k) -> p ss k", ss=4)),
                    after=(trq[-1],), name=f"bso{t}_{ti}_{batch}")
                lt["pbf1_rd"] = o_ev
                tev[(ti, batch)] = o_ev

        avd = {}
        for sh in range(8):
            pA = PS[(sh % 2) * 2]
            pB = PS[(sh % 2) * 2 + 1]
            mmE = p.op("t", lambda t_, sh=sh, pA=pA: t_.matmul(
                pA[:, 0:128], Q2T[0:64, sh * 128:(sh + 1) * 128],
                K2T[0:64, sh * 128:(sh + 1) * 128], start=True, stop=True,
                tile_position=(0, 0)),
                after=(tev[(0, sh // 4)], tev[(1, sh // 4)],
                       lt.get(("sps", (sh % 2) * 2)), lt.get("kvps0"),
                       lt.get("kvps1"), lt.get("latps"), lt.get("qps0"),
                       lt.get("qps1")),
                name=f"bsc{t}_{sh}e")
            mmO = p.op("t", lambda t_, sh=sh, pB=pB: t_.matmul(
                pB[:, 0:128], Q2T[64:128, sh * 128:(sh + 1) * 128],
                K2T[64:128, sh * 128:(sh + 1) * 128], start=True, stop=True,
                tile_position=(64, 0)),
                after=(lt.get(("sps", (sh % 2) * 2 + 1)),
                       lt.get("qps1") if sh % 2 == 0 else lt.get("kvps1")),
                name=f"bsc{t}_{sh}o")
            for par, (mm, bank) in enumerate(((mmE, pA), (mmO, pB))):
                s = sh * 2 + par
                mk = p.op("v", lambda v, bank=bank: v.tensor_tensor(
                    smsk[:, :], bank[:, 0:128], mask_sb[:, :], ALU.add),
                    after=(mm, mld, lt.get("smsk_rd")), name=f"bma{t}_{s}")
                lt[("sps", (sh % 2) * 2 + par)] = mk
                ex = p.op("s", lambda s_: s_.activation(
                    att_b[:, :], smsk[:, :], AF.Exp, bias=0.0, scale=0.125,
                    accum_out=rsum[:, 0:1]),
                    after=(mk, lt.get("att_rd"), lt.get("rs_rd")),
                    name=f"bex{t}_{s}")
                lt["smsk_rd"] = ex
                rc = p.op("v", lambda v: v.reciprocal(rsum[:, 1:2],
                                                      rsum[:, 0:1]),
                          after=(ex, lt.get("an_prev")), name=f"brc{t}_{s}")
                lt["rs_rd"] = rc
                an = p.op("v", lambda v, s=s: v.tensor_scalar(
                    attn_b[s % 2][:, :], att_b[:, :], rsum[:, 1:2], None,
                    ALU.mult), after=(rc, lt.get(("attn_rd", s % 2))),
                    name=f"ban{t}_{s}")
                lt["att_rd"] = an
                lt["an_prev"] = an
                atr = p.op("t", lambda t_, s=s: t_.transpose(
                    pbf[0][:, (s % 2) * 128:(s % 2) * 128 + 128],
                    attn_b[s % 2][:, :], G["ident"]),
                    after=(an, lt.get(("atps", s % 2)), lt.get("xnT_ev")),
                    name=f"batr{t}_{s}")
                aev = p.op("v", lambda v, s=s: v.tensor_copy(
                    attT_b[s % 2][:, :],
                    pbf[0][:, (s % 2) * 128:(s % 2) * 128 + 128]),
                    after=(atr, lt.get(("attT_rd", s % 2))),
                    name=f"bae{t}_{s}")
                lt[("atps", s % 2)] = aev
                lt[("attn_rd", s % 2)] = aev
                av = p.op("t", lambda t_, s=s, hb=hb: t_.matmul(
                    PS[4 + s % 2][0:64, 0:128], V2[hb][:, s * 64:(s + 1) * 64],
                    attT_b[s % 2][:, :], start=True, stop=True),
                    after=(aev, lt.get(("avps", s % 2)), lt.get("kvps2"),
                           lt.get("kvps3"), lt.get(("h4d", 2, hb))),
                    name=f"bav{t}_{s}")
                lt[("attT_rd", s % 2)] = av
                oev = p.op("s", lambda s_, s=s: s_.copy(
                    o2Ty[0:64, :].rearrange("p (h sub tl) -> p h sub tl",
                                            h=16, sub=16)[:, :, s],
                    PS[4 + s % 2][0:64, 0:128].rearrange("p (h tl) -> p h tl",
                                                         h=16)),
                    after=(av, lt.get(("o2T_rd",)) if s == 0 else None),
                    name=f"boe{t}_{s}")
                lt[("avps", s % 2)] = oev
                avd[s] = oev
        av_all = [avd[15], avd[14]]

        ldr = p.dma("q", xrb[b], src[t * 128:(t + 1) * 128, :],
                    after=(lt.get(("xr", b)),), name=f"brl{t}", group=f"bxr{b}")
        res_ops = []
        for col in range(2):
            bank = PS[2 + col]

            def mk_op(bank=bank, col=col):
                def emit(t_):
                    r = None
                    for h in range(16):
                        r = t_.matmul(
                            bank[:, :], o2Ty[0:64, h * 128:(h + 1) * 128],
                            wout[0:64, h * D + col * 512:h * D + col * 512 + 512],
                            start=(h == 0), stop=False)
                    r = t_.matmul(bank[:, :], ones_b[0:1, 0:128],
                                  bout_r[0:1, col * 512:(col + 1) * 512],
                                  start=False, stop=True)
                    return r
                return emit
            dep = [avd[15], avd[14], w_ready, lt.get(("ops", col)),
                   lt.get(("sps", 2 + col)),
                   lt.get("latps") if col == 0 else lt.get("kvps1"),
                   lt.get("kvps0") if col == 0 else None]
            mo = p.op("t", mk_op(), after=tuple(dep), name=f"bop{t}_{col}")
            ro = p.op("v", lambda v, col=col, b=b, bank=bank: 
                      v.scalar_tensor_tensor(
                          xo_buf[b][:, col * 512:(col + 1) * 512], bank[:, :],
                          1.0, xrb[b][:, col * 512:(col + 1) * 512],
                          ALU.mult, ALU.add),
                      after=(mo, ldr, lt.get(("xo_st", b))),
                      name=f"bro{t}_{col}")
            lt[("ops", col)] = ro
            lt[("sps", 2 + col)] = ro
            res_ops.append(ro)
        lt[("o2T_rd",)] = res_ops[-1]
        lt[("xr", b)] = res_ops[-1]
        lt[("i2_rd", 0, hb)] = tev[(0, 1)]
        lt[("i2_rd", 1, hb)] = tev[(1, 1)]
        lt[("i2_rd", 2, hb)] = avd[15]
        lt["q2t_rd"] = avd[15]
        if G.get("dbg_l1b") and t == 0:
            p.dma("q", G["dbg_gnw"].bitcast(BF16)[:, 0:1024], q_rope[:, :], after=(rq,), name="dbq",
                  group="dbg")
            p.dma("q", G["dbg_gnb"].bitcast(BF16)[:, 0:1024], k_rope[:, :], after=(rk,), name="dbk",
                  group="dbg")
            p.dma("q", G["dbg_gate"][:, 0:1024], Q2[hb][:, :],
                  after=(lt[("h4d", 0, hb)],), name="dbQ2", group="dbg")
            p.dma("q", G["dbg_gate"][:, 1024:2048], V2[hb][:, :],
                  after=(lt[("h4d", 2, hb)],), name="dbV2", group="dbg")
            p.dma("q", G["dbg_yT"][0:64, 0:2048], o2Ty[0:64, :], after=(avd[15],),
                  name="dbo2", group="dbg")
            p.barrier("dbgl1b")
        st = p.dma("q", dst[t * 128:(t + 1) * 128, :], xo_buf[b],
                   after=tuple(res_ops), name=f"bst{t}", group=f"bso{b}")
        lt[("xo_st", b)] = st
        stores.append(st)

    prev_cc = None
    for t in range(NT):
        cc = phase1(t)
        if prev_cc is not None:
            phase2(t - 1, prev_cc)
        prev_cc = cc
    phase2(NT - 1, prev_cc)
    ar.pop()
    return stores


def stage_prep(p, ar, G, names):
    NRAW, NBFB, NEVB = 4, 8, 4
    PSTRIP = 512
    LDQ = ["q", "s", "g", "q"]
    STQ = ["g", "g", "s", "s"]
    raw = []
    for i in range(NRAW):
        ar.off = (ar.off + 31) & ~31
        st_ = ar.off
        ar.off += PSTRIP * 4
        ar.peak = max(ar.peak, ar.off)
        assert ar.off <= SBUF_BYTES, f"SBUF overflow at praw{i}"
        sl = ar.ap[:, st_:st_ + PSTRIP * 4]
        raw.append((sl.bitcast(F32), sl.bitcast(I32)))
    bfb = [ar.alloc(PSTRIP, BF16, f"pbf{i}") for i in range(NBFB)]
    evb = [ar.alloc(PSTRIP, BF16, f"pev{i}") for i in range(NEVB)]
    pbk = [G["psum_bf"][0], G["psum_bf"][1]]
    state = {"ext": {0: None, 1: None}, "lcur": 0, "tcur": 0}
    prev = {}

    strips = []

    def add_mat(src_ap, dst_ap, rows, cols, is_int):
        for r0 in range(0, rows, 128):
            rh = min(128, rows - r0)
            for c0 in range(0, cols, PSTRIP):
                cw = min(PSTRIP, cols - c0)
                strips.append((src_ap, dst_ap, r0, c0, rh, cw, is_int))

    order = [n for n in names if n.startswith("ffn0")] + \
            [n for n in names if not n.startswith("ffn0")]
    for nm in order:
        if nm == "moe_w":
            for e in range(8):
                add_mat(G["L1_ffn_w1"][e], G["moe_w1T"][e], 2048, 1024, False)
                add_mat(G["L1_ffn_w2"][e], G["moe_w2T"][e], 1024, 2048, False)
        else:
            srcn, (rows, cols), is_int = PREP_TRANSPOSE[nm]
            add_mat(G[srcn], G[nm], rows, cols, is_int)
    state["n"] = len(strips)

    def emit_load(k):
        src_ap, dst_ap, r0, c0, rh, cw, is_int = strips[k]
        i = k % NRAW
        bslot = k % NBFB
        rb = raw[i][1] if is_int else raw[i][0]
        ld = p.dma(LDQ[i], rb[0:rh, 0:cw],
                   src_ap[r0:r0 + rh, c0:c0 + cw],
                   after=(prev.get(("cast", i)),), name="pld",
                   group=f"prepld{i}")
        ceng = "v" if (is_int or k % 2 == 0) else "s"
        if ceng == "v":
            cv = p.op("v", lambda v, cw=cw, bslot=bslot, rh=rh, rb=rb:
                      v.tensor_copy(bfb[bslot][0:rh, 0:cw],
                                    rb[0:rh, 0:cw]),
                      after=(ld, prev.get(("tr", bslot))), name="pcv")
        else:
            cv = p.op("s", lambda s, cw=cw, bslot=bslot, rh=rh, rb=rb:
                      s.copy(bfb[bslot][0:rh, 0:cw], rb[0:rh, 0:cw]),
                      after=(ld, prev.get(("tr", bslot))), name="pcv")
        prev[("cast", i)] = cv
        prev[("castb", bslot)] = cv

    def emit_tr(k):
        src_ap, dst_ap, r0, c0, rh, cw, is_int = strips[k]
        bslot = k % NBFB
        e = k % NEVB
        bk = k % 2
        pb = pbk[bk]
        nblk = cw // 128
        cv = prev[("castb", bslot)]
        trs = []
        for bi in range(nblk):
            o = p.op("t", lambda t_, bi=bi, bslot=bslot, rh=rh, pb=pb:
                     t_.transpose(
                pb[:, bi * rh:(bi + 1) * rh],
                bfb[bslot][0:rh, bi * 128:(bi + 1) * 128],
                G["ident"][0:rh, 0:rh]),
                after=(cv, prev.get(("evb", bk)), state["ext"][bk],
                       G["setup"]) if bi == 0
                else (cv,), name="ptr")
            trs.append(o)
        prev[("tr", bslot)] = trs[-1]
        if k % 2 == 0:
            ev = p.op("v", lambda v, e=e, rh=rh, nblk=nblk, pb=pb:
                      v.tensor_copy(evb[e][:, 0:nblk * rh],
                                    pb[:, 0:nblk * rh]),
                      after=(trs[-1], prev.get(("st", e))), name="pev")
        else:
            ev = p.op("s", lambda s, e=e, rh=rh, nblk=nblk, pb=pb:
                      s.copy(evb[e][:, 0:nblk * rh],
                             pb[:, 0:nblk * rh]),
                      after=(trs[-1], prev.get(("st", e))), name="pev")
        prev[("evb", bk)] = ev
        state[("evbank", bk)] = ev
        state["last_ev"] = ev
        st = None
        for bi in range(nblk):
            st = p.dma(STQ[e], dst_ap[c0 + bi * 128:c0 + (bi + 1) * 128,
                                      r0:r0 + rh],
                       evb[e][:, bi * rh:(bi + 1) * rh], after=(ev,),
                       name="pst", group=f"prepst{e}")
        prev[("st", e)] = st
        state["last"] = st

    state["emit_load"] = emit_load
    state["emit_tr"] = emit_tr
    return state


def build_program(stages):
    nc = bass.Bass("TRN2", target_bir_lowering=False)
    es = ExitStack()
    G = {}
    G["x0"] = nc.dram_tensor("x0", [TOK, D], F32, kind="ExternalInput").ap()
    G["y"] = nc.dram_tensor("y", [TOK, D], F32, kind="ExternalOutput").ap()
    G["xS1"] = nc.dram_tensor("xS1", [TOK, D], F32, kind="Internal").ap()
    G["xS2"] = nc.dram_tensor("xS2", [TOK, D], F32, kind="Internal").ap()
    G["l1b_tmp"] = nc.dram_tensor("l1b_tmp", [3, 2, 128, 1024], BF16,
                                  kind="Internal").ap()
    G["dbg_stats"] = nc.dram_tensor("dbg_stats", [128, 64], F32,
                                    kind="ExternalOutput").ap()
    G["dbg_yT"] = nc.dram_tensor("dbg_yT", [128, 8 * 512], BF16,
                                 kind="ExternalOutput").ap()
    G["dbg_qT"] = nc.dram_tensor("dbg_qT", [128, 4 * 512], BF16,
                                 kind="ExternalOutput").ap()
    G["dbg_kT"] = nc.dram_tensor("dbg_kT", [128, 4, 512], BF16,
                                 kind="ExternalOutput").ap()
    G["dbg_v"] = nc.dram_tensor("dbg_v", [128, 4 * 8 * 65], BF16,
                                 kind="ExternalOutput").ap()
    G["dbg_qraw"] = nc.dram_tensor("dbg_qraw", [128, 512], BF16,
                                   kind="ExternalOutput").ap()
    G["dbg_qrope"] = nc.dram_tensor("dbg_qrope", [128, 512], BF16,
                                    kind="ExternalOutput").ap()
    G["dbg_gate"] = nc.dram_tensor("dbg_gate", [128, 2048], BF16,
                                   kind="ExternalOutput").ap()
    G["dbg_gnw"] = nc.dram_tensor("dbg_gnw", [128, 1024], F32,
                                  kind="ExternalOutput").ap()
    G["dbg_gnb"] = nc.dram_tensor("dbg_gnb", [128, 1024], F32,
                                  kind="ExternalOutput").ap()
    in_names = set()

    def ext_in(nm):
        if nm not in G:
            dt = I32 if INPUT_DTYPES[nm] == "int32" else F32
            G[nm] = nc.dram_tensor(nm, list(INPUT_SHAPES[nm]), dt,
                                   kind="ExternalInput").ap()
            in_names.add(nm)

    need_preps = []
    tbl_names = {"ident"}
    for st in stages:
        for nm in RAW_INPUTS[st]:
            ext_in(nm)
        tbl_names.update(STAGE_TABLES[st])
        for nm in STAGE_PREPS[st]:
            if nm not in need_preps:
                need_preps.append(nm)
    for nm in tbl_names:
        arr = TABLES[nm]
        dt = BF16 if arr.dtype == np.uint16 else F32
        G["tbl_" + nm] = nc.dram_tensor("tbl_" + nm, list(arr.shape), dt,
                                        kind="ExternalInput").ap()
        in_names.add("tbl_" + nm)
    for nm in need_preps:
        if nm == "moe_w":
            G["moe_w1T"] = nc.dram_tensor("moe_w1T", [8, D, 2048], BF16,
                                          kind="Internal").ap()
            G["moe_w2T"] = nc.dram_tensor("moe_w2T", [8, 2048, D], BF16,
                                          kind="Internal").ap()
            ext_in("L1_ffn_w1")
            ext_in("L1_ffn_w2")
        else:
            srcn, (rows, cols), is_int = PREP_TRANSPOSE[nm]
            G[nm] = nc.dram_tensor("prep_" + nm, [cols, rows], BF16,
                                   kind="Internal").ap()
            ext_in(srcn)

    arena_t = nc.alloc_sbuf_tensor("arena", [128, SBUF_BYTES], mybir.dt.uint8)
    ar = Arena(arena_t.ap())
    G["psum_d"] = [es.enter_context(nc.psum_tensor(f"pd{i}", [128, 1024], F32))
                   for i in range(3)]
    G["psum_f32"] = [G["psum_d"][i // 2][:, (i % 2) * 512:(i % 2) * 512 + 512]
                     for i in range(6)]
    G["psum_bf"] = [es.enter_context(nc.psum_tensor(f"pbx{i}", [128, 1024], BF16))
                    for i in range(2)]

    p = P(nc)
    ident_sb = ar.alloc(128, BF16, "ident")
    barrier = p.dma("q", ident_sb, G["tbl_ident"], name="ldid", group="const")
    G["ident"] = ident_sb
    G["eps"] = ar.alloc(1, F32, "eps")
    G["consts_ready"] = p.op("g", lambda g: g.memset(G["eps"][:, :], EPS),
                             name="memeps")

    G["setup"] = barrier
    prep_mark = ar.off
    prep_state = [None, None]
    if need_preps:
        ps = stage_prep(p, ar, G, need_preps)
        prep_state = [ps, ps]
        W = 8
        for k in range(min(W, ps["n"])):
            ps["emit_load"](k)
        ps["lcur"] = min(W, ps["n"])

        def tick(ext0=None, ext1=None, n=4):
            if ps["tcur"] >= ps["n"]:
                return None
            ps["ext"] = {0: ext0, 1: ext1}
            for _ in range(n):
                if ps["tcur"] >= ps["n"]:
                    break
                ps["emit_tr"](ps["tcur"])
                ps["tcur"] += 1
                if ps["lcur"] < ps["n"]:
                    ps["emit_load"](ps["lcur"])
                    ps["lcur"] += 1
            return (ps.get(("evbank", 0)), ps.get(("evbank", 1)))
        G["prep_tick"] = tick
    p.new_stage()

    cur = G["x0"]
    nxt = ["xS1", "xS2"]
    stores = [barrier]
    for i, st in enumerate(stages):
        out = G["y"] if i == len(stages) - 1 else G[nxt[i % 2]]
        if st in ("ffn0a", "ffn0b"):
            stores = stage_ffn0(p, ar, G, cur, out, barrier)
        elif st == "l0a":
            stores = stage_attn(p, ar, G, {"wsrc": "qkv", "post": "softmax"},
                                cur, out, barrier)
        elif st == "l0b":
            stores = stage_attn(p, ar, G, {"wsrc": "prep", "wq": "l0b_qT",
                                           "wk": "l0b_kT", "wv": "l0b_vT",
                                           "wo": "l0b_oT", "post": "softmax"},
                                cur, out, barrier)
        elif st == "l1a":
            stores = stage_attn(p, ar, G, {"wsrc": "prep", "wq": "l1a_qT",
                                           "wk": "l1a_kT", "wv": "l1a_vT",
                                           "wg": "l1a_gT", "wo": "l1a_oT",
                                           "post": "retention", "gate": True,
                                           "wo_pairs": True},
                                cur, out, barrier)
        elif st in ("moea", "moeb"):
            stores = stage_moe(p, ar, G, cur, out, barrier)
        elif st == "l1b":
            stores = stage_l1b(p, ar, G, cur, out, barrier)
        elif st == "dbg_ln":
            stores = stage_dbg_ln(p, ar, G, cur, out, barrier)
        else:
            raise NotImplementedError(st)
        if i == 0 and prep_state[0] is not None:
            ps = prep_state[0]
            if ps["tcur"] < ps["n"]:
                deps = list(p.last.values())
                j = p.op("q", lambda h: h.nop(), after=tuple(deps),
                         name="prejoin")
                ps["ext"] = {0: j, 1: j}
                while ps["tcur"] < ps["n"]:
                    ps["emit_tr"](ps["tcur"])
                    ps["tcur"] += 1
                    if ps["lcur"] < ps["n"]:
                        ps["emit_load"](ps["lcur"])
                        ps["lcur"] += 1
            G["prep_tick"] = None
        barrier = p.barrier(f"{st}_done")
        p.new_stage()
        ar.off = prep_mark
        cur = out
    p.op("q", lambda q: q.nop(), after=tuple(stores), name="final")
    p.emit_all(es)
    es.close()
    return nc, in_names


STAGE_GROUPS = [list(STAGES)]


def kernel(**inputs):
    xfull = inputs["L0_attn_a_x"]
    shards = [np.ascontiguousarray(np.asarray(xfull)[b]) for b in range(NCORE)]
    host = {k: np.ascontiguousarray(np.asarray(v)) for k, v in inputs.items()
            if k != "L0_attn_a_x"}
    for k, v in TABLES.items():
        host["tbl_" + k] = v
    cur = shards
    for grp in STAGE_GROUPS:
        nc, in_names = build_program(grp)
        in_maps = []
        for b in range(NCORE):
            m = {"x0": cur[b]}
            for nm in in_names:
                m[nm] = host[nm]
            in_maps.append(m)
        res = run_bass_kernel_spmd(nc, in_maps, core_ids=list(range(NCORE)))
        cur = [np.asarray(res.results[b]["y"], dtype=np.float32)
               for b in range(NCORE)]
    return np.stack(cur, axis=0)


# version 32 (final)
# speedup vs baseline: 1.0399x; 1.0399x over previous
import os
import numpy as np
import ml_dtypes
from contextlib import ExitStack

import concourse.bass as bass
import concourse.mybir as mybir
from concourse.bass_utils import run_bass_kernel_spmd

F32 = mybir.dt.float32
BF16 = mybir.dt.bfloat16
I32 = mybir.dt.int32
AF = mybir.ActivationFunctionType
ALU = mybir.AluOpType
BF = ml_dtypes.bfloat16

D = 1024
TOK = 4096
NT = TOK // 128
NCORE = 8
EPS = 1e-5

ENG_ATTR = {"t": "tensor", "v": "vector", "s": "scalar", "g": "gpsimd", "q": "sync"}


class Op:
    __slots__ = ("eng", "emit", "after", "inc", "needed", "target", "name", "chan")

    def __init__(self, eng, emit, after, inc, name, chan):
        self.eng = eng
        self.emit = emit
        self.after = after
        self.inc = inc
        self.needed = False
        self.target = 0
        self.name = name
        self.chan = chan


class P:

    def __init__(self, nc):
        self.nc = nc
        self.ops = []
        self.last = {}
        self.entry = None
        self.sid = 0

    def _add(self, o):
        if self.entry is not None and o is not self.entry:
            o.after.append(self.entry)
        self.ops.append(o)
        self.last[o.chan] = o
        return o

    def op(self, eng, emit, after=(), name=""):
        return self._add(Op(eng, emit, [a for a in after if a is not None], 1,
                            name, ("c", eng, self.sid)))

    def dma(self, eng, out, in_, after=(), name="", group=None):
        assert group is not None, f"dma {name} needs a semaphore group"
        return self._add(Op(eng,
                            lambda h, out=out, in_=in_: h.dma_start(out=out,
                                                                    in_=in_),
                            [a for a in after if a is not None], 16, name,
                            ("d", group)))

    def barrier(self, name="bar"):
        deps = list(self.last.values())
        b = Op("q", lambda h: h.nop(), deps, 1, name, ("c", "q", self.sid))
        self._add(b)
        self.entry = b
        return b

    def new_stage(self):
        self.sid += 1

    def emit_all(self, es):
        import numpy as np
        for o in self.ops:
            if o.inc == 16:
                o.needed = True
            for a in o.after:
                a.needed = True
        chans = {}
        chan_id = {}
        for o in self.ops:
            if o.needed:
                if o.chan not in chan_id:
                    chan_id[o.chan] = len(chan_id)
                chans[o.chan] = chans.get(o.chan, 0) + o.inc
                o.target = chans[o.chan]
        NCH = len(chan_id)
        clocks = {}
        zero = np.zeros(NCH, dtype=np.int64)
        for idx, o in enumerate(self.ops):
            cl = None
            for a in o.after:
                ca = clocks.get(id(a))
                if ca is None:
                    continue
                cl = ca.copy() if cl is None else np.maximum(cl, ca)
            if o.needed:
                if cl is None:
                    cl = zero.copy()
                ci = chan_id[o.chan]
                if o.target > cl[ci]:
                    cl[ci] = o.target
                clocks[id(o)] = cl
            elif cl is not None:
                clocks[id(o)] = cl
        sems = {c: es.enter_context(self.nc.semaphore("sem_" + "_".join(map(str, c))))
                for c in chans}
        nwait = [0, 0]
        with self.nc.Block() as blk:
            for eng, attr in ENG_ATTR.items():
                def section(h, eng=eng):
                    guar = zero.copy()
                    for o in self.ops:
                        if o.eng != eng:
                            continue
                        for a in o.after:
                            ci = chan_id[a.chan]
                            nwait[1] += 1
                            if a.target > guar[ci]:
                                h.wait_ge(sems[a.chan], a.target)
                                nwait[0] += 1
                                ca = clocks.get(id(a))
                                if ca is not None:
                                    np.maximum(guar, ca, out=guar)
                                else:
                                    guar[ci] = max(guar[ci], a.target)
                        inst = o.emit(h)
                        if o.needed:
                            inst.then_inc(sems[o.chan], o.inc)
                getattr(blk, attr)(section)
        if os.environ.get("BASS_DEBUG_WAITS"):
            print(f"[emit] waits emitted {nwait[0]} / {nwait[1]} deps; "
                  f"{len(chans)} chans; {len(self.ops)} ops")


SBUF_BYTES = 212832


class Arena:
    def __init__(self, ap):
        self.ap = ap
        self.off = 0
        self.marks = []
        self.peak = 0

    def alloc(self, cols, dtype, name=""):
        size = {F32: 4, BF16: 2, I32: 4}[dtype]
        self.off = (self.off + 31) & ~31
        start = self.off
        self.off += cols * size
        self.peak = max(self.peak, self.off)
        assert self.off <= SBUF_BYTES, f"SBUF overflow at {name}: {self.off}"
        return self.ap[:, start:start + cols * size].bitcast(dtype)

    def push(self):
        self.marks.append(self.off)

    def pop(self):
        self.off = self.marks.pop()


def make_tables():
    t = {}
    t["ident"] = np.eye(128, dtype=BF).view(np.uint16)
    half = 32
    theta = 10000.0 ** (-np.arange(half, dtype=np.float64) / half)
    ang = np.arange(TOK, dtype=np.float64)[:, None] * theta[None, :]
    t["crep"] = np.tile(np.cos(ang), (1, 16)).astype(BF).view(np.uint16)
    t["srep"] = np.tile(np.sin(ang), (1, 16)).astype(BF).view(np.uint16)
    H = 16
    gam = 1.0 - 2.0 ** (-5.0 - np.arange(H, dtype=np.float64))
    j = np.arange(128, dtype=np.float64)
    hm = np.zeros((128, H * 36), dtype=np.float64)
    for h in range(H):
        for mi in range(36):
            m = mi - 3
            hm[:, h * 36 + mi] = np.exp((128.0 * m - j) * np.log(gam[h]))
    t["decayB"] = hm.astype(np.float32)
    ga = np.zeros((128, H * 4), dtype=np.float64)
    i = np.arange(128, dtype=np.float64)
    for h in range(H):
        for sub in range(4):
            ga[:, h * 4 + sub] = 0.125 * gam[h] ** (sub * 128 + i)
    t["gammaA"] = ga.astype(np.float32)
    m4 = np.zeros((128, 4 * 1024), dtype=np.float64)
    jj = np.arange(128)[:, None]
    qq = np.arange(512)[None, :]
    for oi, o in enumerate((0, 128, 256, 384)):
        blk = ((o + jj) <= qq)
        m4[:, oi * 1024:oi * 1024 + 512] = blk
        m4[:, oi * 1024 + 512:(oi + 1) * 1024] = blk
    t["mask4"] = m4.astype(BF).view(np.uint16)
    pm = np.full((128, 128), -1e32, dtype=np.float32)
    pp_, ff_ = np.arange(128)[:, None], np.arange(128)[None, :]
    pm[(pp_ % 8) == (ff_ % 8)] = 0.0
    t["l1bmask"] = pm
    return t


TABLES = make_tables()

PREP_TRANSPOSE = {
    "l0b_qT": ("L0_attn_b_Wq", (D, D), False),
    "l0b_kT": ("L0_attn_b_Wk", (D, D), False),
    "l0b_vT": ("L0_attn_b_Wv", (D, D), False),
    "l0b_oT": ("L0_attn_b_Wo", (D, D), False),
    "l1a_qT": ("L1_attn_a_Wq", (D, D), False),
    "l1a_kT": ("L1_attn_a_Wk", (D, D), False),
    "l1a_vT": ("L1_attn_a_Wv", (D, D), False),
    "l1a_gT": ("L1_attn_a_Wg", (D, D), False),
    "l1a_oT": ("L1_attn_a_Wo", (D, D), False),
    "ffn0_gT": ("L0_ffn_w_gate", (3072, D), True),
    "ffn0_uT": ("L0_ffn_w_up", (3072, D), True),
    "ffn0_dT": ("L0_ffn_w_down", (D, 3072), True),
    "gate_wT": ("L1_ffn_gate_w", (8, D), False),
}

STAGES = ["l0a", "ffn0a", "l0b", "ffn0b", "l1a", "moea", "l1b", "moeb"]

STAGE_PREPS = {
    "dbg_ln": [],
    "l0a": [],
    "ffn0a": ["ffn0_gT", "ffn0_uT", "ffn0_dT"],
    "l0b": ["l0b_qT", "l0b_kT", "l0b_vT", "l0b_oT"],
    "ffn0b": ["ffn0_gT", "ffn0_uT", "ffn0_dT"],
    "l1a": ["l1a_qT", "l1a_kT", "l1a_vT", "l1a_gT", "l1a_oT"],
    "moea": ["gate_wT", "moe_w"],
    "l1b": [],
    "moeb": ["gate_wT", "moe_w"],
}

RAW_INPUTS = {
    "dbg_ln": [],
    "l0a": ["L0_attn_a_W_qkv", "L0_attn_a_W_o"],
    "ffn0a": [],
    "l0b": [],
    "ffn0b": [],
    "l1a": ["L1_attn_a_gn_w", "L1_attn_a_gn_b"],
    "moea": ["L1_ffn_ln_gamma", "L1_ffn_ln_beta"],
    "l1b": ["L1_attn_b_Wq", "L1_attn_b_bq", "L1_attn_b_Wdown", "L1_attn_b_bdown",
            "L1_attn_b_Wup", "L1_attn_b_bup", "L1_attn_b_Wout", "L1_attn_b_bout"],
    "moeb": ["L1_ffn_ln_gamma", "L1_ffn_ln_beta"],
}

STAGE_TABLES = {
    "dbg_ln": [],
    "l0a": ["crep", "srep", "mask4"],
    "ffn0a": [],
    "l0b": ["crep", "srep", "mask4"],
    "ffn0b": [],
    "l1a": ["crep", "srep", "decayB", "gammaA", "mask4"],
    "moea": [],
    "l1b": ["crep", "srep", "l1bmask"],
    "moeb": [],
}

INPUT_SHAPES = {
    "L0_attn_a_x": (8, 4096, 1024),
    "L0_attn_a_W_qkv": (1024, 3072), "L0_attn_a_W_o": (1024, 1024),
    "L0_attn_b_Wq": (1024, 1024), "L0_attn_b_Wk": (1024, 1024),
    "L0_attn_b_Wv": (1024, 1024), "L0_attn_b_Wo": (1024, 1024),
    "L0_ffn_w_gate": (3072, 1024), "L0_ffn_w_up": (3072, 1024),
    "L0_ffn_w_down": (1024, 3072),
    "L1_attn_a_Wq": (1024, 1024), "L1_attn_a_Wk": (1024, 1024),
    "L1_attn_a_Wv": (1024, 1024), "L1_attn_a_Wg": (1024, 1024),
    "L1_attn_a_Wo": (1024, 1024), "L1_attn_a_gn_w": (1024,),
    "L1_attn_a_gn_b": (1024,),
    "L1_attn_b_Wq": (1024, 1024), "L1_attn_b_bq": (1024,),
    "L1_attn_b_Wdown": (1024, 256), "L1_attn_b_bdown": (256,),
    "L1_attn_b_Wup": (256, 2048), "L1_attn_b_bup": (2048,),
    "L1_attn_b_Wout": (1024, 1024), "L1_attn_b_bout": (1024,),
    "L1_ffn_ln_gamma": (1024,), "L1_ffn_ln_beta": (1024,),
    "L1_ffn_gate_w": (8, 1024), "L1_ffn_w1": (8, 2048, 1024),
    "L1_ffn_w2": (8, 1024, 2048),
}
INPUT_DTYPES = {k: ("int32" if k.startswith("L0_ffn_w") else "float32")
                for k in INPUT_SHAPES}


def ln_stats_pass(p, ar, G, src, n_tiles=None):
    n_tiles = NT if n_tiles is None else n_tiles
    stats_m = ar.alloc(n_tiles, F32, "stats_m")
    stats_v = ar.alloc(n_tiles, F32, "stats_v")
    rstd = ar.alloc(n_tiles, F32, "rstd")
    negmr = ar.alloc(n_tiles, F32, "negmr")
    bnbuf = ar.alloc(14, F32, "bnbuf")
    xb = [ar.alloc(D, F32, f"lnx{i}") for i in range(2)]
    prev_use = [None, None]
    last = None
    for t in range(n_tiles):
        b = t % 2
        ld = p.dma("q", xb[b], src[t * 128:(t + 1) * 128, :],
                   after=(prev_use[b],), name=f"lnld{t}", group=f"lnx{b}")
        b1 = p.op("v", lambda v, b=b: v.bn_stats(bnbuf[:, 0:6], xb[b][:, 0:512]),
                  after=(ld, last), name=f"bns{t}a")
        b2 = p.op("v", lambda v, b=b: v.bn_stats(bnbuf[:, 6:12],
                                                 xb[b][:, 512:1024]),
                  after=(ld,), name=f"bns{t}b")
        ba = p.op("v", lambda v: v.bn_aggr(bnbuf[:, 12:14], bnbuf[:, 0:12]),
                  after=(b1, b2), name=f"bna{t}")
        cp = p.op("v", lambda v, t=t: v.tensor_copy(stats_m[:, t:t + 1],
                                                    bnbuf[:, 12:13]),
                  after=(ba,), name=f"cpm{t}")
        cp2 = p.op("v", lambda v, t=t: v.tensor_copy(stats_v[:, t:t + 1],
                                                     bnbuf[:, 13:14]),
                   after=(ba,), name=f"cpv{t}")
        prev_use[b] = cp2
        last = cp2
    statbar = p.barrier("statbar")
    o1 = p.op("s", lambda s: s.activation(rstd[:, :], stats_v[:, :], AF.Ln,
                                          bias=G["eps"], scale=1.0),
              after=(last, statbar, G["consts_ready"]), name="lnv")
    o2 = p.op("s", lambda s: s.activation(rstd[:, :], rstd[:, :], AF.Exp,
                                          bias=0.0, scale=-0.5), after=(o1,),
              name="expv")
    o3 = p.op("v", lambda v: v.scalar_tensor_tensor(
        negmr[:, :], stats_m[:, :], -1.0, rstd[:, :], ALU.mult, ALU.mult),
        after=(o2,), name="negmr")
    return rstd, negmr, o3


def stage_ffn0(p, ar, G, src, dst, barrier):
    HID = 3072
    NHC = HID // 128
    TPB = 2
    BW = TPB * 128
    NB = NT // TPB
    SC = 1.0 / 127.0
    SC2 = 1.0 / (127.0 * 127.0)
    PS = G["psum_f32"]
    pb = G["psum_bf"][0]
    ar.push()
    rstd, negmr, stat_done = ln_stats_pass(p, ar, G, src)
    wg = ar.alloc(8 * HID, BF16, "wgT")
    wu = ar.alloc(8 * HID, BF16, "wuT")
    wd = [ar.alloc(12 * D, BF16, f"wdT{h}") for h in range(2)]
    xnT = ar.alloc(8 * BW, BF16, "xnT")
    mT = ar.alloc(NHC * BW, BF16, "mT")
    x_blk = [ar.alloc(D, F32, f"xb{i}") for i in range(TPB)]
    xn_buf = ar.alloc(D, BF16, "xn")
    gs_buf = ar.alloc(BW, BF16, "gs")
    uu_buf = ar.alloc(BW, BF16, "uu")
    xo_buf = [ar.alloc(D, F32, f"xo{i}") for i in range(2)]

    wld = []
    for c in range(8):
        wld.append(p.dma("q", wg[:, c * HID:(c + 1) * HID],
                         G["ffn0_gT"][c * 128:(c + 1) * 128, :],
                         after=(barrier,), name=f"wgld{c}", group="wbig"))
        wld.append(p.dma("s", wu[:, c * HID:(c + 1) * HID],
                         G["ffn0_uT"][c * 128:(c + 1) * 128, :],
                         after=(barrier,), name=f"wuld{c}", group="wbigu"))
    w_ready = wld[-1]
    wdT_v = G["ffn0_dT"].rearrange("(k pp) c -> pp k c", pp=128)

    stores = []
    last_tr = None
    last_trev = None
    last_gu_mm = None
    last_m = {}
    last_down = {}
    last_gu_ev = {}
    wd_ld_prev = [None, None]
    for blk in range(NB):
        wd_ld = []
        for hf in range(2):
            dep = [barrier, wd_ld_prev[hf]]
            if blk > 0:
                dep += [last_down.get((blk - 1, c, tt)) for c in range(2)
                        for tt in range(TPB)]
            wd_ld.append(p.dma("q",
                               wd[hf].rearrange("p (k c) -> p k c", k=12),
                               wdT_v[:, hf * 12:(hf + 1) * 12, :],
                               after=tuple(dep), name=f"wdld{blk}_{hf}",
                               group=f"wd{hf}"))
            wd_ld_prev[hf] = wd_ld[-1]
        xn_done = []
        for tt in range(TPB):
            t = blk * TPB + tt
            dep = [stat_done]
            if blk > 0:
                dep += [last_down.get((blk - 1, c, tt)) for c in range(2)]
            ld = p.dma("q", x_blk[tt], src[t * 128:(t + 1) * 128, :],
                       after=tuple(dep), name=f"xld{t}", group=f"xb{tt}")
            xo = p.op("s", lambda s, tt=tt, t=t: s.activation(
                xn_buf[:, :], x_blk[tt][:, :], AF.Identity,
                bias=negmr[:, t:t + 1], scale=rstd[:, t:t + 1]),
                after=(ld, stat_done, last_tr), name=f"xn{t}")
            trs = []
            for c in range(8):
                dep2 = (xo, last_trev, G["setup"]) if c == 0 else (xo,)
                o = p.op("t", lambda t_, c=c: t_.transpose(
                    pb[:, c * 128:(c + 1) * 128], xn_buf[:, c * 128:(c + 1) * 128],
                    G["ident"]), after=dep2, name=f"tr{t}_{c}")
                trs.append(o)
            last_tr = trs[-1]
            ev = p.op("v", lambda v, tt=tt: v.tensor_copy(
                xnT[:, :].rearrange("p (c k) -> p c k", c=8)[:, :, tt * 128:(tt + 1) * 128],
                pb[:, :].rearrange("p (c k) -> p c k", c=8)),
                after=(last_tr, last_gu_mm if tt == 0 else None),
                name=f"trev{t}")
            last_trev = ev
            xn_done.append(ev)
        for hc in range(NHC):
            bank_g = PS[hc % 2]
            bank_u = PS[2 + hc % 2]

            def mk_gu(bank, w, hc=hc):
                def emit(t_):
                    r = None
                    for c in range(8):
                        r = t_.matmul(bank[:, 0:BW],
                                      w[:, c * HID + hc * 128:c * HID + hc * 128 + 128],
                                      xnT[:, c * BW:(c + 1) * BW],
                                      start=(c == 0), stop=(c == 7))
                    return r
                return emit
            mm_g = p.op("t", mk_gu(bank_g, wg),
                        after=(xn_done[-1], w_ready, last_gu_ev.get((hc - 2, "g"))),
                        name=f"g{blk}_{hc}")
            mm_u = p.op("t", mk_gu(bank_u, wu),
                        after=(last_gu_ev.get((hc - 2, "u")),), name=f"u{blk}_{hc}")
            last_gu_mm = mm_u
            evg = p.op("s", lambda s, bank_g=bank_g: s.activation(
                gs_buf[:, :], bank_g[:, 0:BW], AF.Silu, bias=0.0, scale=SC),
                after=(mm_g, last_m.get((blk, hc - 1))), name=f"evg{blk}_{hc}")
            evu = p.op("v", lambda v, bank_u=bank_u: v.tensor_copy(
                uu_buf[:, :], bank_u[:, 0:BW]),
                after=(mm_u, last_m.get((blk, hc - 1))), name=f"evu{blk}_{hc}")
            dep_m = [evg, evu]
            if blk > 0:
                dep_m += [last_down.get((blk - 1, c, TPB - 1)) for c in range(2)]
            mm_m = p.op("v", lambda v, hc=hc: v.tensor_tensor(
                mT[:, hc * BW:(hc + 1) * BW], gs_buf[:, :], uu_buf[:, :],
                ALU.mult), after=tuple(dep_m), name=f"m{blk}_{hc}")
            last_gu_ev[(hc, "g")] = evg
            last_gu_ev[(hc, "u")] = evu
            last_m[(blk, hc)] = mm_m
        m_all = last_m[(blk, NHC - 1)]
        for tt in range(TPB):
            t = blk * TPB + tt
            res_ops = []
            for col in range(2):
                bank = PS[4 + col]

                def mk_down(bank=bank, col=col, tt=tt):
                    def emit(t_):
                        r = None
                        for hc in range(NHC):
                            hf, k = divmod(hc, 12)
                            r = t_.matmul(
                                bank[:, :],
                                mT[:, hc * BW + tt * 128:hc * BW + tt * 128 + 128],
                                wd[hf][:, k * D + col * 512:k * D + col * 512 + 512],
                                start=(hc == 0), stop=(hc == NHC - 1))
                        return r
                    return emit
                dep = (m_all, wd_ld[0], wd_ld[1], last_down.get((blk, col, tt - 1)),
                       last_down.get((blk - 1, col, TPB - 1)) if tt == 0 else None)
                o = p.op("t", mk_down(), after=dep, name=f"d{blk}_{tt}_{col}")
                xo = xo_buf[tt % 2]
                ro = p.op("v", lambda v, col=col, tt=tt, xo=xo, bank=bank:
                          v.scalar_tensor_tensor(
                              xo[:, col * 512:(col + 1) * 512], bank[:, :], SC2,
                              x_blk[tt][:, col * 512:(col + 1) * 512],
                              ALU.mult, ALU.add),
                          after=(o, stores[-2] if len(stores) >= 2 else None),
                          name=f"res{blk}_{tt}_{col}")
                res_ops.append(ro)
                last_down[(blk, col, tt)] = ro
            st = p.dma("q", dst[t * 128:(t + 1) * 128, :], xo_buf[tt % 2],
                       after=tuple(res_ops), name=f"xst{t}", group=f"st{tt % 2}")
            stores.append(st)
    ar.pop()
    return stores



def stage_dbg_ln(p, ar, G, src, dst, barrier):
    ar.push()
    rstd, negmr, stat_done = ln_stats_pass(p, ar, G, src)
    p.dma("q", G["dbg_stats"][0:128, 0:32], rstd[:, :], after=(stat_done,),
          name="dmpr", group="dbgs")
    p.dma("q", G["dbg_stats"][0:128, 32:64], negmr[:, :], after=(stat_done,),
          name="dmpn", group="dbgs")
    xb = [ar.alloc(D, F32, f"dxb{i}") for i in range(2)]
    xn = [ar.alloc(D, F32, f"dxn{i}") for i in range(2)]
    stores = []
    prev = {}
    for t in range(NT):
        b = t % 2
        ld = p.dma("q", xb[b], src[t * 128:(t + 1) * 128, :],
                   after=(stat_done, prev.get(("xn", b))), name=f"dld{t}",
                   group=f"dxb{b}")
        xo = p.op("s", lambda s, b=b, t=t: s.activation(
            xn[b][:, :], xb[b][:, :], AF.Identity,
            bias=negmr[:, t:t + 1], scale=rstd[:, t:t + 1]),
            after=(ld, stat_done, prev.get(("st", b))), name=f"dxn{t}")
        prev[("xn", b)] = xo
        st = p.dma("q", dst[t * 128:(t + 1) * 128, :], xn[b][:, :],
                   after=(xo,), name=f"dst{t}", group=f"dst{b}")
        prev[("st", b)] = st
        stores.append(st)
    ar.pop()
    return stores



def stage_attn(p, ar, G, cfg, src, dst, barrier):
    NH_G = 8
    BIS = cfg.get("bisect", 6)
    NPAIR = NH_G // 2
    STW = 512
    NST = TOK // STW
    PS = G["psum_f32"]
    pbf = G["psum_bf"]
    ar.push()
    rstd, negmr, stat_done = ln_stats_pass(p, ar, G, src)
    kT = ar.alloc(NPAIR * TOK, BF16, "kT")
    v65 = ar.alloc(NT * NH_G * 65, BF16, "v65")
    qT = ar.alloc(NPAIR * STW, BF16, "qT")
    RET = cfg["post"] == "retention"
    if not RET:
        yT = ar.alloc(NH_G * STW, BF16, "yT")
    xnT = ar.alloc(8 * 128, BF16, "xnT_a")
    wq = ar.alloc(8 * STW, BF16, "awq")
    wk = ar.alloc(8 * STW, BF16, "awk")
    wv = ar.alloc(8 * STW, BF16, "awv")
    RET0 = cfg["post"] == "retention"
    wo = ar.alloc((NPAIR if RET0 else NH_G) * D, BF16, "awo")
    xb = [ar.alloc(D, F32, f"axb{i}") for i in range(2)]
    xrb = [ar.alloc(D, F32, f"xrb{i}") for i in range(2)]
    xn_buf = ar.alloc(D, BF16, "axn")
    qk_raw = [ar.alloc(2 * STW, BF16, f"qkraw{i}") for i in range(2)]
    qk_rope = [ar.alloc(2 * STW, BF16, f"qkrope{i}") for i in range(2)]
    rtmpA = ar.alloc(2 * STW, BF16, "rtmpA")
    rtmpB = rtmpA
    ctab = [ar.alloc(2 * STW, BF16, f"ctab{i}") for i in range(1 if RET0 else 2)]
    if RET0:
        ctab = ctab * 2
    stab = [c[:, :].rearrange("p (a x) -> p a x", a=2)[:, ::-1] for c in ctab]
    NATT = 2 if cfg["post"] == "retention" else 3
    att = [ar.alloc(2 * STW, BF16, f"attp{i}") for i in range(NATT)]
    if cfg["post"] == "softmax":
        rec_sb = ar.alloc(STW, F32, "rec_sb")
        rec_bc = [ar.alloc(STW, F32, f"recbc{h}") for h in range(2)]
    ones_f32 = ar.alloc(128, F32, "ones_f32")
    mask_sb = ar.alloc(4 * 512 if RET0 else 4 * 1024, BF16, "mask_sb")
    xo_buf = [ar.alloc(D, F32, f"axo{i}") for i in range(2)]

    ones_set = p.op("g", lambda g_: g_.memset(ones_f32[:, :], 1.0), name="aones")
    if RET0:
        mld = p.dma("q", mask_sb.rearrange("p (a x) -> p a x", a=4),
                    G["tbl_mask4"].rearrange("p (a d x) -> p a d x", a=4,
                                             d=2)[:, :, 0],
                    name="amask", group="const")
    else:
        mld = p.dma("q", mask_sb, G["tbl_mask4"], name="amask", group="const")
    if RET:
        wgt = ar.alloc(8 * STW, BF16, "awg")
        gate_st = ar.alloc(4 * STW, BF16, "gate_st")
        sil_r = ar.alloc(STW, F32, "sil_r")
        decayB_sb = ar.alloc(16 * 36, F32, "decayB_sb")
        gammaA_sb = ar.alloc(16 * 4, F32, "gammaA_sb")
        yraw = ar.alloc(STW, BF16, "yraw")
        y_tok = ar.alloc(4 * STW, F32, "y_tok")
        y_gn = y_tok
        gated = ar.alloc(4 * STW, BF16, "gated")
        gatedT = ar.alloc(NPAIR * STW, BF16, "gatedT")
        gnw_bc = ar.alloc(D, F32, "gnw_bc")
        gnb_bc = ar.alloc(D, F32, "gnb_bc")
        gn_row = ar.alloc(D, F32, "gn_row")
        gn_y2 = ar.alloc(STW, F32, "gn_y2")
        sums = ar.alloc(32, F32, "gn_s1")
        sumsq = ar.alloc(32, F32, "gn_s2")
        mbuf = ar.alloc(32, F32, "gn_m")
        vbuf = ar.alloc(32, F32, "gn_v")
        rstd2 = ar.alloc(32, F32, "gn_r")
        negm2 = ar.alloc(32, F32, "gn_n")
        bn2 = ar.alloc(8, F32, "gn_bn")
        G["decayB_sb"] = decayB_sb
        d1 = p.dma("q", decayB_sb, G["tbl_decayB"], name="dcl", group="const")
        d2 = p.dma("q", gammaA_sb, G["tbl_gammaA"], name="gal", group="const")
        d3 = p.dma("q", gn_row[0:1, :], G["L1_attn_a_gn_w"].rearrange("(a c) -> a c", a=1),
                   name="gnwl", group="const")
        bc1 = p.op("t", lambda t_: t_.matmul(G["psum_f32"][4][:, :],
                                             ones_f32[0:1, 0:128], gn_row[0:1, 0:512],
                                             start=True, stop=True),
                   after=(d3, ones_set, G["setup"]), name="gnbc1")
        bc2 = p.op("t", lambda t_: t_.matmul(G["psum_f32"][5][:, :],
                                             ones_f32[0:1, 0:128], gn_row[0:1, 512:1024],
                                             start=True, stop=True),
                   after=(d3, ones_set), name="gnbc2")
        e1 = p.op("s", lambda s: s.copy(gnw_bc[:, 0:512], G["psum_f32"][4][:, :]),
                  after=(bc1,), name="gnev1")
        e2 = p.op("s", lambda s: s.copy(gnw_bc[:, 512:1024], G["psum_f32"][5][:, :]),
                  after=(bc2,), name="gnev2")
        d4 = p.dma("q", gn_row[0:1, :], G["L1_attn_a_gn_b"].rearrange("(a c) -> a c", a=1),
                   after=(bc1, bc2), name="gnbl", group="const")
        bc3 = p.op("t", lambda t_: t_.matmul(G["psum_f32"][4][:, :],
                                             ones_f32[0:1, 0:128], gn_row[0:1, 0:512],
                                             start=True, stop=True),
                   after=(d4, e1), name="gnbc3")
        bc4 = p.op("t", lambda t_: t_.matmul(G["psum_f32"][5][:, :],
                                             ones_f32[0:1, 0:128], gn_row[0:1, 512:1024],
                                             start=True, stop=True),
                   after=(d4, e2), name="gnbc4")
        e3 = p.op("s", lambda s: s.copy(gnb_bc[:, 0:512], G["psum_f32"][4][:, :]),
                  after=(bc3,), name="gnev3")
        e4 = p.op("s", lambda s: s.copy(gnb_bc[:, 512:1024], G["psum_f32"][5][:, :]),
                  after=(bc4,), name="gnev4")
        gn_ready = p.op("q", lambda h: h.nop(), after=(e3, e4, d1, d2), name="gnrdy")

    stores_all = []
    for g in range(2):
        wld = []
        for c in range(8):
            sl = slice(c * 128, (c + 1) * 128)
            if cfg["wsrc"] == "qkv":
                W = G["L0_attn_a_W_qkv"]
                for wbuf, off in ((wq, 0), (wk, 1024), (wv, 2048)):
                    wld.append(p.dma("g", wbuf[:, c * STW:(c + 1) * STW],
                                     W[sl, off + g * 512:off + g * 512 + 512],
                                     name=f"awl{g}{c}{off}", group="awq"))
            else:
                wlist = [(wq, cfg["wq"]), (wk, cfg["wk"]), (wv, cfg["wv"])]
                if RET:
                    wlist.append((wgt, cfg["wg"]))
                for wbuf, nm in wlist:
                    wld.append(p.dma("q", wbuf[:, c * STW:(c + 1) * STW],
                                     G[nm][sl, g * 512:g * 512 + 512],
                                     name=f"awl{nm}{g}{c}", group="awq"))
        if cfg.get("wo_pairs"):
            for pr in range(NPAIR):
                rsl = slice(g * 512 + pr * 128, g * 512 + pr * 128 + 128)
                wld.append(p.dma("q", wo[:, pr * D:(pr + 1) * D],
                                 G[cfg["wo"]][rsl, :], name=f"awo{g}{pr}",
                                 group="awq"))
        else:
            for h in range(NH_G):
                rsl = slice(g * 512 + h * 64, g * 512 + h * 64 + 64)
                if cfg["wsrc"] == "qkv":
                    wld.append(p.dma("g", wo[0:64, h * D:(h + 1) * D],
                                     G["L0_attn_a_W_o"][rsl, :],
                                     name=f"awo{g}{h}", group="awq"))
                else:
                    wld.append(p.dma("q", wo[0:64, h * D:(h + 1) * D],
                                     G[cfg["wo"]][rsl, :],
                                     name=f"awo{g}{h}", group="awq"))
        w_ready = wld[-1]
        vones = p.op("g", lambda g_: g_.memset(
            v65[:, :].rearrange("p (t h e) -> p t h e", t=NT, h=NH_G)[:, :, :, 64:65],
            1.0), after=(ones_set,), name=f"avones{g}")

        lt = {}
        if RET and g == 0:
            lt[("ynorm", 0)] = e3
            lt[("ynorm", 1)] = e4
        st_stores = []
        for st in range(NST):
            def emit_qktr(tsk):
                bb = tsk % 2
                trq = []
                for pr in range(NPAIR):
                    o = p.op("t", lambda t_, pr=pr, bb=bb: t_.transpose(
                        pbf[1][:, pr * 128:(pr + 1) * 128],
                        qk_rope[bb][:, pr * 128:(pr + 1) * 128], G["ident"]),
                        after=(lt.get(("rope2", bb)), lt.get("qkT_ev"),
                               lt.get("prep_ev1"))
                        if pr == 0 else (),
                        name=f"aqtr{g}{tsk}_{pr}")
                    trq.append(o)
                for pr in range(NPAIR):
                    o = p.op("t", lambda t_, pr=pr, bb=bb: t_.transpose(
                        pbf[1][:, 512 + pr * 128:512 + (pr + 1) * 128],
                        qk_rope[bb][:, 512 + pr * 128:512 + (pr + 1) * 128],
                        G["ident"]), after=(), name=f"aktr{g}{tsk}_{pr}")
                    trq.append(o)
                tt_in = tsk % 4
                evq2 = p.op("v", lambda v, tt_in=tt_in: v.tensor_copy(
                    qT[:, :].rearrange("p (pr x) -> p pr x", pr=NPAIR)[
                        :, :, tt_in * 128:(tt_in + 1) * 128],
                    pbf[1][:, 0:512].rearrange("p (pr x) -> p pr x", pr=NPAIR)),
                    after=(trq[-1], lt.get("sT_last") if tt_in == 0 else None),
                    name=f"aqev{g}{tsk}")
                evk2 = p.op("v", lambda v, tsk=tsk: v.tensor_copy(
                    kT[:, :].rearrange("p (pr x) -> p pr x", pr=NPAIR)[
                        :, :, tsk * 128:(tsk + 1) * 128],
                    pbf[1][:, 512:1024].rearrange("p (pr x) -> p pr x", pr=NPAIR)),
                    after=(trq[-1],), name=f"akev{g}{tsk}")
                lt["qkT_ev"] = evk2
                lt["qT_ev"] = evq2
                lt["kT_ev"] = evk2
                lt[("rope_done", tsk % 2)] = evk2

            for tt in range(4):
                t = st * 4 + tt
                b = t % 2
                ld = p.dma("q", xb[b], src[t * 128:(t + 1) * 128, :],
                           after=(stat_done, lt.get(("xb", b)),),
                           name=f"axld{g}_{t}", group=f"axb{b}")
                ldc = p.dma("q", ctab[b][:, 0:512],
                            G["tbl_crep"][t * 128:(t + 1) * 128, :],
                            after=(lt.get(("ropetab", b)),), name=f"acr{g}_{t}",
                            group=f"acr{b}")
                lds = p.dma("q", ctab[b][:, 512:1024],
                            G["tbl_srep"][t * 128:(t + 1) * 128, :],
                            after=(lt.get(("ropetab", b)),), name=f"asr{g}_{t}",
                            group=f"acr{b}")
                xo = p.op("s", lambda s, b=b, t=t: s.activation(
                    xn_buf[:, :], xb[b][:, :], AF.Identity,
                    bias=negmr[:, t:t + 1], scale=rstd[:, t:t + 1]),
                    after=(ld, stat_done, lt.get("xn_tr")), name=f"axn{g}{t}")
                lt[("xb", b)] = xo
                trs = []
                for c in range(8):
                    o = p.op("t", lambda t_, c=c: t_.transpose(
                        pbf[0][:, c * 128:(c + 1) * 128],
                        xn_buf[:, c * 128:(c + 1) * 128], G["ident"]),
                        after=(xo, lt.get("xnT_ev"), lt.get("prep_ev0"),
                               G["setup"]) if c == 0 else (xo,),
                        name=f"atr{g}{t}_{c}")
                    trs.append(o)
                lt["xn_tr"] = trs[-1]
                ev = p.op("s", lambda s: s.copy(
                    xnT[:, :].rearrange("p (c k) -> p c k", c=8),
                    pbf[0][:, :].rearrange("p (c k) -> p c k", c=8)),
                    after=(trs[-1], lt.get("proj_mm")), name=f"atrev{g}{t}")
                lt["xnT_ev"] = ev

                def mk_proj(wbuf, bank, perm):
                    def emit(t_):
                        r = None
                        for c in range(8):
                            rhs = wbuf[:, c * STW:(c + 1) * STW]
                            if perm:
                                rhs = rhs.rearrange("p (h half fr) -> p half h fr",
                                                    half=2, h=8, fr=32)
                            r = t_.matmul(bank[:, :], xnT[:, c * 128:(c + 1) * 128],
                                          rhs, start=(c == 0), stop=(c == 7))
                        return r
                    return emit
                mmq = p.op("t", mk_proj(wq, PS[0], True),
                           after=(ev, w_ready, lt.get(("attev", 0)),
                                  lt.get("pevq")), name=f"amq{g}{t}")
                mmk = p.op("t", mk_proj(wk, PS[1], True),
                           after=(lt.get("pevk"),), name=f"amk{g}{t}")
                mmv = p.op("t", mk_proj(wv, PS[2], False),
                           after=(lt.get(("attev", 1)), lt.get("pevv")),
                           name=f"amv{g}{t}")
                lt["proj_mm"] = mmv
                qv = qk_raw[b][:, :].rearrange("p (half qk x) -> p half qk x",
                                               half=2, qk=2)
                evq = p.op("s", lambda s, qv=qv: s.copy(
                    qv[:, :, 0], PS[0][:, :].rearrange("p (half x) -> p half x",
                                                       half=2)),
                    after=(mmq, lt.get(("rope13", b))), name=f"aevq{g}{t}")
                evk = p.op("s", lambda s, qv=qv: s.copy(
                    qv[:, :, 1], PS[1][:, :].rearrange("p (half x) -> p half x",
                                                       half=2)),
                    after=(mmk,), name=f"aevk{g}{t}")
                lt["pevq"] = evq
                lt["pevk"] = evk
                evv = p.op("s", lambda s, t=t: s.copy(
                    v65[:, t * NH_G * 65:(t + 1) * NH_G * 65].rearrange(
                        "p (h e) -> p h e", h=NH_G)[:, :, 0:64],
                    PS[2][:, :].rearrange("p (h d) -> p h d", h=NH_G)),
                    after=(mmv, vones), name=f"aevv{g}{t}")
                lt["pevv"] = evv
                if RET:
                    mmg = p.op("t", mk_proj(wgt, PS[3], False),
                               after=(lt.get("pevg"),), name=f"amg{g}{t}")
                    sv = p.op("s", lambda s: s.activation(
                        sil_r[:, :], PS[3][:, :], AF.Exp, bias=0.0, scale=-1.0),
                        after=(mmg, lt.get("silr"), lt.get("silr2")),
                        name=f"asx{g}{t}")
                    s2 = p.op("v", lambda v: v.tensor_scalar_add(
                        sil_r[:, :], sil_r[:, :], 1.0), after=(sv,),
                        name=f"asa{g}{t}")
                    s3 = p.op("v", lambda v: v.reciprocal(sil_r[:, :], sil_r[:, :]),
                              after=(s2,), name=f"asr{g}{t}")
                    s4 = p.op("v", lambda v, tt=tt: v.scalar_tensor_tensor(
                        gate_st[:, tt * STW:(tt + 1) * STW], PS[3][:, :], 1.0,
                        sil_r[:, :], ALU.mult, ALU.mult),
                        after=(s3, lt.get(("gate_rd", tt))), name=f"asg{g}{t}")
                    lt["pevg"] = s4
                    lt["silr"] = s4
                    lt["silr2"] = s4
                r1 = p.op("v", lambda v, b=b: v.tensor_tensor(
                    rtmpA[:, :], qk_raw[b][:, :], ctab[b][:, :], ALU.mult),
                    after=(evq, evk, ldc, lds, lt.get("rtmpA_rd")),
                    name=f"arp{g}{t}a")
                r2 = p.op("v", lambda v, b=b: v.tensor_tensor(
                    qk_rope[b][:, :].rearrange(
                        "p (qk h half fr) -> p qk h half fr",
                        qk=2, h=8, half=2)[:, :, :, 0],
                    rtmpA[:, 0:512].rearrange("p (qk h fr) -> p qk h fr",
                                              qk=2, h=8),
                    rtmpA[:, 512:1024].rearrange("p (qk h fr) -> p qk h fr",
                                                 qk=2, h=8),
                    ALU.subtract), after=(r1,), name=f"arp{g}{t}b")
                r3 = p.op("v", lambda v, b=b: v.tensor_tensor(
                    rtmpB[:, :].rearrange("p (a x) -> p a x", a=2),
                    qk_raw[b][:, :].rearrange("p (a x) -> p a x", a=2),
                    stab[b], ALU.mult),
                    after=(evq, evk, ldc, lds, lt.get("rtmpB_rd"),
                           r2),
                    name=f"arp{g}{t}c")
                lt[("rope13", b)] = r3
                lt["rtmpA_rd"] = r2
                r4 = p.op("v", lambda v, b=b: v.tensor_tensor(
                    qk_rope[b][:, :].rearrange("p (qk h half fr) -> p qk h half fr",
                                               qk=2, h=8, half=2)[:, :, :, 1],
                    rtmpB[:, 0:512].rearrange("p (qk h fr) -> p qk h fr",
                                              qk=2, h=8),
                    rtmpB[:, 512:1024].rearrange("p (qk h fr) -> p qk h fr",
                                                 qk=2, h=8),
                    ALU.add), after=(r3, r2), name=f"arp{g}{t}d")
                lt["rtmpB_rd"] = r4
                lt["rtmpA_rd"] = r4
                if RET:
                    lt[("ropetab", 0)] = r4
                    lt[("ropetab", 1)] = r4
                else:
                    lt[("ropetab", b)] = r4
                lt[("rope2", b)] = r4
                lt[("rope", b)] = r4
                if G.get("prep_tick"):
                    pv = G["prep_tick"](lt.get("xnT_ev"), lt.get("qkT_ev"))
                    if pv is not None:
                        lt["prep_ev0"] = pv[0] or lt.get("prep_ev0")
                        lt["prep_ev1"] = pv[1] or lt.get("prep_ev1")
                if tt >= 1:
                    emit_qktr(t - 1)
            emit_qktr(st * 4 + 3)
            nkc = 4 * (st + 1)
            av_last = []
            for pr in range(NPAIR):
                hA, hB = 2 * pr, 2 * pr + 1
                avp = []
                pend = {}

                def emit_avs(kc, last_chunk):
                    rr = kc % NATT
                    e = pend.pop(kc)
                    for hi, h in enumerate((hA, hB)):
                        bankC = PS[4 + hi]
                        dep = [e]
                        if kc == 0:
                            dep += [lt.get(("ynorm", hi)), lt.get("res_last"),
                                    lt.get(("bcE", hi))]
                        av = p.op("t", lambda t_, kc=kc, h=h, hi=hi,
                                  bankC=bankC, rr=rr, last_chunk=last_chunk:
                                  t_.matmul(
                            bankC[0:65, :],
                            v65[:, (kc * NH_G + h) * 65:(kc * NH_G + h) * 65 + 65],
                            att[rr][:, hi * 512:hi * 512 + 512],
                            start=(kc == 0), stop=last_chunk),
                            after=tuple(dep), name=f"av{g}_{st}_{pr}_{kc}_{hi}")
                        avp.append(av)
                    lt[("avr", rr)] = avp[-1]

                for kc in range(nkc):
                    pp = kc % 2
                    rr = kc % NATT
                    psd = G["psum_d"][pp]
                    mmA = p.op("t", lambda t_, pr=pr, kc=kc, pp=pp: t_.matmul(
                        PS[2 * pp][:, :],
                        kT[0:64, pr * TOK + kc * 128:pr * TOK + kc * 128 + 128],
                        qT[0:64, pr * STW:(pr + 1) * STW],
                        start=True, stop=True, tile_position=(0, 0)),
                        after=(lt.get("qT_ev"), lt.get("kT_ev"),
                               lt.get("pevq"), lt.get("pevk"), lt.get("pevv"),
                               lt.get("pevg"), lt.get(("attev", pp))),
                        name=f"sT{g}_{st}_{pr}_{kc}A")
                    mmB = p.op("t", lambda t_, pr=pr, kc=kc, pp=pp: t_.matmul(
                        PS[2 * pp + 1][:, :],
                        kT[64:128, pr * TOK + kc * 128:pr * TOK + kc * 128 + 128],
                        qT[64:128, pr * STW:(pr + 1) * STW],
                        start=True, stop=True, tile_position=(64, 0)),
                        after=(), name=f"sT{g}_{st}_{pr}_{kc}B")
                    ab = att[rr]
                    if cfg["post"] == "softmax":
                        e = p.op("s", lambda s, psd=psd, ab=ab: s.activation(
                            ab[:, :], psd[:, :], AF.Exp, bias=0.0, scale=0.125),
                            after=(mmA, mmB, lt.get(("avr", rr))),
                            name=f"ex{g}_{st}_{pr}_{kc}")
                    else:
                        h_abs = g * NH_G + 2 * pr
                        m = 4 * st - kc + 3
                        eA = p.op("s", lambda s, psd=psd, ab=ab, h_abs=h_abs,
                                  m=m: s.activation(
                                      ab[:, 0:512], psd[:, 0:512], AF.Identity,
                                      bias=0.0,
                                      scale=G["decayB_sb"][:, h_abs * 36 + m:h_abs * 36 + m + 1]),
                                  after=(mmA, lt.get(("avr", rr))),
                                  name=f"dcA{g}_{st}_{pr}_{kc}")
                        e = p.op("v", lambda v, psd=psd, ab=ab, h_abs=h_abs,
                                 m=m: v.tensor_scalar(
                                     ab[:, 512:1024], psd[:, 512:1024],
                                     G["decayB_sb"][:, (h_abs + 1) * 36 + m:(h_abs + 1) * 36 + m + 1],
                                     None, ALU.mult),
                                 after=(mmB, eA, lt.get(("avr", rr))),
                                 name=f"dcB{g}_{st}_{pr}_{kc}")
                    if kc >= 4 * st:
                        oi = kc - 4 * st
                        if RET:
                            eH = p.op("v", lambda v, ab=ab, oi=oi: v.tensor_tensor(
                                ab[:, 0:512], ab[:, 0:512],
                                mask_sb[:, oi * 512:(oi + 1) * 512], ALU.mult),
                                after=(e, mld), name=f"mkA{g}_{st}_{pr}_{kc}")
                            e = p.op("v", lambda v, ab=ab, oi=oi: v.tensor_tensor(
                                ab[:, 512:1024], ab[:, 512:1024],
                                mask_sb[:, oi * 512:(oi + 1) * 512], ALU.mult),
                                after=(eH,), name=f"mkB{g}_{st}_{pr}_{kc}")
                        else:
                            e = p.op("v", lambda v, ab=ab, oi=oi: v.tensor_tensor(
                                ab[:, :], ab[:, :],
                                mask_sb[:, oi * 1024:(oi + 1) * 1024], ALU.mult),
                                after=(e, mld), name=f"mk{g}_{st}_{pr}_{kc}")
                    lt[("attev", pp)] = e
                    pend[kc] = e
                    if kc >= 1:
                        emit_avs(kc - 1, False)
                emit_avs(nkc - 1, True)
                lt["sT_last"] = avp[-1]
                av_last = avp[-2:]
                for hi, h in enumerate((hA, hB)):
                    bankC = PS[4 + hi]
                    if cfg["post"] == "softmax":
                        rc = p.op("v", lambda v, bankC=bankC: v.reciprocal(
                            rec_sb[64:65, :], bankC[64:65, :]),
                            after=(avp[-1], lt.get("rec_rd")),
                            name=f"rc{g}_{st}_{pr}_{hi}")
                        bc = p.op("t", lambda t_, bankC=bankC: t_.matmul(
                            bankC[64:128, :], ones_f32[64:65, 0:64],
                            rec_sb[64:65, :], start=True, stop=True,
                            tile_position=(64, 64)),
                            after=(rc, ones_set), name=f"bc{g}_{st}_{pr}_{hi}")
                        lt["rec_rd"] = bc
                        evb = p.op("s", lambda s, bankC=bankC, hi=hi: s.copy(
                            rec_bc[hi][0:64, :], bankC[64:128, :]),
                            after=(bc, lt.get(("recbc_rd", hi))),
                            name=f"eb{g}_{st}_{pr}_{hi}")
                        lt[("bcE", hi)] = evb
                        yn = p.op("v", lambda v, h=h, hi=hi, bankC=bankC:
                                  v.scalar_tensor_tensor(
                                      yT[0:64, h * STW:(h + 1) * STW],
                                      bankC[0:64, :], 1.0, rec_bc[hi][0:64, :],
                                      ALU.mult, ALU.mult),
                                  after=(evb, lt.get(("yT_rd", h))),
                                  name=f"yn{g}_{st}_{pr}_{hi}")
                        lt[("recbc_rd", hi)] = yn
                        lt[("ynorm", hi)] = yn
                    else:
                        h_abs = g * NH_G + h
                        yev = p.op("s", lambda s, bankC=bankC: s.copy(
                            yraw[0:64, :], bankC[0:64, :]),
                            after=(avp[-1], lt.get("yraw_rd")),
                            name=f"ye{g}_{st}_{pr}_{hi}")
                        ytr = []
                        for sub in range(4):
                            o = p.op("t", lambda t_, sub=sub: t_.transpose(
                                pbf[1][:, sub * 64:(sub + 1) * 64],
                                yraw[0:64, sub * 128:(sub + 1) * 128],
                                G["ident"][0:64, 0:64]),
                                after=(yev, lt.get("ypsum_rd"),
                                       lt.get("qkT_ev")) if sub == 0 else (yev,),
                                name=f"ytr{g}_{st}_{pr}_{hi}_{sub}")
                            ytr.append(o)
                        lt["yraw_rd"] = ytr[-1]
                        yevs = []
                        for sub in range(4):
                            o = p.op("s", lambda s, sub=sub, h=h, h_abs=h_abs:
                                     s.activation(
                                         y_tok[:, sub * STW + h * 64:sub * STW + h * 64 + 64],
                                         pbf[1][:, sub * 64:(sub + 1) * 64],
                                         AF.Identity, bias=0.0,
                                         scale=gammaA_sb[:, h_abs * 4 + sub:h_abs * 4 + sub + 1]),
                                     after=(ytr[-1], lt.get("ytok_rd") if sub == 0 else None),
                                     name=f"ysc{g}_{st}_{pr}_{hi}_{sub}")
                            yevs.append(o)
                        if yevs:
                            lt["ypsum_rd"] = yevs[-1]
                            lt["qkT_ev"] = yevs[-1]
                        lt[("ynorm", hi)] = yevs[-1] if yevs else yev
                        lt[("ytok_w", h)] = yevs[-1]
            if RET:
                ytok_done = [lt[("ytok_w", h)] for h in range(NH_G)]
                if cfg.get("dbg") and g == 0 and st == 0:
                    p.dma("q", G["dbg_yT"][:, 0:4096].bitcast(F32)[:, 0:2048],
                          y_tok[:, :], after=tuple(ytok_done), name="dytok",
                          group="dbg")
                    p.dma("q", G["dbg_qT"], qT[:, :], after=tuple(ytok_done),
                          name="dqT2", group="dbg")
                    p.barrier("dbgbar1")
                gops = []
                for sub in (range(4) if BIS >= 3 else []):
                    sq = p.op("s", lambda s, sub=sub: s.square(
                        gn_y2[:, :], y_tok[:, sub * STW:(sub + 1) * STW]),
                        after=tuple(ytok_done) + (lt.get("y2_rd"),),
                        name=f"gsq{g}_{st}_{sub}")
                    r1 = p.op("v", lambda v, sub=sub: v.reduce_sum(
                        sums[:, sub * 8:(sub + 1) * 8].rearrange("p (a o) -> p a o", o=1),
                        y_tok[:, sub * STW:(sub + 1) * STW].rearrange(
                            "p (a d) -> p a d", a=8), axis=mybir.AxisListType.X),
                        after=tuple(ytok_done) + (lt.get("sums_rd"),),
                        name=f"gr1{g}_{st}_{sub}")
                    r2 = p.op("v", lambda v, sub=sub: v.reduce_sum(
                        sumsq[:, sub * 8:(sub + 1) * 8].rearrange("p (a o) -> p a o", o=1),
                        gn_y2[:, :].rearrange("p (a d) -> p a d", a=8),
                        axis=mybir.AxisListType.X),
                        after=(sq, lt.get("sums_rd")), name=f"gr2{g}_{st}_{sub}")
                    lt["y2_rd"] = r2
                    gops += [r1, r2]
                if BIS < 3:
                    g6 = ytok_done[-1]
                else:
                    g1 = p.op("v", lambda v: v.tensor_scalar_mul(
                        mbuf[:, :], sums[:, :], 1.0 / 64.0),
                        after=tuple(gops), name=f"gm{g}_{st}")
                    g2 = p.op("v", lambda v: v.tensor_tensor(
                        vbuf[:, :], mbuf[:, :], mbuf[:, :], ALU.mult),
                        after=(g1,), name=f"gv1{g}_{st}")
                    g3 = p.op("v", lambda v: v.scalar_tensor_tensor(
                        vbuf[:, :], sumsq[:, :], 1.0 / 64.0, vbuf[:, :],
                        ALU.mult, ALU.subtract), after=(g2,), name=f"gv2{g}_{st}")
                    g4 = p.op("s", lambda s: s.activation(
                        rstd2[:, :], vbuf[:, :], AF.Ln, bias=G["eps"], scale=1.0),
                        after=(g3,), name=f"gl{g}_{st}")
                    g5 = p.op("s", lambda s: s.activation(
                        rstd2[:, :], rstd2[:, :], AF.Exp, bias=0.0, scale=-0.5),
                        after=(g4,), name=f"ge{g}_{st}")
                    g6 = p.op("v", lambda v: v.scalar_tensor_tensor(
                        negm2[:, :], mbuf[:, :], -1.0, rstd2[:, :],
                        ALU.mult, ALU.mult), after=(g5,), name=f"gn{g}_{st}")
                    lt["sums_rd"] = g6
                if cfg.get("dbg") and g == 0 and st == 0:


                    p.dma("q", G["dbg_stats"][:, 32:64], rstd2[:, :], after=(g6,),
                          name="drs", group="dbg")
                    p.barrier("dbgbar3")
                seg_ops = []
                for sub in (range(4) if BIS >= 4 else []):
                    for h in range(NH_G):
                        idx = sub * 8 + h
                        o = p.op("s", lambda s, sub=sub, h=h, idx=idx: s.activation(
                            y_gn[:, sub * STW + h * 64:sub * STW + h * 64 + 64],
                            y_tok[:, sub * STW + h * 64:sub * STW + h * 64 + 64],
                            AF.Identity, bias=negm2[:, idx:idx + 1],
                            scale=rstd2[:, idx:idx + 1]),
                            after=(g6, lt.get("ygn_rd") if idx == 0 else None),
                            name=f"gs{g}_{st}_{sub}_{h}")
                        seg_ops.append(o)
                if not seg_ops:
                    seg_ops = [g6]
                lt["ytok_rd"] = seg_ops[-1]
                if cfg.get("dbg") and g == 0 and st == 0:
                    p.dma("q", G["dbg_gnb"][:, 0:512].bitcast(F32)[:, 0:512][:, 0:512]
                          if False else G["dbg_gnb"][:, 0:512], y_gn[:, 0:512],
                          after=(seg_ops[-1],), name="dseg", group="dbg")
                    p.dma("q", G["dbg_stats"][:, 0:32], negm2[:, :],
                          after=(seg_ops[-1],), name="dnm", group="dbg")
                    p.barrier("dbgbar6")
                gate_done = []
                for sub in (range(4) if BIS >= 5 else []):
                    c1 = p.op("v", lambda v, sub=sub, g=g: v.tensor_tensor(
                        gn_y2[:, :], y_gn[:, sub * STW:(sub + 1) * STW],
                        gnw_bc[:, g * 512:g * 512 + 512], ALU.mult),
                        after=(seg_ops[-1], lt.get("y2_rd"), gn_ready),
                        name=f"gc1{g}_{st}_{sub}")
                    c2 = p.op("v", lambda v, sub=sub, g=g: v.tensor_tensor(
                        gn_y2[:, :], gn_y2[:, :],
                        gnb_bc[:, g * 512:g * 512 + 512], ALU.add),
                        after=(c1,), name=f"gc2{g}_{st}_{sub}")
                    if cfg.get("dbg") and g == 0 and st == 0 and sub == 0:
                        p.dma("q", G["dbg_gnw"][:, 512:1024], gn_y2[:, :],
                              after=(c1,), name="dgy1", group="dbg")
                        p.barrier("dbgbar5")
                        p.dma("q", G["dbg_gnw"][:, 0:512], gn_y2[:, :],
                              after=(c2,), name="dgy2", group="dbg")
                        p.barrier("dbgbar4")
                    c3 = p.op("v", lambda v, sub=sub: v.tensor_tensor(
                        gated[:, sub * STW:(sub + 1) * STW], gn_y2[:, :],
                        gate_st[:, sub * STW:(sub + 1) * STW], ALU.mult),
                        after=(c2, lt.get(("gated_rd", sub))),
                        name=f"gc3{g}_{st}_{sub}")
                    lt["y2_rd"] = c3
                    lt[("gate_rd", sub)] = c3
                    gate_done.append(c3)
                if not gate_done:
                    gate_done = [seg_ops[-1]] * 4
                lt["ygn_rd"] = gate_done[-1]
                trb = []
                for sub in (range(4) if BIS >= 6 else []):
                    bk = pbf[sub % 2]
                    for pr in range(NPAIR):
                        o = p.op("t", lambda t_, sub=sub, pr=pr, bk=bk: t_.transpose(
                            bk[:, pr * 128:(pr + 1) * 128],
                            gated[:, sub * STW + pr * 128:sub * STW + pr * 128 + 128],
                            G["ident"]),
                            after=(gate_done[sub], lt.get(("gtb_ev", sub % 2)),
                                   lt.get("ypsum_rd") if sub <= 1 else None)
                            if pr == 0 else (gate_done[sub],),
                            name=f"gt{g}_{st}_{sub}_{pr}")
                        trb.append(o)
                    ev = p.op("v", lambda v, sub=sub, bk=bk: v.tensor_copy(
                        gatedT[:, :].rearrange("p (pr x) -> p pr x", pr=NPAIR)[
                            :, :, sub * 128:(sub + 1) * 128],
                        bk[:, 0:512].rearrange("p (pr x) -> p pr x", pr=NPAIR)),
                        after=(trb[-1], lt.get(("gatedT_rd", sub)),),
                        name=f"gte{g}_{st}_{sub}")
                    lt[("gtb_ev", sub % 2)] = ev
                    lt[("gated_rd", sub)] = ev
                if BIS >= 6:
                    lt["xnT_ev"] = lt[("gtb_ev", 1)]
                    y_done = [lt[("gtb_ev", 0)], lt[("gtb_ev", 1)]]
                else:
                    y_done = [gate_done[-1], gate_done[-1]]
                if cfg.get("dbg") and g == 0 and st == 0:
                    p.dma("q", G["dbg_v"][:, 0:2048].bitcast(F32)[:, 0:1024],
                          y_gn[:, 0:1024], after=(gate_done[-1],),
                          name="dygn", group="dbg")
                    p.dma("q", G["dbg_gate"], gate_st[:, :],
                          after=(gate_done[-1],), name="dgate", group="dbg")
                    p.dma("q", G["dbg_qrope"][:, 0:512], gated[:, 0:512],
                          after=(gate_done[-1],), name="dgated", group="dbg")
                    p.dma("q", G["dbg_qraw"][:, 0:512], gated[:, 512:1024],
                          after=(gate_done[-1],), name="dgated2", group="dbg")
                    p.barrier("dbgbar2")
            else:
                y_done = [lt[("ynorm", 0)], lt[("ynorm", 1)]]
            if cfg.get("dbg") and g == 0 and st == 0 and not RET:
                p.dma("q", G["dbg_yT"][0:64, :], yT[0:64, :], after=tuple(y_done),
                      name="dyT", group="dbg")
                p.dma("q", G["dbg_qT"], qT[:, :], after=tuple(y_done),
                      name="dqT", group="dbg")
                p.dma("q", G["dbg_kT"], kT[:, 0:NPAIR * STW].rearrange(
                    "p (pr x) -> p pr x", pr=NPAIR), after=tuple(y_done),
                    name="dkT", group="dbg")
                p.dma("q", G["dbg_v"], v65[:, 0:4 * NH_G * 65],
                      after=tuple(y_done), name="dv", group="dbg")
            base = src if g == 0 else dst
            for tt in range(4):
                t = st * 4 + tt
                b = tt % 2
                ldr = p.dma("q", xrb[b], base[t * 128:(t + 1) * 128, :],
                            after=(lt.get(("xr", b)),), name=f"arl{g}_{t}",
                            group=f"axr{b}")
                res_ops = []
                for col in range(2):
                    bank = PS[4 + col]

                    if cfg.get("wo_pairs"):
                        def mk_op(bank=bank, col=col, tt=tt):
                            def emit(t_):
                                r = None
                                for pr in range(NPAIR):
                                    r = t_.matmul(
                                        bank[:, :],
                                        gatedT[:, pr * STW + tt * 128:pr * STW + tt * 128 + 128],
                                        wo[:, pr * D + col * 512:pr * D + col * 512 + 512],
                                        start=(pr == 0), stop=(pr == NPAIR - 1))
                                return r
                            return emit
                    else:
                        def mk_op(bank=bank, col=col, tt=tt):
                            def emit(t_):
                                r = None
                                for h in range(NH_G):
                                    r = t_.matmul(
                                        bank[:, :],
                                        yT[0:64, h * STW + tt * 128:h * STW + tt * 128 + 128],
                                        wo[0:64, h * D + col * 512:h * D + col * 512 + 512],
                                        start=(h == 0), stop=(h == NH_G - 1))
                                return r
                            return emit
                    dep = [y_done[0], y_done[1], w_ready, lt.get(("res_bank", col))]
                    mo = p.op("t", mk_op(), after=tuple(dep),
                              name=f"op{g}_{st}_{tt}_{col}")
                    lt[("yT_rd", 2 * col)] = mo
                    lt[("yT_rd", 2 * col + 1)] = mo
                    ro = p.op("v", lambda v, col=col, b=b, bank=bank, tt=tt:
                              v.scalar_tensor_tensor(
                                  xo_buf[tt % 2][:, col * 512:(col + 1) * 512],
                                  bank[:, :], 1.0,
                                  xrb[b][:, col * 512:(col + 1) * 512],
                                  ALU.mult, ALU.add),
                              after=(mo, ldr, lt.get(("xo_st", tt % 2))),
                              name=f"ro{g}_{st}_{tt}_{col}")
                    lt[("res_bank", col)] = ro
                    res_ops.append(ro)
                for h in range(NH_G):
                    lt[("yT_rd", h)] = res_ops[-1]
                for sub in range(4):
                    lt[("gatedT_rd", sub)] = res_ops[-1]
                lt[("xr", b)] = res_ops[-1]
                lt["res_last"] = res_ops[-1]
                sto = p.dma("q", dst[t * 128:(t + 1) * 128, :], xo_buf[tt % 2],
                            after=tuple(res_ops), name=f"ast{g}_{t}",
                            group=f"ast{tt % 2}")
                lt[("xo_st", tt % 2)] = sto
                st_stores.append(sto)
        p.barrier(f"attnpass{g}")
        stores_all = st_stores
    ar.pop()
    return stores_all


def stage_moe(p, ar, G, src, dst, barrier):
    F_ = 2048
    NFC = F_ // 128
    QW = min(1024, TOK)
    NQ = TOK // QW
    TPQ = QW // 128
    NHALF = QW // 512
    PS = G["psum_f32"]
    pbf = G["psum_bf"]
    ar.push()
    rstd1, negm1, stat1 = ln_stats_pass(p, ar, G, src)
    stats2m = ar.alloc(NT, F32, "m2")
    stats2v = ar.alloc(NT, F32, "v2")
    s12 = ar.alloc(NT, F32, "s12")
    b12 = ar.alloc(NT, F32, "b12")
    bn2 = ar.alloc(14, F32, "mbn")
    ln1b = ar.alloc(D, F32, "ln1")
    xb2 = [ar.alloc(D, F32, f"mxb{i}") for i in range(2)]
    lt = {}
    last = None
    prev_use = [None, None]
    for t in range(NT):
        b = t % 2
        ld = p.dma("q", xb2[b], src[t * 128:(t + 1) * 128, :],
                   after=(stat1, prev_use[b]), name=f"mld{t}", group=f"mxb{b}")
        af = p.op("s", lambda s, b=b, t=t: s.activation(
            ln1b[:, :], xb2[b][:, :], AF.Identity,
            bias=negm1[:, t:t + 1], scale=rstd1[:, t:t + 1]),
            after=(ld, stat1, last), name=f"maf{t}")
        b1 = p.op("v", lambda v: v.bn_stats(bn2[:, 0:6], ln1b[:, 0:512]),
                  after=(af, last), name=f"mb1{t}")
        b2 = p.op("v", lambda v: v.bn_stats(bn2[:, 6:12], ln1b[:, 512:1024]),
                  after=(af,), name=f"mb2{t}")
        ba = p.op("v", lambda v: v.bn_aggr(bn2[:, 12:14], bn2[:, 0:12]),
                  after=(b1, b2), name=f"mba{t}")
        c1 = p.op("v", lambda v, t=t: v.tensor_copy(stats2m[:, t:t + 1],
                                                    bn2[:, 12:13]),
                  after=(ba,), name=f"mc1{t}")
        c2 = p.op("v", lambda v, t=t: v.tensor_copy(stats2v[:, t:t + 1],
                                                    bn2[:, 13:14]),
                  after=(ba,), name=f"mc2{t}")
        prev_use[b] = af
        last = c2
    o1 = p.op("s", lambda s: s.activation(s12[:, :], stats2v[:, :], AF.Ln,
                                          bias=G["eps"], scale=1.0),
              after=(last, G["consts_ready"]), name="mlnv")
    o2 = p.op("s", lambda s: s.activation(s12[:, :], s12[:, :], AF.Exp,
                                          bias=0.0, scale=-0.5), after=(o1,),
              name="mexpv")
    o3 = p.op("v", lambda v: v.tensor_tensor(b12[:, :], negm1[:, :],
                                             stats2m[:, :], ALU.subtract),
              after=(o2, last), name="mb12a")
    o4 = p.op("v", lambda v: v.tensor_tensor(b12[:, :], b12[:, :], s12[:, :],
                                             ALU.mult), after=(o3,), name="mb12b")
    o5 = p.op("v", lambda v: v.tensor_tensor(s12[:, :], s12[:, :], rstd1[:, :],
                                             ALU.mult), after=(o4,), name="ms12")
    stat2 = o5

    gb_row = ar.alloc(D, F32, "mgrow")
    gam_bc = ar.alloc(D, F32, "mgam")
    bet_bc = ar.alloc(D, F32, "mbet")
    ones_f = ar.alloc(128, F32, "mones")
    ones_set = p.op("g", lambda g_: g_.memset(ones_f[:, :], 1.0), name="mone")
    d3 = p.dma("q", gb_row[0:1, :],
               G["L1_ffn_ln_gamma"].rearrange("(a c) -> a c", a=1),
               name="mgl", group="const")
    bc1 = p.op("t", lambda t_: t_.matmul(PS[0][:, :], ones_f[0:1, 0:128],
                                         gb_row[0:1, 0:512], start=True,
                                         stop=True),
               after=(d3, ones_set, G["setup"]), name="mbc1")
    bc2 = p.op("t", lambda t_: t_.matmul(PS[1][:, :], ones_f[0:1, 0:128],
                                         gb_row[0:1, 512:1024], start=True,
                                         stop=True),
               after=(d3, ones_set), name="mbc2")
    e1 = p.op("s", lambda s: s.copy(gam_bc[:, 0:512], PS[0][:, :]),
              after=(bc1,), name="mge1")
    e2 = p.op("s", lambda s: s.copy(gam_bc[:, 512:1024], PS[1][:, :]),
              after=(bc2,), name="mge2")
    d4 = p.dma("q", gb_row[0:1, :],
               G["L1_ffn_ln_beta"].rearrange("(a c) -> a c", a=1),
               after=(bc1, bc2), name="mbl", group="const")
    bc3 = p.op("t", lambda t_: t_.matmul(PS[0][:, :], ones_f[0:1, 0:128],
                                         gb_row[0:1, 0:512], start=True,
                                         stop=True), after=(d4, e1), name="mbc3")
    bc4 = p.op("t", lambda t_: t_.matmul(PS[1][:, :], ones_f[0:1, 0:128],
                                         gb_row[0:1, 512:1024], start=True,
                                         stop=True), after=(d4, e2), name="mbc4")
    e3 = p.op("s", lambda s: s.copy(bet_bc[:, 0:512], PS[0][:, :]),
              after=(bc3,), name="mge3")
    e4 = p.op("s", lambda s: s.copy(bet_bc[:, 512:1024], PS[1][:, :]),
              after=(bc4,), name="mge4")
    gb_ready = p.op("q", lambda h: h.nop(), after=(e3, e4), name="mgbr")

    gw_sb = ar.alloc(64, BF16, "mgw")
    gwld = p.dma("q", gw_sb.rearrange("p (k c) -> p k c", k=8),
                 G["gate_wT"].rearrange("(k pp) c -> pp k c", pp=128),
                 name="mgwl", group="const")

    rn_f = ln1b
    rn_bf = ar.alloc(D, BF16, "mrnb")
    rnT_t = ar.alloc(D, BF16, "mrntt")
    comb = ar.alloc(NT * 8, F32, "mcomb")
    rbuf = ar.alloc(8, F32, "mrb")
    mbuf = ar.alloc(8, F32, "mmk")
    tb1 = ar.alloc(8, F32, "mtb")
    rcol = ar.alloc(4, F32, "mrcl")

    def emit_rn(t, b, deps, name):
        a1 = p.op("s", lambda s, b=b, t=t: s.activation(
            rn_f[:, :], xb2[b][:, :], AF.Identity,
            bias=b12[:, t:t + 1], scale=s12[:, t:t + 1]),
            after=deps + (stat2, lt.get("rnf_rd")), name=name + "a")
        a2 = p.op("v", lambda v: v.tensor_tensor(rn_f[:, :], rn_f[:, :],
                                                 gam_bc[:, :], ALU.mult),
                  after=(a1, gb_ready), name=name + "b")
        a3 = p.op("v", lambda v: v.tensor_tensor(rn_bf[:, :], rn_f[:, :],
                                                 bet_bc[:, :], ALU.add),
                  after=(a2, lt.get("rnbf_rd")), name=name + "c")
        lt["rnf_rd"] = a3
        return a3

    prev_ld = [stat2, stat2]
    for t in range(NT):
        b = t % 2
        ld = p.dma("q", xb2[b], src[t * 128:(t + 1) * 128, :],
                   after=(prev_ld[b],), name=f"mrld{t}", group=f"mxb{b}")
        rn = emit_rn(t, b, (ld,), f"mrn{t}")
        prev_ld[b] = rn
        trs = []
        for c in range(8):
            o = p.op("t", lambda t_, c=c: t_.transpose(
                pbf[0][:, c * 128:(c + 1) * 128],
                rn_bf[:, c * 128:(c + 1) * 128], G["ident"]),
                after=(rn, G["setup"], lt.get("pbf0_rd")) if c == 0 else (rn,),
                name=f"mtr{t}_{c}")
            trs.append(o)
        lt["rnbf_rd"] = trs[-1]
        evt = p.op("v", lambda v: v.tensor_copy(rnT_t[:, :], pbf[0][:, :]),
                   after=(trs[-1], lt.get("rntt_rd")), name=f"mte{t}")
        lt["pbf0_rd"] = evt

        def mk_logit():
            def emit(t_):
                r = None
                for c in range(8):
                    r = t_.matmul(PS[2][:, 0:8],
                                  rnT_t[:, c * 128:(c + 1) * 128],
                                  gw_sb[:, c * 8:(c + 1) * 8],
                                  start=(c == 0), stop=(c == 7))
                return r
            return emit
        lg = p.op("t", mk_logit(), after=(evt, gwld, lt.get("lg_rd")),
                  name=f"mlg{t}")
        lt["rntt_rd"] = lg
        s1 = p.op("v", lambda v: v.tensor_reduce(
            rcol[:, 0:1], PS[2][:, 0:8], mybir.AxisListType.X, ALU.max),
            after=(lg, lt.get("rt_done")), name=f"mmx{t}")
        s2 = p.op("v", lambda v: v.tensor_scalar_mul(rcol[:, 1:2],
                                                     rcol[:, 0:1], -1.0),
                  after=(s1,), name=f"mnm{t}")
        s3 = p.op("s", lambda s: s.activation(rbuf[:, :], PS[2][:, 0:8],
                                              AF.Exp, bias=rcol[:, 1:2],
                                              scale=1.0),
                  after=(s2,), name=f"mex{t}")
        lt["lg_rd"] = s3
        s4 = p.op("v", lambda v: v.tensor_reduce(
            rcol[:, 2:3], rbuf[:, :], mybir.AxisListType.X, ALU.add),
            after=(s3,), name=f"msm{t}")
        s5 = p.op("v", lambda v: v.reciprocal(rcol[:, 3:4], rcol[:, 2:3]),
                  after=(s4,), name=f"mrc{t}")
        t1 = p.op("v", lambda v: v.tensor_reduce(
            rcol[:, 0:1], rbuf[:, :], mybir.AxisListType.X, ALU.max),
            after=(s5,), name=f"mt1{t}")
        t2 = p.op("v", lambda v: v.tensor_scalar(
            mbuf[:, :], rbuf[:, :], rcol[:, 0:1], None, ALU.is_ge),
            after=(t1,), name=f"mt2{t}")
        t3 = p.op("v", lambda v: v.tensor_tensor(tb1[:, :], rbuf[:, :],
                                                 mbuf[:, :], ALU.mult),
                  after=(t2,), name=f"mt3{t}")
        t4 = p.op("v", lambda v: v.tensor_tensor(tb1[:, :], rbuf[:, :],
                                                 tb1[:, :], ALU.subtract),
                  after=(t3,), name=f"mt4{t}")
        t5 = p.op("v", lambda v: v.tensor_reduce(
            rcol[:, 0:1], tb1[:, :], mybir.AxisListType.X, ALU.max),
            after=(t4,), name=f"mt5{t}")
        t6 = p.op("v", lambda v: v.tensor_scalar(
            tb1[:, :], tb1[:, :], rcol[:, 0:1], None, ALU.is_ge),
            after=(t5,), name=f"mt6{t}")
        t7 = p.op("v", lambda v: v.tensor_tensor(mbuf[:, :], mbuf[:, :],
                                                 tb1[:, :], ALU.add),
                  after=(t6,), name=f"mt7{t}")
        t8 = p.op("v", lambda v: v.tensor_tensor(mbuf[:, :], rbuf[:, :],
                                                 mbuf[:, :], ALU.mult),
                  after=(t7,), name=f"mt8{t}")
        t9 = p.op("v", lambda v, t=t: v.tensor_scalar(
            comb[:, t * 8:(t + 1) * 8], mbuf[:, :], rcol[:, 3:4], None,
            ALU.mult), after=(t8, lt.get("comb_rd")), name=f"mt9{t}")
        lt["rt_done"] = t9
    router_done = lt["rt_done"]

    rnT = ar.alloc(8 * QW, BF16, "mrnT")
    acc = ar.alloc(TPQ * D, F32, "macc")
    hT = ar.alloc(NFC * 512, BF16, "mhT")
    wb = [ar.alloc(16384, BF16, f"mw{i}") for i in range(3)]
    tmp_y = [ar.alloc(512, F32, f"mty{i}") for i in range(2)]
    stores = []

    w1v = G["moe_w1T"].rearrange("e (k pp) f -> e pp k f", pp=128)
    w2v = G["moe_w2T"].rearrange("e (k pp) c -> e pp k c", pp=128)
    wslot_user = [None, None, None]
    wload = {}

    def load_w(e, which, dep):
        si = (2 * e + (0 if which == 1 else 1)) % 3
        sq = ("q", "q", "g")[si]
        if which == 1:
            dma = p.dma(sq, wb[si].rearrange("p (k f) -> p k f", k=8),
                        w1v[e], after=(wslot_user[si], dep, barrier),
                        name=f"mwl1_{e}", group=f"mwg{si}")
        else:
            dma = p.dma(sq, wb[si].rearrange("p (k c) -> p k c", k=16),
                        w2v[e], after=(wslot_user[si], dep, barrier),
                        name=f"mwl2_{e}", group=f"mwg{si}")
        wload[(e, which)] = dma
        return si

    for q in range(NQ):
        rn_done = []
        for tt in range(TPQ):
            t = q * TPQ + tt
            b = tt % 2
            ld = p.dma("q", xb2[b], src[t * 128:(t + 1) * 128, :],
                       after=(lt.get(("mxa", b)), lt.get(("mxr", b)),
                              router_done),
                       name=f"mqld{t}", group=f"mxb{b}")
            rn = emit_rn(t, b, (ld,), f"mqrn{t}")
            ai = p.op("v", lambda v, tt=tt, b=b: v.tensor_copy(
                acc[:, tt * D:(tt + 1) * D], xb2[b][:, :]),
                after=(ld, lt.get(("acc_rd", tt)),), name=f"macc{q}_{tt}")
            lt[("mxa", b)] = ai
            lt[("mxr", b)] = rn
            lt[("macc_i", tt)] = ai
            trs = []
            for c in range(8):
                o = p.op("t", lambda t_, c=c: t_.transpose(
                    pbf[0][:, c * 128:(c + 1) * 128],
                    rn_bf[:, c * 128:(c + 1) * 128], G["ident"]),
                    after=(rn, lt.get("pbf0_rd")) if c == 0 else (rn,),
                    name=f"mqtr{t}_{c}")
                trs.append(o)
            lt["rnbf_rd"] = trs[-1]
            ev = p.op("v", lambda v, tt=tt: v.tensor_copy(
                rnT[:, :].rearrange("p (c k) -> p c k", c=8)[:, :, tt * 128:(tt + 1) * 128],
                pbf[0][:, :].rearrange("p (c k) -> p c k", c=8)),
                after=(trs[-1], lt.get("rnT_rd")), name=f"mqev{t}")
            lt["pbf0_rd"] = ev
            rn_done.append(ev)
        for e in range(8):
            if q == 0 and e == 0:
                s1i = load_w(0, 1, None)
                s2i = load_w(0, 2, None)
            else:
                s1i = (2 * e) % 3
                s2i = (2 * e + 1) % 3
            h_ops = []
            y_ops_all = []
            for half in range(NHALF):
                hev = []
                for fc in range(NFC):
                    bank = PS[fc % 2]

                    def mk_h(bank=bank, fc=fc, half=half, s1i=s1i):
                        def emit(t_):
                            r = None
                            for c in range(8):
                                r = t_.matmul(
                                    bank[:, :],
                                    wb[s1i][:, c * F_ + fc * 128:c * F_ + fc * 128 + 128],
                                    rnT[:, c * QW + half * 512:c * QW + half * 512 + 512],
                                    start=(c == 0), stop=(c == 7))
                            return r
                        return emit
                    dep = [rn_done[-1], wload[(e, 1)], lt.get(("hps", fc % 2))]
                    if half == 0 and fc == 0:
                        dep.append(lt.get("y_done"))
                    mm = p.op("t", mk_h(), after=tuple(dep),
                              name=f"mh{q}_{e}_{half}_{fc}")
                    h_ops.append(mm)
                    ge = p.op("s", lambda s, bank=bank, fc=fc: s.activation(
                        hT[:, fc * 512:(fc + 1) * 512], bank[:, :], AF.Gelu,
                        bias=0.0, scale=1.0),
                        after=(mm, lt.get(("hT_rd", fc))),
                        name=f"mg{q}_{e}_{half}_{fc}")
                    lt[("hps", fc % 2)] = ge
                    hev.append(ge)
                for sub in range(4):
                    t_glob = q * TPQ + half * 4 + sub
                    for col in range(2):
                        ybank = PS[2 + ((sub * 2 + col) % 4)]

                        def mk_y(ybank=ybank, sub=sub, col=col, s2i=s2i):
                            def emit(t_):
                                r = None
                                for fc in range(NFC):
                                    r = t_.matmul(
                                        ybank[:, :],
                                        hT[:, fc * 512 + sub * 128:fc * 512 + sub * 128 + 128],
                                        wb[s2i][:, fc * D + col * 512:fc * D + col * 512 + 512],
                                        start=(fc == 0), stop=(fc == NFC - 1))
                                    pass
                                return r
                            return emit
                        dep = [hev[-1], wload[(e, 2)],
                               lt.get(("yps", (sub * 2 + col) % 4))]
                        my = p.op("t", mk_y(), after=tuple(dep),
                                  name=f"my{q}_{e}_{half}_{sub}_{col}")
                        y_ops_all.append(my)
                        tb = tmp_y[(sub * 2 + col) % 2]
                        ev1 = p.op("s", lambda s, ybank=ybank, tb=tb, t_glob=t_glob,
                                   e=e: s.activation(
                                       tb[:, :], ybank[:, :], AF.Identity,
                                       bias=0.0,
                                       scale=comb[:, t_glob * 8 + e:t_glob * 8 + e + 1]),
                                   after=(my, lt.get(("tmp_rd", (sub * 2 + col) % 2))),
                                   name=f"mye{q}_{e}_{half}_{sub}_{col}")
                        lt[("yps", (sub * 2 + col) % 4)] = ev1
                        add = p.op("v", lambda v, tb=tb, half=half, sub=sub,
                                   col=col: v.tensor_tensor(
                                       acc[:, (half * 4 + sub) * D + col * 512:
                                           (half * 4 + sub) * D + col * 512 + 512],
                                       acc[:, (half * 4 + sub) * D + col * 512:
                                           (half * 4 + sub) * D + col * 512 + 512],
                                       tb[:, :], ALU.add),
                                   after=(ev1,
                                          lt.get(("accadd", half * 4 + sub, col)),
                                          lt.get(("macc_i", half * 4 + sub))),
                                   name=f"mya{q}_{e}_{half}_{sub}_{col}")
                        lt[("tmp_rd", (sub * 2 + col) % 2)] = add
                        lt[("accadd", half * 4 + sub, col)] = add
                for fc in range(NFC):
                    lt[("hT_rd", fc)] = y_ops_all[-1]
            lt["y_done"] = y_ops_all[-1]
            lt["rnT_rd_cand"] = h_ops[-1]
            wslot_user[s1i] = h_ops[-1]
            wslot_user[s2i] = y_ops_all[-1]
            if e < 7:
                load_w(e + 1, 1, None)
                load_w(e + 1, 2, None)
            elif q < NQ - 1:
                load_w(0, 1, None)
                load_w(0, 2, None)
        lt["rnT_rd"] = lt["rnT_rd_cand"]
        for tt in range(TPQ):
            t = q * TPQ + tt
            st = p.dma("q", dst[t * 128:(t + 1) * 128, :],
                       acc[:, tt * D:(tt + 1) * D],
                       after=(lt.get(("accadd", tt, 0)), lt.get(("accadd", tt, 1))),
                       name=f"mst{t}", group=f"mst{tt % 2}")
            lt[("acc_rd", tt)] = st
            stores.append(st)
    ar.pop()
    return stores


def stage_l1b(p, ar, G, src, dst, barrier):
    PS = G["psum_f32"]
    pbf = G["psum_bf"]
    ar.push()
    rstd, negmr, stat_done = ln_stats_pass(p, ar, G, src)
    wq = ar.alloc(8 * D, BF16, "bwq")
    wdn = ar.alloc(8 * 256, BF16, "bwdn")
    wup = ar.alloc(2 * 2048, BF16, "bwup")
    wout = ar.alloc(16 * D, BF16, "bwout")
    bq_r = ar.alloc(D, BF16, "bbq")
    bdn_r = ar.alloc(256, BF16, "bbdn")
    bup_r = ar.alloc(2048, BF16, "bbup")
    bout_r = ar.alloc(D, BF16, "bbout")
    ones_b = ar.alloc(128, BF16, "bones")
    mask_sb = ar.alloc(128, F32, "bmask")
    xb = [ar.alloc(D, F32, f"bxb{i}") for i in range(2)]
    xrb = [ar.alloc(D, F32, f"bxr{i}") for i in range(2)]
    xn_buf = ar.alloc(D, BF16, "bxn")
    xnT = ar.alloc(8 * 128, BF16, "bxnT")
    q_raw = ar.alloc(D, BF16, "bqraw")
    q_rope = ar.alloc(D, BF16, "bqrope")
    k_sep = ar.alloc(D, BF16, "bksep")
    k_rope = ar.alloc(D, BF16, "bkrope")
    v_nat = ar.alloc(D, BF16, "bvnat")
    lat_bf = ar.alloc(256, BF16, "blat")
    latT = ar.alloc(256, BF16, "blatT")
    tmp1 = ar.alloc(512, BF16, "btmp1")
    tmp2 = ar.alloc(512, BF16, "btmp2")
    crep_t = [ar.alloc(512, BF16, f"bcr{i}") for i in range(2)]
    srep_t = [ar.alloc(512, BF16, f"bsr{i}") for i in range(2)]
    Q2 = [ar.alloc(D, BF16, f"bQ2{i}") for i in range(2)]
    K2 = [ar.alloc(D, BF16, f"bK2{i}") for i in range(2)]
    V2 = [ar.alloc(D, BF16, f"bV2{i}") for i in range(2)]
    Q2T = ar.alloc(8 * 128, BF16, "bQ2T")
    K2T = ar.alloc(8 * 128, BF16, "bK2T")
    smsk = ar.alloc(128, F32, "bsmsk")
    att_b = ar.alloc(128, BF16, "batt")
    attn_b = [ar.alloc(128, BF16, f"batn{i}") for i in range(2)]
    attT_b = [ar.alloc(128, BF16, f"batT{i}") for i in range(2)]
    rsum = ar.alloc(2, F32, "brs")
    o2Ty = ar.alloc(16 * 128, BF16, "bo2T")
    xo_buf = [ar.alloc(D, F32, f"bxo{i}") for i in range(2)]

    ones_set = p.op("g", lambda g_: g_.memset(ones_b[:, :], 1.0), name="bON")
    mld = p.dma("q", mask_sb, G["tbl_l1bmask"], name="bMSK", group="const")

    for c in range(8):
        p.dma("g", wq[:, c * D:(c + 1) * D],
              G["L1_attn_b_Wq"][c * 128:(c + 1) * 128, :],
              name=f"bwq{c}", group="bw")
    wq_ready = p.dma("g", bq_r[0:1, :],
                     G["L1_attn_b_bq"].rearrange("(a c) -> a c", a=1),
                     name="bbq", group="bw")
    for c in range(8):
        p.dma("g", wdn[:, c * 256:(c + 1) * 256],
              G["L1_attn_b_Wdown"][c * 128:(c + 1) * 128, :],
              name=f"bwd{c}", group="bw")
    wdn_ready = p.dma("g", bdn_r[0:1, :],
                      G["L1_attn_b_bdown"].rearrange("(a c) -> a c", a=1),
                      name="bbd", group="bw")
    for c in range(2):
        p.dma("g", wup[:, c * 2048:(c + 1) * 2048],
              G["L1_attn_b_Wup"][c * 128:(c + 1) * 128, :],
              name=f"bwu{c}", group="bw")
    wup_ready = p.dma("g", bup_r[0:1, :],
                      G["L1_attn_b_bup"].rearrange("(a c) -> a c", a=1),
                      name="bbu", group="bw")
    for h in range(16):
        p.dma("g", wout[0:64, h * D:(h + 1) * D],
              G["L1_attn_b_Wout"][h * 64:(h + 1) * 64, :],
              name=f"bwo{h}", group="bw")
    w_ready = p.dma("g", bout_r[0:1, :],
                    G["L1_attn_b_bout"].rearrange("(a c) -> a c", a=1),
                    name="bbo", group="bw")
    htmp = G["l1b_tmp"]

    lt = {}
    stores = []

    def phase1(t):
        b = t % 2
        ld = p.dma("q", xb[b], src[t * 128:(t + 1) * 128, :],
                   after=(stat_done, lt.get(("xb", b))), name=f"bld{t}",
                   group=f"bxb{b}")
        ldc = p.dma("q", crep_t[b], G["tbl_crep"][t * 128:(t + 1) * 128, :],
                    after=(lt.get(("rope", b)),), name=f"bcrl{t}",
                    group=f"bcr{b}")
        lds = p.dma("q", srep_t[b], G["tbl_srep"][t * 128:(t + 1) * 128, :],
                    after=(lt.get(("rope", b)),), name=f"bsrl{t}",
                    group=f"bcr{b}")
        xo = p.op("s", lambda s, b=b, t=t: s.activation(
            xn_buf[:, :], xb[b][:, :], AF.Identity,
            bias=negmr[:, t:t + 1], scale=rstd[:, t:t + 1]),
            after=(ld, stat_done, lt.get("xn_tr")), name=f"bxn{t}")
        lt[("xb", b)] = xo
        trs = []
        for c in range(8):
            o = p.op("t", lambda t_, c=c: t_.transpose(
                pbf[0][:, c * 128:(c + 1) * 128],
                xn_buf[:, c * 128:(c + 1) * 128], G["ident"]),
                after=(xo, lt.get("xnT_ev"), lt.get(("atps", 0)),
                       lt.get(("atps", 1)), G["setup"]) if c == 0 else (xo,),
                name=f"btr{t}_{c}")
            trs.append(o)
        lt["xn_tr"] = trs[-1]
        ev = p.op("v", lambda v: v.tensor_copy(
            xnT[:, :].rearrange("p (c k) -> p c k", c=8),
            pbf[0][:, :].rearrange("p (c k) -> p c k", c=8)),
            after=(trs[-1], lt.get("xnT_rd")), name=f"btrev{t}")
        lt["xnT_ev"] = ev

        qmm = []
        for colh in range(2):
            bank = PS[colh]

            def mk_q(colh=colh, bank=bank):
                def emit(t_):
                    r = None
                    wv_ = wq[:, :].rearrange("p (c h half fr) -> p c half h fr",
                                             c=8, half=2, h=16, fr=32)
                    for c in range(8):
                        r = t_.matmul(bank[:, :], xnT[:, c * 128:(c + 1) * 128],
                                      wv_[:, c, colh], start=(c == 0),
                                      stop=False)
                    r = t_.matmul(bank[:, :], ones_b[0:1, 0:128],
                                  bq_r[0:1, :].rearrange(
                                      "a (h half fr) -> a half h fr", half=2,
                                      h=16, fr=32)[:, colh],
                                  start=False, stop=True)
                    return r
                return emit
            mm = p.op("t", mk_q(), after=(ev, wq_ready, ones_set,
                                          lt.get(f"qps{colh}"),
                                          lt.get(("sps", colh))),
                      name=f"bq{t}_{colh}")
            qmm.append(mm)
        evqa = p.op("s", lambda s: s.copy(q_raw[:, 0:512], PS[0][:, :]),
                    after=(qmm[0], lt.get("qraw_rd")), name=f"bevq{t}a")
        evqb = p.op("s", lambda s: s.copy(q_raw[:, 512:1024], PS[1][:, :]),
                    after=(qmm[1],), name=f"bevq{t}b")
        lt["qps0"] = evqa
        lt["qps1"] = evqb

        def mk_lat():
            def emit(t_):
                r = None
                for c in range(8):
                    r = t_.matmul(PS[2][:, 0:256],
                                  xnT[:, c * 128:(c + 1) * 128],
                                  wdn[:, c * 256:(c + 1) * 256],
                                  start=(c == 0), stop=False)
                r = t_.matmul(PS[2][:, 0:256], ones_b[0:1, 0:128],
                              bdn_r[0:1, :], start=False, stop=True)
                return r
            return emit
        lmm = p.op("t", mk_lat(), after=(ev, wdn_ready, lt.get("latps"),
                                         lt.get(("sps", 2))),
                   name=f"blat{t}")
        evl = p.op("s", lambda s: s.copy(lat_bf[:, :], PS[2][:, 0:256]),
                   after=(lmm, lt.get("lat_rd")), name=f"bevl{t}")
        lt["latps"] = evl
        lt["xnT_rd"] = lmm
        ltr = []
        for c in range(2):
            o = p.op("t", lambda t_, c=c: t_.transpose(
                pbf[1][:, c * 128:(c + 1) * 128],
                lat_bf[:, c * 128:(c + 1) * 128], G["ident"]),
                after=(evl, lt.get("pbf1_rd")) if c == 0 else (evl,),
                name=f"bltr{t}_{c}")
            ltr.append(o)
        lt["lat_rd"] = ltr[-1]
        evlT = p.op("v", lambda v: v.tensor_copy(latT[:, :], pbf[1][:, 0:256]),
                    after=(ltr[-1], lt.get("latT_rd")), name=f"bevlt{t}")
        lt["pbf1_rd"] = evlT

        kv_specs = [("k1", 0, 32, PS[2], slice(0, 16)),
                    ("k2", 32, 32, PS[3], slice(0, 16)),
                    ("va", 64, 64, PS[4], slice(0, 8)),
                    ("vb", 64, 64, PS[5], slice(8, 16))]
        kv_mm = []
        for si, (nm, joff, jw, bank, hsl) in enumerate(kv_specs):
            def mk_kv(nm=nm, joff=joff, jw=jw, bank=bank, hsl=hsl):
                def emit(t_):
                    r = None
                    for c in range(2):
                        wv_ = wup[:, c * 2048:(c + 1) * 2048].rearrange(
                            "p (h j) -> p h j", h=16)[:, hsl, joff:joff + jw]
                        r = t_.matmul(bank[:, :],
                                      latT[:, c * 128:(c + 1) * 128], wv_,
                                      start=(c == 0), stop=False)
                    bv = bup_r[0:1, :].rearrange("a (h j) -> a h j",
                                                 h=16)[:, hsl, joff:joff + jw]
                    r = t_.matmul(bank[:, :], ones_b[0:1, 0:128], bv,
                                  start=False, stop=True)
                    return r
                return emit
            mm = p.op("t", mk_kv(), after=(evlT, wup_ready,
                                           lt.get(f"kvps{si}"),
                                           (None, lt.get(("sps", 3)),
                                            lt.get(("avps", 0)),
                                            lt.get(("avps", 1)))[si]),
                      name=f"bkv{t}_{nm}")
            kv_mm.append(mm)
        evk1 = p.op("s", lambda s: s.copy(k_sep[:, 0:512], PS[2][:, :]),
                    after=(kv_mm[0], lt.get("ksep_rd")), name=f"bevk{t}a")
        evk2 = p.op("s", lambda s: s.copy(k_sep[:, 512:1024], PS[3][:, :]),
                    after=(kv_mm[1],), name=f"bevk{t}b")
        evv1 = p.op("s", lambda s: s.copy(v_nat[:, 0:512], PS[4][:, :]),
                    after=(kv_mm[2], lt.get("vnat_rd")), name=f"bevv{t}a")
        evv2 = p.op("s", lambda s: s.copy(v_nat[:, 512:1024], PS[5][:, :]),
                    after=(kv_mm[3],), name=f"bevv{t}b")
        lt["kvps0"] = evk1
        lt["kvps1"] = evk2
        lt["kvps2"] = evv1
        lt["kvps3"] = evv2

        def rope16(raw, out_buf, evd, name, b=b, ldc=ldc, lds=lds):
            cr, sr = crep_t[b], srep_t[b]
            o1 = p.op("v", lambda v, cr=cr: v.tensor_tensor(
                tmp1[:, :], raw[:, 0:512], cr[:, :], ALU.mult),
                after=(ldc, lds, evd, lt.get("rope_tmp")), name=name + "a")
            o2 = p.op("v", lambda v, sr=sr: v.tensor_tensor(
                tmp2[:, :], raw[:, 512:1024], sr[:, :], ALU.mult),
                name=name + "b")
            rv = out_buf[:, :].rearrange("p (h half fr) -> p half h fr",
                                         half=2, h=16, fr=32)
            o3 = p.op("v", lambda v, rv=rv: v.tensor_tensor(
                rv[:, 0], tmp1[:, :].rearrange("p (h fr) -> p h fr", h=16),
                tmp2[:, :].rearrange("p (h fr) -> p h fr", h=16),
                ALU.subtract), after=(o1, o2, lt.get(("rope_dst", name[:3]))),
                name=name + "c")
            o4 = p.op("v", lambda v, sr=sr: v.tensor_tensor(
                tmp1[:, :], raw[:, 0:512], sr[:, :], ALU.mult),
                after=(o3,), name=name + "d")
            o5 = p.op("v", lambda v, cr=cr: v.tensor_tensor(
                tmp2[:, :], raw[:, 512:1024], cr[:, :], ALU.mult),
                after=(o3,), name=name + "e")
            o6 = p.op("v", lambda v, rv=rv: v.tensor_tensor(
                rv[:, 1], tmp1[:, :].rearrange("p (h fr) -> p h fr", h=16),
                tmp2[:, :].rearrange("p (h fr) -> p h fr", h=16),
                ALU.add), after=(o4, o5), name=name + "f")
            lt["rope_tmp"] = o6
            return o6
        rq = rope16(q_raw, q_rope, evqb, f"brq{t}")
        lt["qraw_rd"] = rq
        rk = rope16(k_sep, k_rope, evk2, f"brk{t}")
        lt["ksep_rd"] = rk
        lt[("rope", b)] = rk

        hb = t % 2
        d3q = p.dma("q", htmp[0, hb], q_rope[:, :],
                    after=(rq, lt.get(("h4d", 0, hb))), name=f"bh3q{t}",
                    group=f"bhq{hb}")
        d3k = p.dma("s", htmp[1, hb], k_rope[:, :],
                    after=(rk, lt.get(("h4d", 1, hb))), name=f"bh3k{t}",
                    group=f"bhk{hb}")
        d3v = p.dma("g", htmp[2, hb], v_nat[:, :],
                    after=(evv1, evv2, lt.get(("h4d", 2, hb))),
                    name=f"bh3v{t}", group=f"bhv{hb}")
        lt[("rope_dst", "brq")] = d3q
        lt[("rope_dst", "brk")] = d3k
        lt["vnat_rd"] = d3v
        d4 = []
        for ti, (buf, grp, d3) in enumerate(((Q2, "q", d3q), (K2, "k", d3k),
                                             (V2, "v", d3v))):
            iview = htmp[ti, hb].rearrange("(sub tl) (h dd) -> h tl sub dd",
                                           sub=16, h=16)
            o = None
            tq = {"q": "q", "k": "s", "v": "g"}[grp]
            for h in range(16):
                o = p.dma(tq, buf[hb][h * 8:(h + 1) * 8, :].rearrange(
                    "p (sub dd) -> p sub dd", sub=16),
                    iview[h],
                    after=(d3, lt.get(("i2_rd", ti, hb))) if h == 0 else (d3,),
                    name=f"bh4{grp}{t}_{h}", group=f"bi{grp}{hb}")
            d4.append(o)
            lt[("h4d", ti, hb)] = o
        return {"d4": d4, "hb": hb, "b": b}

    def phase2(t, cc):
        d4, hb, b = cc["d4"], cc["hb"], cc["b"]
        tev = {}
        for ti, (buf, tbuf) in enumerate(((Q2, Q2T), (K2, K2T))):
            for batch in range(2):
                trq = []
                for ss in range(8):
                    s = batch * 8 + ss
                    slot = ss // 2
                    base = (ss % 2) * 64
                    o = p.op("t", lambda t_, s=s, slot=slot, base=base,
                             buf=buf, hb=hb: t_.transpose(
                                 pbf[1][base:base + 64,
                                        slot * 128:(slot + 1) * 128],
                                 buf[hb][:, s * 64:(s + 1) * 64],
                                 G["ident"],
                                 tile_position=(0, base) if base else None),
                             after=(d4[ti], lt.get("pbf1_rd")) if ss == 0
                             else (d4[ti],),
                             name=f"bst{t}_{ti}_{s}")
                    trq.append(o)
                e_ev = p.op("v", lambda v, tbuf=tbuf, batch=batch: v.tensor_copy(
                    tbuf[0:64, :].rearrange("p (sh k) -> p sh k", sh=8)[
                        :, batch * 4:batch * 4 + 4],
                    pbf[1][0:64, 0:512].rearrange("p (ss k) -> p ss k", ss=4)),
                    after=(trq[-1], lt.get("q2t_rd")),
                    name=f"bse{t}_{ti}_{batch}")
                o_ev = p.op("v", lambda v, tbuf=tbuf, batch=batch: v.tensor_copy(
                    tbuf[64:128, :].rearrange("p (sh k) -> p sh k", sh=8)[
                        :, batch * 4:batch * 4 + 4],
                    pbf[1][64:128, 0:512].rearrange("p (ss k) -> p ss k", ss=4)),
                    after=(trq[-1],), name=f"bso{t}_{ti}_{batch}")
                lt["pbf1_rd"] = o_ev
                tev[(ti, batch)] = o_ev

        avd = {}
        for sh in range(8):
            pA = PS[(sh % 2) * 2]
            pB = PS[(sh % 2) * 2 + 1]
            mmE = p.op("t", lambda t_, sh=sh, pA=pA: t_.matmul(
                pA[:, 0:128], Q2T[0:64, sh * 128:(sh + 1) * 128],
                K2T[0:64, sh * 128:(sh + 1) * 128], start=True, stop=True,
                tile_position=(0, 0)),
                after=(tev[(0, sh // 4)], tev[(1, sh // 4)],
                       lt.get(("sps", (sh % 2) * 2)), lt.get("kvps0"),
                       lt.get("kvps1"), lt.get("latps"), lt.get("qps0"),
                       lt.get("qps1")),
                name=f"bsc{t}_{sh}e")
            mmO = p.op("t", lambda t_, sh=sh, pB=pB: t_.matmul(
                pB[:, 0:128], Q2T[64:128, sh * 128:(sh + 1) * 128],
                K2T[64:128, sh * 128:(sh + 1) * 128], start=True, stop=True,
                tile_position=(64, 0)),
                after=(lt.get(("sps", (sh % 2) * 2 + 1)),
                       lt.get("qps1") if sh % 2 == 0 else lt.get("kvps1")),
                name=f"bsc{t}_{sh}o")
            for par, (mm, bank) in enumerate(((mmE, pA), (mmO, pB))):
                s = sh * 2 + par
                mk = p.op("v", lambda v, bank=bank: v.tensor_tensor(
                    smsk[:, :], bank[:, 0:128], mask_sb[:, :], ALU.add),
                    after=(mm, mld, lt.get("smsk_rd")), name=f"bma{t}_{s}")
                lt[("sps", (sh % 2) * 2 + par)] = mk
                ex = p.op("s", lambda s_: s_.activation(
                    att_b[:, :], smsk[:, :], AF.Exp, bias=0.0, scale=0.125,
                    accum_out=rsum[:, 0:1]),
                    after=(mk, lt.get("att_rd"), lt.get("rs_rd")),
                    name=f"bex{t}_{s}")
                lt["smsk_rd"] = ex
                rc = p.op("v", lambda v: v.reciprocal(rsum[:, 1:2],
                                                      rsum[:, 0:1]),
                          after=(ex, lt.get("an_prev")), name=f"brc{t}_{s}")
                lt["rs_rd"] = rc
                an = p.op("v", lambda v, s=s: v.tensor_scalar(
                    attn_b[s % 2][:, :], att_b[:, :], rsum[:, 1:2], None,
                    ALU.mult), after=(rc, lt.get(("attn_rd", s % 2))),
                    name=f"ban{t}_{s}")
                lt["att_rd"] = an
                lt["an_prev"] = an
                atr = p.op("t", lambda t_, s=s: t_.transpose(
                    pbf[0][:, (s % 2) * 128:(s % 2) * 128 + 128],
                    attn_b[s % 2][:, :], G["ident"]),
                    after=(an, lt.get(("atps", s % 2)), lt.get("xnT_ev")),
                    name=f"batr{t}_{s}")
                aev = p.op("v", lambda v, s=s: v.tensor_copy(
                    attT_b[s % 2][:, :],
                    pbf[0][:, (s % 2) * 128:(s % 2) * 128 + 128]),
                    after=(atr, lt.get(("attT_rd", s % 2))),
                    name=f"bae{t}_{s}")
                lt[("atps", s % 2)] = aev
                lt[("attn_rd", s % 2)] = aev
                av = p.op("t", lambda t_, s=s, hb=hb: t_.matmul(
                    PS[4 + s % 2][0:64, 0:128], V2[hb][:, s * 64:(s + 1) * 64],
                    attT_b[s % 2][:, :], start=True, stop=True),
                    after=(aev, lt.get(("avps", s % 2)), lt.get("kvps2"),
                           lt.get("kvps3"), lt.get(("h4d", 2, hb))),
                    name=f"bav{t}_{s}")
                lt[("attT_rd", s % 2)] = av
                oev = p.op("s", lambda s_, s=s: s_.copy(
                    o2Ty[0:64, :].rearrange("p (h sub tl) -> p h sub tl",
                                            h=16, sub=16)[:, :, s],
                    PS[4 + s % 2][0:64, 0:128].rearrange("p (h tl) -> p h tl",
                                                         h=16)),
                    after=(av, lt.get(("o2T_rd",)) if s == 0 else None),
                    name=f"boe{t}_{s}")
                lt[("avps", s % 2)] = oev
                avd[s] = oev
        av_all = [avd[15], avd[14]]

        ldr = p.dma("q", xrb[b], src[t * 128:(t + 1) * 128, :],
                    after=(lt.get(("xr", b)),), name=f"brl{t}", group=f"bxr{b}")
        res_ops = []
        for col in range(2):
            bank = PS[2 + col]

            def mk_op(bank=bank, col=col):
                def emit(t_):
                    r = None
                    for h in range(16):
                        r = t_.matmul(
                            bank[:, :], o2Ty[0:64, h * 128:(h + 1) * 128],
                            wout[0:64, h * D + col * 512:h * D + col * 512 + 512],
                            start=(h == 0), stop=False)
                    r = t_.matmul(bank[:, :], ones_b[0:1, 0:128],
                                  bout_r[0:1, col * 512:(col + 1) * 512],
                                  start=False, stop=True)
                    return r
                return emit
            dep = [avd[15], avd[14], w_ready, lt.get(("ops", col)),
                   lt.get(("sps", 2 + col)),
                   lt.get("latps") if col == 0 else lt.get("kvps1"),
                   lt.get("kvps0") if col == 0 else None]
            mo = p.op("t", mk_op(), after=tuple(dep), name=f"bop{t}_{col}")
            ro = p.op("v", lambda v, col=col, b=b, bank=bank: 
                      v.scalar_tensor_tensor(
                          xo_buf[b][:, col * 512:(col + 1) * 512], bank[:, :],
                          1.0, xrb[b][:, col * 512:(col + 1) * 512],
                          ALU.mult, ALU.add),
                      after=(mo, ldr, lt.get(("xo_st", b))),
                      name=f"bro{t}_{col}")
            lt[("ops", col)] = ro
            lt[("sps", 2 + col)] = ro
            res_ops.append(ro)
        lt[("o2T_rd",)] = res_ops[-1]
        lt[("xr", b)] = res_ops[-1]
        lt[("i2_rd", 0, hb)] = tev[(0, 1)]
        lt[("i2_rd", 1, hb)] = tev[(1, 1)]
        lt[("i2_rd", 2, hb)] = avd[15]
        lt["q2t_rd"] = avd[15]
        if G.get("dbg_l1b") and t == 0:
            p.dma("q", G["dbg_gnw"].bitcast(BF16)[:, 0:1024], q_rope[:, :], after=(rq,), name="dbq",
                  group="dbg")
            p.dma("q", G["dbg_gnb"].bitcast(BF16)[:, 0:1024], k_rope[:, :], after=(rk,), name="dbk",
                  group="dbg")
            p.dma("q", G["dbg_gate"][:, 0:1024], Q2[hb][:, :],
                  after=(lt[("h4d", 0, hb)],), name="dbQ2", group="dbg")
            p.dma("q", G["dbg_gate"][:, 1024:2048], V2[hb][:, :],
                  after=(lt[("h4d", 2, hb)],), name="dbV2", group="dbg")
            p.dma("q", G["dbg_yT"][0:64, 0:2048], o2Ty[0:64, :], after=(avd[15],),
                  name="dbo2", group="dbg")
            p.barrier("dbgl1b")
        st = p.dma("q", dst[t * 128:(t + 1) * 128, :], xo_buf[b],
                   after=tuple(res_ops), name=f"bst{t}", group=f"bso{b}")
        lt[("xo_st", b)] = st
        stores.append(st)

    prev_cc = None
    for t in range(NT):
        cc = phase1(t)
        if prev_cc is not None:
            phase2(t - 1, prev_cc)
        prev_cc = cc
    phase2(NT - 1, prev_cc)
    ar.pop()
    return stores


def stage_prep(p, ar, G, names):
    NRAW, NBFB, NEVB = 4, 8, 4
    PSTRIP = 512
    LDQ = ["q", "s", "g", "q"]
    STQ = ["g", "g", "s", "s"]
    raw = []
    for i in range(NRAW):
        ar.off = (ar.off + 31) & ~31
        st_ = ar.off
        ar.off += PSTRIP * 4
        ar.peak = max(ar.peak, ar.off)
        assert ar.off <= SBUF_BYTES, f"SBUF overflow at praw{i}"
        sl = ar.ap[:, st_:st_ + PSTRIP * 4]
        raw.append((sl.bitcast(F32), sl.bitcast(I32)))
    bfb = [ar.alloc(PSTRIP, BF16, f"pbf{i}") for i in range(NBFB)]
    evb = [ar.alloc(PSTRIP, BF16, f"pev{i}") for i in range(NEVB)]
    pbk = [G["psum_bf"][0], G["psum_bf"][1]]
    state = {"ext": {0: None, 1: None}, "lcur": 0, "tcur": 0}
    prev = {}

    strips = []

    def add_mat(src_ap, dst_ap, rows, cols, is_int):
        for r0 in range(0, rows, 128):
            rh = min(128, rows - r0)
            for c0 in range(0, cols, PSTRIP):
                cw = min(PSTRIP, cols - c0)
                strips.append((src_ap, dst_ap, r0, c0, rh, cw, is_int))

    order = [n for n in names if n.startswith("ffn0")] + \
            [n for n in names if not n.startswith("ffn0")]
    for nm in order:
        if nm == "moe_w":
            for e in range(8):
                add_mat(G["L1_ffn_w1"][e], G["moe_w1T"][e], 2048, 1024, False)
                add_mat(G["L1_ffn_w2"][e], G["moe_w2T"][e], 1024, 2048, False)
        else:
            srcn, (rows, cols), is_int = PREP_TRANSPOSE[nm]
            add_mat(G[srcn], G[nm], rows, cols, is_int)
    state["n"] = len(strips)

    def emit_load(k):
        src_ap, dst_ap, r0, c0, rh, cw, is_int = strips[k]
        i = k % NRAW
        bslot = k % NBFB
        rb = raw[i][1] if is_int else raw[i][0]
        ld = p.dma(LDQ[i], rb[0:rh, 0:cw],
                   src_ap[r0:r0 + rh, c0:c0 + cw],
                   after=(prev.get(("cast", i)),), name="pld",
                   group=f"prepld{i}")
        ceng = "v" if (is_int or k % 2 == 0) else "s"
        if ceng == "v":
            cv = p.op("v", lambda v, cw=cw, bslot=bslot, rh=rh, rb=rb:
                      v.tensor_copy(bfb[bslot][0:rh, 0:cw],
                                    rb[0:rh, 0:cw]),
                      after=(ld, prev.get(("tr", bslot))), name="pcv")
        else:
            cv = p.op("s", lambda s, cw=cw, bslot=bslot, rh=rh, rb=rb:
                      s.copy(bfb[bslot][0:rh, 0:cw], rb[0:rh, 0:cw]),
                      after=(ld, prev.get(("tr", bslot))), name="pcv")
        prev[("cast", i)] = cv
        prev[("castb", bslot)] = cv

    def emit_tr(k):
        src_ap, dst_ap, r0, c0, rh, cw, is_int = strips[k]
        bslot = k % NBFB
        e = k % NEVB
        bk = k % 2
        pb = pbk[bk]
        nblk = cw // 128
        cv = prev[("castb", bslot)]
        trs = []
        for bi in range(nblk):
            o = p.op("t", lambda t_, bi=bi, bslot=bslot, rh=rh, pb=pb:
                     t_.transpose(
                pb[:, bi * rh:(bi + 1) * rh],
                bfb[bslot][0:rh, bi * 128:(bi + 1) * 128],
                G["ident"][0:rh, 0:rh]),
                after=(cv, prev.get(("evb", bk)), state["ext"][bk],
                       G["setup"]) if bi == 0
                else (cv,), name="ptr")
            trs.append(o)
        prev[("tr", bslot)] = trs[-1]
        if k % 2 == 0:
            ev = p.op("v", lambda v, e=e, rh=rh, nblk=nblk, pb=pb:
                      v.tensor_copy(evb[e][:, 0:nblk * rh],
                                    pb[:, 0:nblk * rh]),
                      after=(trs[-1], prev.get(("st", e))), name="pev")
        else:
            ev = p.op("s", lambda s, e=e, rh=rh, nblk=nblk, pb=pb:
                      s.copy(evb[e][:, 0:nblk * rh],
                             pb[:, 0:nblk * rh]),
                      after=(trs[-1], prev.get(("st", e))), name="pev")
        prev[("evb", bk)] = ev
        state[("evbank", bk)] = ev
        state["last_ev"] = ev
        st = None
        for bi in range(nblk):
            st = p.dma(STQ[e], dst_ap[c0 + bi * 128:c0 + (bi + 1) * 128,
                                      r0:r0 + rh],
                       evb[e][:, bi * rh:(bi + 1) * rh], after=(ev,),
                       name="pst", group=f"prepst{e}")
        prev[("st", e)] = st
        state["last"] = st

    state["emit_load"] = emit_load
    state["emit_tr"] = emit_tr
    return state


def build_program(stages):
    nc = bass.Bass("TRN2", target_bir_lowering=False)
    es = ExitStack()
    G = {}
    G["x0"] = nc.dram_tensor("x0", [TOK, D], F32, kind="ExternalInput").ap()
    G["y"] = nc.dram_tensor("y", [TOK, D], F32, kind="ExternalOutput").ap()
    G["xS1"] = nc.dram_tensor("xS1", [TOK, D], F32, kind="Internal").ap()
    G["xS2"] = nc.dram_tensor("xS2", [TOK, D], F32, kind="Internal").ap()
    G["l1b_tmp"] = nc.dram_tensor("l1b_tmp", [3, 2, 128, 1024], BF16,
                                  kind="Internal").ap()
    G["dbg_stats"] = nc.dram_tensor("dbg_stats", [128, 64], F32,
                                    kind="ExternalOutput").ap()
    G["dbg_yT"] = nc.dram_tensor("dbg_yT", [128, 8 * 512], BF16,
                                 kind="ExternalOutput").ap()
    G["dbg_qT"] = nc.dram_tensor("dbg_qT", [128, 4 * 512], BF16,
                                 kind="ExternalOutput").ap()
    G["dbg_kT"] = nc.dram_tensor("dbg_kT", [128, 4, 512], BF16,
                                 kind="ExternalOutput").ap()
    G["dbg_v"] = nc.dram_tensor("dbg_v", [128, 4 * 8 * 65], BF16,
                                 kind="ExternalOutput").ap()
    G["dbg_qraw"] = nc.dram_tensor("dbg_qraw", [128, 512], BF16,
                                   kind="ExternalOutput").ap()
    G["dbg_qrope"] = nc.dram_tensor("dbg_qrope", [128, 512], BF16,
                                    kind="ExternalOutput").ap()
    G["dbg_gate"] = nc.dram_tensor("dbg_gate", [128, 2048], BF16,
                                   kind="ExternalOutput").ap()
    G["dbg_gnw"] = nc.dram_tensor("dbg_gnw", [128, 1024], F32,
                                  kind="ExternalOutput").ap()
    G["dbg_gnb"] = nc.dram_tensor("dbg_gnb", [128, 1024], F32,
                                  kind="ExternalOutput").ap()
    in_names = set()

    def ext_in(nm):
        if nm not in G:
            dt = I32 if INPUT_DTYPES[nm] == "int32" else F32
            G[nm] = nc.dram_tensor(nm, list(INPUT_SHAPES[nm]), dt,
                                   kind="ExternalInput").ap()
            in_names.add(nm)

    need_preps = []
    tbl_names = {"ident"}
    for st in stages:
        for nm in RAW_INPUTS[st]:
            ext_in(nm)
        tbl_names.update(STAGE_TABLES[st])
        for nm in STAGE_PREPS[st]:
            if nm not in need_preps:
                need_preps.append(nm)
    for nm in tbl_names:
        arr = TABLES[nm]
        dt = BF16 if arr.dtype == np.uint16 else F32
        G["tbl_" + nm] = nc.dram_tensor("tbl_" + nm, list(arr.shape), dt,
                                        kind="ExternalInput").ap()
        in_names.add("tbl_" + nm)
    for nm in need_preps:
        if nm == "moe_w":
            G["moe_w1T"] = nc.dram_tensor("moe_w1T", [8, D, 2048], BF16,
                                          kind="Internal").ap()
            G["moe_w2T"] = nc.dram_tensor("moe_w2T", [8, 2048, D], BF16,
                                          kind="Internal").ap()
            ext_in("L1_ffn_w1")
            ext_in("L1_ffn_w2")
        else:
            srcn, (rows, cols), is_int = PREP_TRANSPOSE[nm]
            G[nm] = nc.dram_tensor("prep_" + nm, [cols, rows], BF16,
                                   kind="Internal").ap()
            ext_in(srcn)

    arena_t = nc.alloc_sbuf_tensor("arena", [128, SBUF_BYTES], mybir.dt.uint8)
    ar = Arena(arena_t.ap())
    G["psum_d"] = [es.enter_context(nc.psum_tensor(f"pd{i}", [128, 1024], F32))
                   for i in range(3)]
    G["psum_f32"] = [G["psum_d"][i // 2][:, (i % 2) * 512:(i % 2) * 512 + 512]
                     for i in range(6)]
    G["psum_bf"] = [es.enter_context(nc.psum_tensor(f"pbx{i}", [128, 1024], BF16))
                    for i in range(2)]

    p = P(nc)
    ident_sb = ar.alloc(128, BF16, "ident")
    barrier = p.dma("q", ident_sb, G["tbl_ident"], name="ldid", group="const")
    G["ident"] = ident_sb
    G["eps"] = ar.alloc(1, F32, "eps")
    G["consts_ready"] = p.op("g", lambda g: g.memset(G["eps"][:, :], EPS),
                             name="memeps")

    G["setup"] = barrier
    prep_mark = ar.off
    prep_state = [None, None]
    if need_preps:
        ps = stage_prep(p, ar, G, need_preps)
        prep_state = [ps, ps]
        W = 8
        for k in range(min(W, ps["n"])):
            ps["emit_load"](k)
        ps["lcur"] = min(W, ps["n"])

        def tick(ext0=None, ext1=None, n=4):
            if ps["tcur"] >= ps["n"]:
                return None
            ps["ext"] = {0: ext0, 1: ext1}
            for _ in range(n):
                if ps["tcur"] >= ps["n"]:
                    break
                ps["emit_tr"](ps["tcur"])
                ps["tcur"] += 1
                if ps["lcur"] < ps["n"]:
                    ps["emit_load"](ps["lcur"])
                    ps["lcur"] += 1
            return (ps.get(("evbank", 0)), ps.get(("evbank", 1)))
        G["prep_tick"] = tick
    p.new_stage()

    cur = G["x0"]
    nxt = ["xS1", "xS2"]
    stores = [barrier]
    for i, st in enumerate(stages):
        out = G["y"] if i == len(stages) - 1 else G[nxt[i % 2]]
        if st in ("ffn0a", "ffn0b"):
            stores = stage_ffn0(p, ar, G, cur, out, barrier)
        elif st == "l0a":
            stores = stage_attn(p, ar, G, {"wsrc": "qkv", "post": "softmax"},
                                cur, out, barrier)
        elif st == "l0b":
            stores = stage_attn(p, ar, G, {"wsrc": "prep", "wq": "l0b_qT",
                                           "wk": "l0b_kT", "wv": "l0b_vT",
                                           "wo": "l0b_oT", "post": "softmax"},
                                cur, out, barrier)
        elif st == "l1a":
            stores = stage_attn(p, ar, G, {"wsrc": "prep", "wq": "l1a_qT",
                                           "wk": "l1a_kT", "wv": "l1a_vT",
                                           "wg": "l1a_gT", "wo": "l1a_oT",
                                           "post": "retention", "gate": True,
                                           "wo_pairs": True},
                                cur, out, barrier)
        elif st in ("moea", "moeb"):
            stores = stage_moe(p, ar, G, cur, out, barrier)
        elif st == "l1b":
            stores = stage_l1b(p, ar, G, cur, out, barrier)
        elif st == "dbg_ln":
            stores = stage_dbg_ln(p, ar, G, cur, out, barrier)
        else:
            raise NotImplementedError(st)
        if i == 0 and prep_state[0] is not None:
            ps = prep_state[0]
            if ps["tcur"] < ps["n"]:
                deps = list(p.last.values())
                j = p.op("q", lambda h: h.nop(), after=tuple(deps),
                         name="prejoin")
                ps["ext"] = {0: j, 1: j}
                while ps["tcur"] < ps["n"]:
                    ps["emit_tr"](ps["tcur"])
                    ps["tcur"] += 1
                    if ps["lcur"] < ps["n"]:
                        ps["emit_load"](ps["lcur"])
                        ps["lcur"] += 1
            G["prep_tick"] = None
        barrier = p.barrier(f"{st}_done")
        p.new_stage()
        ar.off = prep_mark
        cur = out
    p.op("q", lambda q: q.nop(), after=tuple(stores), name="final")
    p.emit_all(es)
    es.close()
    return nc, in_names


STAGE_GROUPS = [list(STAGES)]


def kernel(**inputs):
    xfull = inputs["L0_attn_a_x"]
    shards = [np.ascontiguousarray(np.asarray(xfull)[b]) for b in range(NCORE)]
    host = {k: np.ascontiguousarray(np.asarray(v)) for k, v in inputs.items()
            if k != "L0_attn_a_x"}
    for k, v in TABLES.items():
        host["tbl_" + k] = v
    cur = shards
    for grp in STAGE_GROUPS:
        nc, in_names = build_program(grp)
        in_maps = []
        for b in range(NCORE):
            m = {"x0": cur[b]}
            for nm in in_names:
                m[nm] = host[nm]
            in_maps.append(m)
        res = run_bass_kernel_spmd(nc, in_maps, core_ids=list(range(NCORE)))
        cur = [np.asarray(res.results[b]["y"], dtype=np.float32)
               for b in range(NCORE)]
    return np.stack(cur, axis=0)
